# version 22
# baseline (speedup 1.0000x reference)
import numpy as np
import concourse.bass as bass
import concourse.mybir as mybir
from concourse.bass_utils import run_bass_kernel_spmd
from contextlib import ExitStack

F32 = mybir.dt.float32
BF16 = mybir.dt.bfloat16
I32 = mybir.dt.int32
AF = mybir.ActivationFunctionType
ALU = mybir.AluOpType
AX = mybir.AxisListType

NCORES = 8
DM = 4096
CI = DM // 128
DH = 1024
DC = DH // 128
FQKV = 3072
EPS = 1e-5
ISCALE = 1.0 / 32.0
EMARGIN = 3.0
CTEMP = 10.0
CTHR = 0.01

STORE_KEYS = ["stxlt0", "stxlt1", "stq0", "stq1", "stkv0", "stkv1",
              "stxm0", "stxm1", "stx2_0", "stx2_1", "stout0", "stout1",
              "stout2", "stcnt"]

SEMS = ["v", "s", "t", "g",
        "ldx0", "ldx1", "ldx8_0", "ldx8_1", "ldw0", "ldw1", "ldg",
        "ldk0", "ldk1", "ldv0", "ldv1", "ldq", "ldr0", "ldr1",
        "ldd0", "ldd1", "ldd2", "ldm0", "ldm1", "ldm2", "ldwr0", "ldwr1", "ldwr2", "ldwt0", "ldwt1", "ldcnt", "ldd",
        "stxlt0", "stxlt1", "stq0", "stq1", "stkv0", "stkv1",
        "stxm0", "stxm1", "stx2_0", "stx2_1", "stout0", "stout1", "stout2", "stcnt",
        "cc1", "cc2"]


class Prog:
    def __init__(self):
        self.emit = False
        self.marks = {}
        self.cnt = {k: 0 for k in SEMS}
        self.sem = {}
        self.nc = None
        self.stack = None

    def reset_counts(self):
        self.cnt = {k: 0 for k in SEMS}


class Eng:
    def __init__(self, P, name, prog_key=None):
        self.P = P
        self.name = name
        self.key = prog_key

    def _eng(self):
        return getattr(self.P.nc, self.name)

    def op(self, f, inc=1):
        P = self.P
        if P.emit:
            if self.key:
                self._eng().wait_ge(P.sem[self.key], P.cnt[self.key])
            instr = f()
            if self.key and inc:
                instr.then_inc(P.sem[self.key], inc)
        if self.key and inc:
            P.cnt[self.key] += inc

    def mm(self, f, inc=0):
        P = self.P
        if P.emit:
            instr = f()
            if inc:
                instr.then_inc(P.sem["t"], inc)
        if inc:
            P.cnt["t"] += inc

    def dma(self, f, sem_key):
        P = self.P
        if P.emit:
            if self.key:
                self._eng().wait_ge(P.sem[self.key], P.cnt[self.key])
            f().then_inc(P.sem[sem_key], 16)
        P.cnt[sem_key] += 16
        return (sem_key, P.cnt[sem_key])

    def wait(self, mark_name):
        P = self.P
        if P.emit:
            sem_key, val = P.marks[mark_name]
            self._eng().wait_ge(P.sem[sem_key], val)

    def wait_sem(self, sem_key, val):
        if val <= 0:
            return
        if self.P.emit:
            self._eng().wait_ge(self.P.sem[sem_key], val)

    def wait_tok(self, tok):
        if tok is not None:
            self.wait_sem(*tok)

    def mark(self, name, sem_key=None):
        P = self.P
        key = sem_key or self.key
        cur = (key, P.cnt[key])
        if not P.emit:
            P.marks[name] = cur
        else:
            assert P.marks.get(name) == cur, (
                f"mark mismatch {name}: dry {P.marks.get(name)} vs emit {cur}")


def run_block(P, bodies, name=None):
    if not P.emit:
        for ename, key, fn in bodies:
            fn(Eng(P, ename, key))
    else:
        with P.nc.Block(name, no_gpsimd_drain=True) as block:
            for ename, key, fn in bodies:
                deco = getattr(block, ename)

                def body(_eng, fn=fn, ename=ename, key=key):
                    fn(Eng(P, ename, key))
                deco(body)



def build(T=2048, dbg=()):
    NT = T // 128
    NSEG = NT // 2
    KB = T // 2
    NKB = 2 * NCORES
    KSB = max(1, KB // 512)
    NSC = min(512, KB)
    KC = KB // 128
    QG = min(4, NT)
    NG = NT // QG

    P = Prog()

    def build_all():
        nc = P.nc

        def dram(name, shape, dt, kind=None):
            if not P.emit:
                return None
            if kind:
                return nc.dram_tensor(name, shape, dt, kind=kind)
            return nc.dram_tensor(name, shape, dt)

        def sbuf(name, shape, dt):
            if not P.emit:
                return None
            return P.stack.enter_context(nc.sbuf_tensor(name, shape, dt))

        x_s = dram("x_s", [T, DM], F32, "ExternalInput")
        wqkv = dram("wqkv", [DM, FQKV], F32, "ExternalInput")
        bqkv = dram("bqkv", [1, FQKV], F32, "ExternalInput")
        wo = dram("wo", [DH, DM], F32, "ExternalInput")
        bo = dram("bo", [1, DM], F32, "ExternalInput")
        wffn = dram("wffn", [DM, DM], F32, "ExternalInput")
        bffn = dram("bffn", [1, DM], F32, "ExternalInput")
        cent = dram("cent", [64, DM], F32, "ExternalInput")
        asg = dram("asg", [1, DM], I32, "ExternalInput")
        ident = dram("ident", [128, 128], F32, "ExternalInput")
        onesr = dram("onesr", [1, 512], F32, "ExternalInput")
        onesc = dram("onesc", [128, 1], F32, "ExternalInput")
        iota64 = dram("iota64", [64, 1], F32, "ExternalInput")
        out_s = dram("out_s", [T, DM], F32, "ExternalOutput")

        XLT = dram("XLT", [NT, 128, DM], BF16)
        XLT2 = dram("XLT2", [NT, 128, DM], BF16)
        XMID = dram("XMID", [T, DM], F32)
        KVIN = dram("KVIN", [2048, T], BF16)
        KGk = (nc.dram_tensor("KGk", [NCORES * 1024, T], BF16,
                              addr_space="Shared") if P.emit else None)
        KGv = (nc.dram_tensor("KGv", [NCORES * 1024, T], BF16,
                              addr_space="Shared") if P.emit else None)
        QTD = dram("QTD", [DC, 128, T], BF16)
        WTD = dram("WTD", [CI, 128, DM], BF16)
        CNTI = dram("CNTI", [1, 64], F32)
        CNTO = dram("CNTO", [1, 64], F32)

        identB = sbuf("identB", [128, 128], BF16)
        onesrB = sbuf("onesrB", [1, 512], BF16)
        onescB = sbuf("onescB", [128, 1], BF16)
        iotaC = sbuf("iotaC", [64, 1], F32)
        boB = sbuf("boB", [1, DM], BF16)
        centT = sbuf("centT", [128, CI * 64], BF16)
        qsel = sbuf("qsel", [128, NT], F32)

        NCONST = 8

        phase_a(P, nc, locals())
        phase_bc(P, nc, locals())
        phase_d(P, nc, locals())
        if dbg:
            dbg_block(P, nc, locals(), dbg)

    def phase_a(P, nc, env):
        g = env

        with ExitStack() as sta0:
            wreg = ([sta0.enter_context(nc.sbuf_tensor(f"wreg{i}", [128, CI, 512], BF16))
                     for i in range(2)] if P.emit else [None, None])
            bqkvB = (sta0.enter_context(nc.sbuf_tensor("bqkvB", [1, FQKV], BF16))
                     if P.emit else None)
            env["bqkvB"] = bqkvB
            phase_a_inner(P, nc, env, wreg)

    def phase_a_inner(P, nc, env, wreg):
        g = env
        identB, onesrB, onescB = g["identB"], g["onesrB"], g["onescB"]
        bqkvB, centT = g["bqkvB"], g["centT"]
        hpseq0 = [2, 3, 4, 5, 0, 1]

        with ExitStack() as sta:
            def asb(name, shape, dt):
                return (sta.enter_context(nc.sbuf_tensor(name, shape, dt))
                        if P.emit else None)

            def aps(name, shape):
                return (sta.enter_context(nc.psum_tensor(name, shape, F32))
                        if P.emit else None)

            xbuf = [asb(f"xbuf{i}", [128, DM], F32) for i in range(2)]
            stats = asb("stats", [128, 48], F32)
            mv = asb("mv", [128, 2], F32)
            lnv = asb("lnv", [128, 1], F32)
            rstd = asb("rstd", [128, 1], F32)
            xlnb = [asb(f"xlnb{i}", [128, DM], BF16) for i in range(2)]
            xltsb = [asb(f"xltsb{i}", [128, DM], BF16) for i in range(2)]
            centS = asb("centS", [64, DM], BF16)
            trps = [aps(f"trps{i}", [128, 1024]) for i in range(2)]

            def a_gpsimd(e):
                e.dma(lambda: nc.gpsimd.dma_start(out=identB[:, :], in_=g["ident"][:, :]), "ldg")
                e.dma(lambda: nc.gpsimd.dma_start(out=onesrB[:, :], in_=g["onesr"][:, :]), "ldg")
                e.dma(lambda: nc.gpsimd.dma_start(out=onescB[:, :], in_=g["onesc"][:, :]), "ldg")
                e.dma(lambda: nc.gpsimd.dma_start(out=g["iotaC"][:, :], in_=g["iota64"][:, :]), "ldg")
                e.dma(lambda: nc.gpsimd.dma_start(out=bqkvB[:, :], in_=g["bqkv"][:, :]), "ldg")
                e.dma(lambda: nc.gpsimd.dma_start(out=g["boB"][:, :], in_=g["bo"][:, :]), "ldg")
                e.dma(lambda: nc.gpsimd.dma_start(out=centS[:, :], in_=g["cent"][:, :]), "ldg")
                for pos in range(2):
                    off = hpseq0[pos] * 512
                    e.dma(lambda off=off, pos=pos: nc.gpsimd.dma_start(
                        out=wreg[pos % 2][:, :, :],
                        in_=g["wqkv"][:, off:off + 512].rearrange(
                            "(ci p) f -> p ci f", p=128)), f"ldw{pos % 2}")

            def a_sync(e):
                for ti in range(NT):
                    if ti >= 2:
                        e.wait(f"a_xnorm{ti - 2}")
                    e.dma(lambda ti=ti: nc.sync.dma_start(
                        out=xbuf[ti % 2][:, :],
                        in_=g["x_s"][ti * 128:(ti + 1) * 128, :]), f"ldx{ti % 2}")

            def a_vector(e):
                for ti in range(NT):
                    e.wait_sem(f"ldx{ti % 2}", 16 * (ti // 2 + 1))
                    if ti >= 1:
                        e.wait(f"a_ln{ti - 1}")
                    for j in range(8):
                        e.op(lambda j=j, ti=ti: nc.vector.bn_stats(
                            stats[:, 6 * j:6 * j + 6],
                            xbuf[ti % 2][:, 512 * j:512 * (j + 1)]))
                    e.op(lambda: nc.vector.bn_aggr(mv[:, :], stats[:, :]))
                    e.mark(f"a_aggr{ti}")
                    e.op(lambda: nc.vector.tensor_scalar(
                        out=mv[:, 1:2], in0=mv[:, 1:2],
                        scalar1=EPS, scalar2=None, op0=ALU.add))
                    e.mark(f"a_veps{ti}")
                    e.op(lambda ti=ti: nc.vector.tensor_scalar(
                        out=xbuf[ti % 2][:, :], in0=xbuf[ti % 2][:, :],
                        scalar1=mv[:, 0:1], scalar2=None, op0=ALU.subtract))
                    e.mark(f"a_sub{ti}")

            def a_scalar(e):
                for ti in range(NT):
                    e.wait(f"a_veps{ti}")
                    e.op(lambda: nc.scalar.activation(
                        out=lnv[:, :], in_=mv[:, 1:2], func=AF.Ln, bias=0.0))
                    e.mark(f"a_ln{ti}")
                    e.op(lambda: nc.scalar.activation(
                        out=rstd[:, :], in_=lnv[:, :], func=AF.Exp, scale=-0.5))
                    e.wait(f"a_sub{ti}")
                    if ti >= 2:
                        e.wait(f"a_tr{ti - 2}_3")
                    e.op(lambda ti=ti: nc.scalar.activation(
                        out=xlnb[ti % 2][:, :], in_=xbuf[ti % 2][:, :],
                        func=AF.Copy, scale=rstd[:, :]))
                    e.mark(f"a_xnorm{ti}")
                    for grp in range(4):
                        e.wait(f"a_tr{ti}_{grp}")
                        e.op(lambda ti=ti, grp=grp: nc.scalar.copy(
                            xltsb[ti % 2][:, grp * 1024:(grp + 1) * 1024],
                            trps[grp % 2][:, :]))
                        e.mark(f"a_trev{ti * 4 + grp}")
                    if ti >= 2:
                        e.wait_sem(f"stxlt{ti % 2}", 16 * (ti // 2))
                    e.dma(lambda ti=ti: nc.scalar.dma_start(
                        out=g["XLT"][ti, :, :], in_=xltsb[ti % 2][:, :]), f"stxlt{ti % 2}")
                for cg in range(2):
                    e.wait(f"a_ctr{cg}")
                    e.op(lambda cg=cg: nc.scalar.copy(
                        centT[:, cg * 1024:(cg + 1) * 1024], trps[cg % 2][:, :]))
                    e.mark(f"a_ctrev{cg}")

            def a_tensor(e):
                e.wait_sem("ldg", 16 * 7)
                for ti in range(NT):
                    e.wait(f"a_xnorm{ti}")
                    for grp in range(4):
                        G = ti * 4 + grp
                        if G >= 2:
                            e.wait(f"a_trev{G - 2}")
                        for j in range(8):
                            ci = grp * 8 + j
                            e.mm(lambda ti=ti, grp=grp, ci=ci, j=j:
                                 nc.tensor.matmul(
                                     trps[grp % 2][:, j * 128:(j + 1) * 128],
                                     xlnb[ti % 2][:, ci * 128:(ci + 1) * 128],
                                     identB[:, :], start=True, stop=True),
                                 inc=1 if j == 7 else 0)
                        e.mark(f"a_tr{ti}_{grp}", "t")
                for cg in range(2):
                    e.wait(f"a_trev{4 * NT - 2 + cg}")
                    for j in range(16):
                        ci = cg * 16 + j
                        e.mm(lambda ci=ci, cg=cg, j=j: nc.tensor.matmul(
                            trps[cg % 2][:, j * 64:(j + 1) * 64],
                            centS[0:64, ci * 128:(ci + 1) * 128],
                            identB[0:64, 0:64], start=True, stop=True),
                            inc=1 if j == 15 else 0)
                    e.mark(f"a_ctr{cg}", "t")

            run_block(P, [
                ("gpsimd", "g", a_gpsimd),
                ("sync", None, a_sync),
                ("vector", "v", a_vector),
                ("scalar", "s", a_scalar),
                ("tensor", None, a_tensor),
            ], "phaseA0")

        QS = min(512, T)
        NQ = T // QS
        TPQ = QS // 128

        with ExitStack() as sta1:
            def asb1(name, shape, dt):
                return (sta1.enter_context(nc.sbuf_tensor(name, shape, dt))
                        if P.emit else None)

            def aps1(name, shape):
                return (sta1.enter_context(nc.psum_tensor(name, shape, F32))
                        if P.emit else None)

            xlt4 = [asb1(f"xlt4_{i}", [128, TPQ, DM], BF16) for i in range(2)]
            qst = [asb1(f"qst{i}", [128, 512], BF16) for i in range(2)]
            qkps = [aps1(f"qkps{i}", [128, 512]) for i in range(2)]

            vview = (g["KVIN"][1024:2048, :].flatten()
                     .rearrange("(t d) -> t d", d=1024) if P.emit else None)

            hpseq = [2, 3, 4, 5, 0, 1]
            items = []
            for pos, hp in enumerate(hpseq):
                for q4 in range(NQ):
                    for k in range(4 if hp < 4 else TPQ):
                        items.append((pos, hp, q4, k))
            n_kv = sum(1 for (pos, hp, q4, k) in items if hp >= 2)
            n_k = sum(1 for (pos, hp, q4, k) in items if hp in (2, 3))

            fvA = {k: P.cnt[k] for k in ["stxlt0", "stxlt1"]}

            def a1_gpsimd(e):
                for k, v in fvA.items():
                    e.wait_sem(k, v)
                for pos, hp in enumerate(hpseq):
                    if pos < 2:
                        continue
                    off = hp * 512
                    e.wait(f"a_wfree{pos - 2}")
                    e.dma(lambda off=off, pos=pos: nc.gpsimd.dma_start(
                        out=wreg[pos % 2][:, :, :],
                        in_=g["wqkv"][:, off:off + 512].rearrange(
                            "(ci p) f -> p ci f", p=128)), f"ldw{pos % 2}")
                e.wait("a_kdone0")
                e.wait("a_kdone1")
                if P.emit:
                    nc.gpsimd.collective_compute(
                        "AllGather", ALU.bypass,
                        replica_groups=[list(range(NCORES))],
                        ins=[g["KVIN"][0:1024, :].opt()],
                        outs=[g["KGk"].ap().opt()],
                    ).then_inc(P.sem["cc1"], 1)
                P.cnt["cc1"] += 1
                e.wait("a_vdone0")
                e.wait("a_vdone1")
                if P.emit:
                    nc.gpsimd.collective_compute(
                        "AllGather", ALU.bypass,
                        replica_groups=[list(range(NCORES))],
                        ins=[g["KVIN"][1024:2048, :].opt()],
                        outs=[g["KGv"].ap().opt()],
                    ).then_inc(P.sem["cc1"], 1)
                P.cnt["cc1"] += 1

            def a1_sync(e):
                for k, v in fvA.items():
                    e.wait_sem(k, v)
                for pos in range(6):
                    for q4 in range(NQ):
                        idx = pos * NQ + q4
                        if idx >= 2:
                            e.wait(f"a_x4free{idx - 2}")
                        e.dma(lambda q4=q4, idx=idx: nc.sync.dma_start(
                            out=xlt4[idx % 2][:, :, :],
                            in_=g["XLT"][q4 * TPQ:(q4 + 1) * TPQ, :, :]
                            .rearrange("ti p e -> p ti e")), f"ldx8_{idx % 2}")

            def a1_scalar(e):
                stg_tok = {0: None, 1: None}
                for k2, v2 in fvA.items():
                    e.wait_sem(k2, v2)
                for u, (pos, hp, q4, k) in enumerate(items):
                    pas, half = hp // 2, hp % 2
                    w = QS if pas < 2 else 512
                    e.wait(f"a_mm{u}")
                    e.wait_tok(stg_tok[u % 2])
                    e.op(lambda u=u, w=w: nc.scalar.copy(
                        qst[u % 2][:, 0:w], qkps[u % 2][:, 0:w]))
                    e.mark(f"a_ev{u}")
                    if pas == 0:
                        fi = half * 4 + k
                        tok = e.dma(lambda fi=fi, q4=q4, u=u:
                                    nc.scalar.dma_start(
                                        out=g["QTD"][fi, :, q4 * QS:(q4 + 1) * QS],
                                        in_=qst[u % 2][:, 0:QS]), f"stq{u % 2}")
                    elif pas == 1:
                        fi = half * 4 + k
                        tok = e.dma(lambda fi=fi, q4=q4, u=u:
                                    nc.scalar.dma_start(
                                        out=g["KVIN"][fi * 128:(fi + 1) * 128,
                                                      q4 * QS:(q4 + 1) * QS],
                                        in_=qst[u % 2][:, 0:QS]), f"stkv{u % 2}")
                    else:
                        ti = q4 * TPQ + k
                        tok = e.dma(lambda ti=ti, half=half, u=u:
                                    nc.scalar.dma_start(
                                        out=vview[ti * 128:(ti + 1) * 128,
                                                  half * 512:(half + 1) * 512],
                                        in_=qst[u % 2][:, :]), f"stkv{u % 2}")
                    stg_tok[u % 2] = tok
                    if u == n_k - 1:
                        e.mark("a_kdone0", "stkv0")
                        e.mark("a_kdone1", "stkv1")
                    if u == n_kv - 1:
                        e.mark("a_vdone0", "stkv0")
                        e.mark("a_vdone1", "stkv1")

            def a1_tensor(e):
                for u, (pos, hp, q4, k) in enumerate(items):
                    pas, half = hp // 2, hp % 2
                    if k == 0 and q4 == 0:
                        e.wait_sem(f"ldw{pos % 2}", 16 * (pos // 2 + 1))
                    idx = pos * NQ + q4
                    if k == 0:
                        e.wait_sem(f"ldx8_{idx % 2}", 16 * (idx // 2 + 1))
                    if u >= 2:
                        e.wait(f"a_ev{u - 2}")
                    if pas < 2:
                        for ci in range(CI):
                            e.mm(lambda k=k, ci=ci, u=u, pos=pos, idx=idx:
                                 nc.tensor.matmul(
                                     qkps[u % 2][:, 0:QS],
                                     wreg[pos % 2][:, ci, k * 128:(k + 1) * 128],
                                     xlt4[idx % 2][:, :, ci * 128:(ci + 1) * 128],
                                     start=(ci == 0), stop=False))
                        boff = pas * 1024 + half * 512 + k * 128
                        e.mm(lambda boff=boff, u=u: nc.tensor.matmul(
                            qkps[u % 2][:, 0:QS],
                            bqkvB[0:1, boff:boff + 128],
                            onesrB[0:1, 0:QS],
                            start=False, stop=True), inc=1)
                    else:
                        for ci in range(CI):
                            e.mm(lambda k=k, ci=ci, u=u, pos=pos, idx=idx:
                                 nc.tensor.matmul(
                                     qkps[u % 2][:, 0:512],
                                     xlt4[idx % 2][:, k, ci * 128:(ci + 1) * 128],
                                     wreg[pos % 2][:, ci, :],
                                     start=(ci == 0), stop=False))
                        boff = 2048 + half * 512
                        e.mm(lambda boff=boff, u=u: nc.tensor.matmul(
                            qkps[u % 2][:, 0:512],
                            onesrB[0:1, 0:128],
                            bqkvB[0:1, boff:boff + 512],
                            start=False, stop=True), inc=1)
                    e.mark(f"a_mm{u}", "t")
                    if k == (4 if pas < 2 else TPQ) - 1:
                        e.mark(f"a_x4free{idx}", "t")
                        if q4 == NQ - 1:
                            e.mark(f"a_wfree{pos}", "t")

            run_block(P, [
                ("gpsimd", "g", a1_gpsimd),
                ("sync", None, a1_sync),
                ("scalar", "s", a1_scalar),
                ("tensor", None, a1_tensor),
            ], "phaseA1")
    def phase_bc(P, nc, env):
        g = env
        identB, onesrB, onescB = g["identB"], g["onesrB"], g["onescB"]
        boB, centT, qsel = g["boB"], g["centT"], g["qsel"]

        with ExitStack() as stb:
            def bsb(name, shape, dt):
                return (stb.enter_context(nc.sbuf_tensor(name, shape, dt))
                        if P.emit else None)

            def bps(name, shape):
                return (stb.enter_context(nc.psum_tensor(name, shape, F32))
                        if P.emit else None)

            WoB = bsb("WoB", [128, DC, DM], BF16)
            QTg = bsb("QTg", [128, DC, QG * 128], BF16)
            oacc = bsb("oacc", [128, QG * 1025], F32)
            colC = bsb("colC", [128, 1], F32)
            auxb = bsb("auxb", [128, 2], BF16)
            cntsb = bsb("cntsb", [1, 64], F32)
            ps_cnt = bps("ps_cnt", [1, 64])

            sh = {}
            for grp in range(NG):
                flash_block(P, nc, env, grp, dict(
                    WoB=WoB, QTg=QTg, oacc=oacc, colC=colC, auxb=auxb,
                    sh=sh))
                c_block(P, nc, env, grp, dict(
                    WoB=WoB, oacc=oacc, cntsb=cntsb,
                    ps_cnt=ps_cnt, sh=sh))

    def flash_block(P, nc, env, grp, h):
        g = env
        identB, onesrB = g["identB"], g["onesrB"]
        QTg, oacc = h["QTg"], h["oacc"]
        colC, auxb = h["colC"], h["auxb"]
        NQC = QG * 128

        with ExitStack() as stf:
            def fsb(name, shape, dt):
                return (stf.enter_context(nc.sbuf_tensor(f"{name}_g{grp}", shape, dt))
                        if P.emit else None)

            def fps(name, shape):
                return (stf.enter_context(nc.psum_tensor(f"{name}_g{grp}", shape, F32))
                        if P.emit else None)

            wrow = ([fsb(f"fwrow{i}", [128, DM], BF16) for i in range(2)]
                    if grp == 0 else None)
            wst = ([fsb(f"fwst{i}", [128, 1024], BF16) for i in range(1)]
                   if grp == 0 else None)
            wr_tok = {}
            wst_tok = {0: None, 1: None}
            Kb = [fsb(f"Kb{i}", [128, DC, KB], BF16) for i in range(2)]
            Vb = [fsb(f"Vb{i}", [128, KC, 1025], BF16) for i in range(2)]
            PT = [fsb(f"PT{i}", [128, KC, NQC], BF16) for i in range(2)]
            pvs = [fsb(f"pvs{i}", [128, 1025], F32) for i in range(2)]
            pmax = pvs[0]
            ps_sT = [fps(f"ps_sT{i}", [128, max(NQC, 512)]) for i in range(2)]
            ps_pv = [fps(f"ps_pv{i}", [128, DH]) for i in range(2)]
            ps_l = fps("ps_l", [128, 512])

            fv = {k: P.cnt[k] for k in STORE_KEYS}

            def fence(e):
                for k, v in fv.items():
                    e.wait_sem(k, v)

            def b_gpsimd(e):
                fence(e)
                if grp == 0:
                    h["sh"]["wob_tok"] = e.dma(lambda: nc.gpsimd.dma_start(
                        out=h["WoB"][:, :, :],
                        in_=g["wo"].rearrange("(dc p) c -> p dc c", p=128)),
                        "ldw0")
                    def wload(j):
                        wr_tok[j] = e.dma(lambda j=j: nc.gpsimd.dma_start(
                            out=wrow[j % 2][:, :],
                            in_=g["wffn"][j * 128:(j + 1) * 128, :]),
                            f"ldwr{j % 2}")
                    wload(0)
                    wload(1)
                    for j in range(2, 32):
                        e.wait(f"fw_free{j - 2}")
                        wload(j)

            def b_sync(e):
                fence(e)
                if grp > 0:
                    e.wait(f"b_qtfree{grp - 1}")
                e.dma(lambda: nc.sync.dma_start(
                    out=QTg[:, :, :],
                    in_=g["QTD"][:, :, grp * NQC:(grp + 1) * NQC]
                    .rearrange("dc p t -> p dc t")), "ldq")
                if grp == 0:
                    e.wait_sem("cc1", 1)
                for kb in range(NKB):
                    r, hf = divmod(kb, 2)
                    if kb >= 2:
                        e.wait(f"b_kfree{grp}_{kb - 2}")
                    e.dma(lambda r=r, hf=hf, kb=kb: nc.sync.dma_start(
                        out=Kb[kb % 2][:, :, :],
                        in_=g["KGk"][r * 1024:(r + 1) * 1024,
                                     hf * KB:(hf + 1) * KB]
                        .rearrange("(dc p) j -> p dc j", p=128)),
                        f"ldk{kb % 2}")
                    if grp == 0 and kb == 0:
                        e.wait_sem("cc1", 2)
                    if kb >= 2:
                        e.wait(f"b_vfree{grp}_{kb - 2}")
                    e.dma(lambda r=r, hf=hf, kb=kb: nc.sync.dma_start(
                        out=Vb[kb % 2][:, :, 0:DH],
                        in_=g["KGv"][r * 1024:(r + 1) * 1024, :]
                        .flatten().rearrange("(t d) -> t d", d=DH)
                        [hf * KB:(hf + 1) * KB, :]
                        .rearrange("(kc p) d -> p kc d", p=128)),
                        f"ldv{kb % 2}")

            def b_tensor(e):
                def S(kb, kc):
                    si = kb * KC + kc
                    if grp == 0 and si <= 1:
                        e.wait(f"fw_ev{254 + si}")
                    if kc == 0:
                        e.wait_sem(f"ldk{kb % 2}", 16 * (kb // 2 + 1 + grp * NKB // 2))
                        if kb == 0:
                            e.wait_sem("ldq", 16 * (grp + 1))
                    if si >= 2:
                        p = si - 2
                        e.wait(f"b_exp{grp}_{p // KC}_{p % KC}")
                    for dc in range(DC):
                        e.mm(lambda kc=kc, dc=dc, kb=kb, si=si: nc.tensor.matmul(
                            ps_sT[si % 2][:, 0:NQC],
                            Kb[kb % 2][:, dc, kc * 128:(kc + 1) * 128],
                            QTg[:, dc, :],
                            start=(dc == 0), stop=(dc == DC - 1)),
                            inc=1 if dc == DC - 1 else 0)
                    e.mark(f"b_sT{grp}_{kb}_{kc}", "t")
                    if kc == KC - 1:
                        e.mark(f"b_kfree{grp}_{kb}", "t")
                        if kb == NKB - 1:
                            e.mark(f"b_qtfree{grp}", "t")

                def PV(kb, qt):
                    if qt == 0:
                        e.wait(f"b_exp{grp}_{kb}_{KC - 1}")
                        e.wait_sem(f"ldv{kb % 2}", 16 * (kb // 2 + 1 + grp * NKB // 2))
                        e.wait(f"b_vset{grp}_{kb % 2}")
                    a = kb * QG + qt
                    if a >= 2:
                        pa = a - 2
                        e.wait(f"b_pvev{grp}_{pa // QG}_{pa % QG}")
                    for kc in range(KC):
                        for sb in range(2):
                            e.mm(lambda kc=kc, kb=kb, qt=qt, sb=sb, a=a:
                                 nc.tensor.matmul(
                                     ps_pv[a % 2][:, sb * 512:(sb + 1) * 512],
                                     PT[kb % 2][:, kc, qt * 128:(qt + 1) * 128],
                                     Vb[kb % 2][:, kc, sb * 512:(sb + 1) * 512],
                                     start=(kc == 0), stop=(kc == KC - 1)))
                    for kc in range(KC):
                        e.mm(lambda kc=kc, kb=kb, qt=qt, a=a:
                             nc.tensor.matmul(
                                 ps_l[:, (a % 2):(a % 2) + 1],
                                 PT[kb % 2][:, kc, qt * 128:(qt + 1) * 128],
                                 Vb[kb % 2][:, kc, 1024:1025],
                                 start=(kc == 0), stop=(kc == KC - 1),
                                 skip_group_check=True),
                             inc=1 if kc == KC - 1 else 0)
                    e.mark(f"b_PV{grp}_{kb}_{qt}", "t")
                    if qt == QG - 1:
                        e.mark(f"b_vfree{grp}_{kb}", "t")

                def aux1():
                    e.wait(f"b_pmaxb{grp}")
                    e.mm(lambda: nc.tensor.matmul(
                        ps_l[0:1, 0:128], auxb[:, 0:1], identB[:, :],
                        start=True, stop=True, skip_group_check=True), inc=1)
                    e.mark(f"b_aux1{grp}", "t")

                def aux2():
                    e.wait(f"b_m1b{grp}")
                    e.mm(lambda: nc.tensor.matmul(
                        ps_l[:, 0:1], onesrB[0:1, 0:128], auxb[0:1, 1:2],
                        start=True, stop=True, skip_group_check=True), inc=1)
                    e.mark(f"b_aux2{grp}", "t")

                if grp == 0:
                    for j in range(32):
                        e.wait_tok(wr_tok[j])
                        for r in range(8):
                            rr = j * 8 + r
                            if rr >= 2:
                                e.wait(f"fw_ev{rr - 2}")
                            for t4 in range(4):
                                ci = r * 4 + t4
                                e.mm(lambda j=j, ci=ci, t4=t4, rr=rr:
                                     nc.tensor.matmul(
                                         ps_sT[rr % 2][:, t4 * 128:(t4 + 1) * 128],
                                         wrow[j % 2][:, ci * 128:(ci + 1) * 128],
                                         identB[:, :], start=True, stop=True),
                                     inc=1 if t4 == 3 else 0)
                            e.mark(f"fw_tr{rr}", "t")
                        e.mark(f"fw_free{j}", "t")
                S(0, 0)
                if KC > 1:
                    S(0, 1)
                aux1()
                aux2()
                for kc in range(min(2, KC), KC):
                    S(0, kc)
                for kb in range(1, NKB):
                    for kc in range(KC):
                        S(kb, kc)
                    for qt in range(QG):
                        PV(kb - 1, qt)
                for qt in range(QG):
                    PV(NKB - 1, qt)

            def b_scalar(e):
                fence(e)
                if grp == 0:
                    for rr in range(1, 256, 2):
                        e.wait(f"fw_ev{rr}")
                        j, b4 = rr // 8, (rr % 8) // 2
                        wst_tok[rr // 2] = e.dma(
                            lambda j=j, b4=b4, rr=rr: nc.scalar.dma_start(
                                out=g["WTD"][b4 * 8:(b4 + 1) * 8, :,
                                             j * 128:(j + 1) * 128]
                                .rearrange("ci p o -> p ci o"),
                                in_=wst[0][:, :]
                                .rearrange("p (a b) -> p a b", a=8)),
                            f"stout{(rr // 2) % 2}")
                for kb in range(NKB):
                    for kc in range(KC):
                        e.wait(f"b_sT{grp}_{kb}_{kc}")
                        if kb == 0 and kc == 0:
                            e.wait(f"b_colC{grp}")
                        si = kb * KC + kc
                        if si >= 2 * KC:
                            e.wait(f"b_PV{grp}_{kb - 2}_{QG - 1}")
                        e.op(lambda kb=kb, kc=kc, si=si: nc.scalar.activation(
                            out=PT[kb % 2][:, kc, :], in_=ps_sT[si % 2][:, 0:NQC],
                            func=AF.Exp, scale=ISCALE, bias=colC[:, :]))
                        e.mark(f"b_exp{grp}_{kb}_{kc}")
                    if kb >= 1:
                        for qt in range(QG):
                            e.wait(f"b_PV{grp}_{kb - 1}_{qt}")
                            a = (kb - 1) * QG + qt
                            if a >= 2:
                                pa = a - 2
                                e.wait(f"b_acc{grp}_{pa // QG}_{pa % QG}")
                            e.op(lambda qt=qt, a=a: nc.scalar.copy(
                                pvs[qt % 2][:, 0:DH], ps_pv[a % 2][:, :]))
                            e.op(lambda qt=qt, a=a: nc.scalar.copy(
                                pvs[qt % 2][:, DH:DH + 1],
                                ps_l[:, (a % 2):(a % 2) + 1]))
                            e.mark(f"b_pvev{grp}_{kb - 1}_{qt}")
                for qt in range(QG):
                    e.wait(f"b_PV{grp}_{NKB - 1}_{qt}")
                    a = (NKB - 1) * QG + qt
                    pa = a - 2
                    e.wait(f"b_acc{grp}_{pa // QG}_{pa % QG}")
                    e.op(lambda qt=qt, a=a: nc.scalar.copy(
                        pvs[qt % 2][:, 0:DH], ps_pv[a % 2][:, :]))
                    e.op(lambda qt=qt, a=a: nc.scalar.copy(
                        pvs[qt % 2][:, DH:DH + 1],
                        ps_l[:, (a % 2):(a % 2) + 1]))
                    e.mark(f"b_pvev{grp}_{NKB - 1}_{qt}")

            def b_vector(e):
                if grp == 0:
                    for rr in range(256):
                        e.wait(f"fw_tr{rr}")
                        if rr >= 2:
                            e.wait_tok(wst_tok[rr // 2 - 1])
                        e.op(lambda rr=rr: nc.vector.tensor_copy(
                            wst[0][:, (rr % 2) * 512:
                                    (rr % 2 + 1) * 512],
                            ps_sT[rr % 2][:, 0:512]))
                        e.mark(f"fw_ev{rr}")
                e.wait(f"b_sT{grp}_0_0")
                e.op(lambda: nc.vector.reduce_max(
                    pmax[:, 0:1], ps_sT[0][:, 0:NQC], axis=AX.X))
                e.op(lambda: nc.vector.tensor_copy(auxb[:, 0:1], pmax[:, 0:1]))
                e.mark(f"b_pmaxb{grp}")
                e.wait(f"b_aux1{grp}")
                e.op(lambda: nc.vector.reduce_max(
                    colC[0:1, 0:1], ps_l[0:1, 0:128], axis=AX.X))
                e.op(lambda: nc.vector.tensor_copy(auxb[0:1, 1:2], colC[0:1, 0:1]))
                e.mark(f"b_m1b{grp}")
                e.wait(f"b_aux2{grp}")
                e.op(lambda: nc.vector.tensor_scalar(
                    out=colC[:, :], in0=ps_l[:, 0:1], scalar1=-ISCALE,
                    scalar2=-EMARGIN, op0=ALU.mult, op1=ALU.add))
                e.mark(f"b_colC{grp}")
                for kb in range(NKB):
                    for qt in range(QG):
                        e.wait(f"b_pvev{grp}_{kb}_{qt}")
                        if kb == 0:
                            e.op(lambda qt=qt: nc.vector.tensor_copy(
                                oacc[:, qt * 1025:(qt + 1) * 1025],
                                pvs[qt % 2][:, :]))
                        else:
                            e.op(lambda qt=qt: nc.vector.tensor_tensor(
                                out=oacc[:, qt * 1025:(qt + 1) * 1025],
                                in0=oacc[:, qt * 1025:(qt + 1) * 1025],
                                in1=pvs[qt % 2][:, :], op=ALU.add))
                        e.mark(f"b_acc{grp}_{kb}_{qt}")

            def b_memset(e):
                for i in range(2):
                    e.op(lambda i=i: nc.vector.memset(
                        Vb[i][:, :, 1024:1025], 1.0))
                    e.mark(f"b_vset{grp}_{i}", "v")

            def b_vector_all(e):
                fence(e)
                b_memset(e)
                b_vector(e)

            run_block(P, [
                ("gpsimd", "g", b_gpsimd),
                ("sync", None, b_sync),
                ("tensor", None, b_tensor),
                ("scalar", "s", b_scalar),
                ("vector", "v", b_vector_all),
            ], f"flash{grp}")

    def c_block(P, nc, env, grp, h):
        g = env
        identB, onesrB, onescB = g["identB"], g["onesrB"], g["onescB"]
        boB, centT, qsel = g["boB"], g["centT"], g["qsel"]
        WoB, oacc = h["WoB"], h["oacc"]
        ps_cnt, cntsb = h["ps_cnt"], h["cntsb"]

        with ExitStack() as stc:
            def csb(name, shape, dt):
                return (stc.enter_context(nc.sbuf_tensor(f"{name}_g{grp}", shape, dt))
                        if P.emit else None)

            def cps(name, shape):
                return (stc.enter_context(nc.psum_tensor(f"{name}_g{grp}", shape, F32))
                        if P.emit else None)

            xre = [csb(f"xre{i}", [128, DM], F32) for i in range(2)]
            pb512 = [csb(f"pb512_{i}", [128, 512], F32) for i in range(2)]
            oab = [csb(f"oab{i}", [128, DH], BF16) for i in range(2)]
            oat = [csb(f"oat{i}", [128, DH], BF16) for i in range(2)]
            linv = [csb(f"linv{i}", [128, 1], F32) for i in range(2)]
            xln2b = csb("xln2b", [128, DM], BF16)
            xmn = csb("xmn", [128, DM], F32)
            x2sb = csb("x2sb", [128, DM], BF16)
            cte = csb("cte", [128, 64], F32)
            hits = [csb(f"hits{i}", [128, 64], BF16) for i in range(QG)]
            ctsum = csb("ctsum", [128, 1], F32)
            thr = csb("thr", [128, 1], F32)
            hitc = csb("hitc", [128, 1], F32)
            ctm = csb("ctm", [128, 1], F32)
            ctb = csb("ctb", [128, 1], F32)
            st2 = csb("st2", [128, 48], F32)
            mv2 = csb("mv2", [128, 2], F32)
            ln2 = csb("ln2", [128, 1], F32)
            rstd2 = csb("rstd2", [128, 1], F32)
            ps_wo = [cps(f"ps_wo{i}", [128, 512]) for i in range(2)]
            ps_tr2 = [cps(f"ps_tr2_{i}", [128, 1024]) for i in range(2)]
            ps_ct = cps("ps_ct", [128, 64])

            qts = [grp * QG + j for j in range(QG)]
            fv = {k: P.cnt[k] for k in STORE_KEYS}

            def fence(e):
                for k, v in fv.items():
                    e.wait_sem(k, v)
            ldr_tok = {}
            stxm_tok = {}
            stx2_tok = {}
            rounds = []
            for qtg in range(QG):
                rounds.append(("oa", qtg, 0))
                if qtg >= 1:
                    for r in range(4):
                        rounds.append(("l2", qtg - 1, r))
            for r in range(4):
                rounds.append(("l2", QG - 1, r))
            RR = {key: i for i, key in enumerate(rounds)}

            def c_sync(e):
                fence(e)
                for qtg, qt in enumerate(qts):
                    if qt >= 2:
                        e.wait(f"c_sub2{qt - 2}")
                        if (qt - 2) in stxm_tok:
                            e.wait_tok(stxm_tok[qt - 2])
                    ldr_tok[qt] = e.dma(lambda qt=qt, qtg=qtg: nc.sync.dma_start(
                        out=xre[qtg % 2][:, :],
                        in_=g["x_s"][qt * 128:(qt + 1) * 128, :]),
                        f"ldr{qtg % 2}")
                    e.wait(f"c_xmidv{qt}")
                    stxm_tok[qt] = e.dma(lambda qt=qt, qtg=qtg: nc.sync.dma_start(
                        out=g["XMID"][qt * 128:(qt + 1) * 128, :],
                        in_=xre[qtg % 2][:, :]), f"stxm{qtg % 2}")
                if grp == NG - 1:
                    e.wait(f"c_cntev")
                    e.dma(lambda: nc.sync.dma_start(
                        out=g["CNTI"][:, :], in_=cntsb[:, :]), "stcnt")

            def c_gpsimd(e):
                if grp == NG - 1:
                    e.wait_sem("stcnt", 16)
                    if P.emit:
                        nc.gpsimd.collective_compute(
                            "AllReduce", ALU.add,
                            replica_groups=[list(range(NCORES))],
                            ins=[g["CNTI"].ap().opt()],
                            outs=[g["CNTO"].ap().opt()],
                        ).then_inc(P.sem["cc2"], 1)
                    P.cnt["cc2"] += 1

            def c_tensor(e):
                if grp == 0:
                    e.wait_tok(h["sh"]["wob_tok"])

                def oaT(qtg, qt):
                    rr = RR[("oa", qtg, 0)]
                    e.wait(f"c_oab{qt}")
                    if rr >= 2:
                        e.wait(f"c_trev{grp}_{rr - 2}")
                    for kc in range(DC):
                        e.mm(lambda kc=kc, rr=rr: nc.tensor.matmul(
                            ps_tr2[rr % 2][:, kc * 128:(kc + 1) * 128],
                            oab[qtg % 2][:, kc * 128:(kc + 1) * 128],
                            identB[:, :], start=True, stop=True),
                            inc=1 if kc == DC - 1 else 0)
                    e.mark(f"c_oat{qt}", "t")

                def Wo(qtg, qt):
                    e.wait(f"c_oatev{qt}")
                    for ob in range(8):
                        if ob >= 2 or qtg > 0:
                            pi = (qt, ob - 2) if ob >= 2 else (qt - 1, 6 + ob)
                            e.wait(f"c_woev{pi[0]}_{pi[1]}")
                        for dc in range(DC):
                            e.mm(lambda dc=dc, ob=ob, qtg=qtg: nc.tensor.matmul(
                                ps_wo[ob % 2][:, :],
                                oat[qtg % 2][:, dc * 128:(dc + 1) * 128],
                                WoB[:, dc, ob * 512:(ob + 1) * 512],
                                start=(dc == 0), stop=False))
                        e.mm(lambda ob=ob: nc.tensor.matmul(
                            ps_wo[ob % 2][:, :],
                            onesrB[0:1, 0:128],
                            boB[0:1, ob * 512:(ob + 1) * 512],
                            start=False, stop=True), inc=1)
                        e.mark(f"c_wo{qt}_{ob}", "t")

                def L2T(qtg, qt):
                    e.wait(f"c_x2n{qt}")
                    for r in range(4):
                        rr = RR[("l2", qtg, r)]
                        if rr >= 2:
                            e.wait(f"c_trev{grp}_{rr - 2}")
                        for j in range(8):
                            ci = r * 8 + j
                            e.mm(lambda ci=ci, j=j, rr=rr: nc.tensor.matmul(
                                ps_tr2[rr % 2][:, j * 128:(j + 1) * 128],
                                xln2b[:, ci * 128:(ci + 1) * 128],
                                identB[:, :], start=True, stop=True),
                                inc=1 if j == 7 else 0)
                        e.mark(f"c_l2t{qt}_{r}", "t")

                def cent(qtg, qt):
                    e.wait(f"c_x2ev{qt}_3")
                    if qtg > 0 or grp > 0:
                        e.wait(f"c_ctdone{qt - 1}")
                    for ci in range(CI):
                        e.mm(lambda ci=ci: nc.tensor.matmul(
                            ps_ct[:, :],
                            x2sb[:, ci * 128:(ci + 1) * 128],
                            centT[:, ci * 64:(ci + 1) * 64],
                            start=(ci == 0), stop=(ci == CI - 1)),
                            inc=1 if ci == CI - 1 else 0)
                    e.mark(f"c_ct{qt}", "t")

                for qtg, qt in enumerate(qts):
                    oaT(qtg, qt)
                    Wo(qtg, qt)
                    if qtg >= 1:
                        L2T(qtg - 1, qt - 1)
                        cent(qtg - 1, qt - 1)
                L2T(QG - 1, qts[-1])
                cent(QG - 1, qts[-1])
                for qtg, qt in enumerate(qts):
                    e.wait(f"c_hits{qt}")
                    e.mm(lambda qt=qt, qtg=qtg: nc.tensor.matmul(
                        ps_cnt[:, :], onescB[:, :], hits[qtg][:, :],
                        start=(qt == 0), stop=(qt == NT - 1),
                        skip_group_check=True), inc=1)
                    e.mark(f"c_cntmm{qt}", "t")

            def c_scalar(e):
                fence(e)

                def head(qtg, qt):
                    rr = RR[("oa", qtg, 0)]
                    e.wait(f"c_linv{qt}")
                    e.op(lambda qtg=qtg: nc.scalar.activation(
                        out=oab[qtg % 2][:, :],
                        in_=oacc[:, qtg * 1025:qtg * 1025 + DH],
                        func=AF.Copy, scale=linv[qtg % 2][:, :]))
                    e.mark(f"c_oab{qt}")
                    e.wait(f"c_oat{qt}")
                    e.op(lambda qtg=qtg, rr=rr: nc.scalar.copy(
                        oat[qtg % 2][:, :], ps_tr2[rr % 2][:, 0:DH]))
                    e.mark(f"c_oatev{qt}")
                    e.mark(f"c_trev{grp}_{rr}")

                def woevs(qtg, qt):
                    for ob in range(8):
                        e.wait(f"c_wo{qt}_{ob}")
                        if ob >= 2 or qtg > 0:
                            pi = (qt, ob - 2) if ob >= 2 else (qt - 1, 6 + ob)
                            e.wait(f"c_add{pi[0]}_{pi[1]}")
                        e.op(lambda ob=ob: nc.scalar.copy(
                            pb512[ob % 2][:, :], ps_wo[ob % 2][:, :]))
                        e.mark(f"c_woev{qt}_{ob}")

                def chain(qtg, qt):
                    e.wait(f"c_veps2{qt}")
                    e.op(lambda: nc.scalar.activation(
                        out=ln2[:, :], in_=mv2[:, 1:2], func=AF.Ln, bias=0.0))
                    e.mark(f"c_ln2m{qt}")
                    e.op(lambda: nc.scalar.activation(
                        out=rstd2[:, :], in_=ln2[:, :], func=AF.Exp, scale=-0.5))
                    e.wait(f"c_sub2{qt}")
                    if qtg > 0 or grp > 0:
                        e.wait(f"c_ct{qt - 1}")
                    e.op(lambda: nc.scalar.activation(
                        out=xln2b[:, :], in_=xmn[:, :],
                        func=AF.Copy, scale=rstd2[:, :]))
                    e.mark(f"c_x2n{qt}")
                    for r in range(4):
                        rr = RR[("l2", qtg, r)]
                        e.wait(f"c_l2t{qt}_{r}")
                        if r == 0 and qt >= 1:
                            e.wait(f"c_ct{qt - 1}")
                        if r == 0 and qt >= 1 and (qt - 1) in stx2_tok:
                            e.wait_tok(stx2_tok[qt - 1])
                        e.op(lambda r=r, rr=rr: nc.scalar.copy(
                            x2sb[:, r * 1024:(r + 1) * 1024],
                            ps_tr2[rr % 2][:, :]))
                        e.mark(f"c_x2ev{qt}_{r}")
                        e.mark(f"c_trev{grp}_{rr}")
                    stx2_tok[qt] = e.dma(lambda qt=qt: nc.scalar.dma_start(
                        out=g["XLT2"][qt, :, :], in_=x2sb[:, :]),
                        f"stx2_{qt % 2}")
                    e.wait(f"c_ct{qt}")
                    e.wait(f"c_ctb{qt}")
                    if qt >= 1:
                        e.wait(f"c_hits{qt - 1}")
                    e.op(lambda qtg=qtg: nc.scalar.activation(
                        out=cte[:, :], in_=ps_ct[:, :], func=AF.Exp,
                        scale=CTEMP, bias=ctb[:, :], accum_out=ctsum[:, :]))
                    e.mark(f"c_cte{qt}")

                for qtg, qt in enumerate(qts):
                    head(qtg, qt)
                    woevs(qtg, qt)
                    if qtg >= 1:
                        chain(qtg - 1, qt - 1)
                chain(QG - 1, qts[-1])

            def c_vector(e):
                fence(e)

                def vqt(qtg, qt):
                    if qtg >= 1:
                        e.wait(f"c_oab{qt - 1}")
                    e.op(lambda qtg=qtg: nc.vector.reciprocal(
                        linv[qtg % 2][:, :],
                        oacc[:, qtg * 1025 + DH:qtg * 1025 + DH + 1]))
                    e.mark(f"c_linv{qt}")
                    for ob in range(8):
                        e.wait(f"c_woev{qt}_{ob}")
                        if ob == 0:
                            e.wait_tok(ldr_tok[qt])
                        e.op(lambda ob=ob, qtg=qtg: nc.vector.tensor_tensor(
                            out=xre[qtg % 2][:, ob * 512:(ob + 1) * 512],
                            in0=xre[qtg % 2][:, ob * 512:(ob + 1) * 512],
                            in1=pb512[ob % 2][:, :], op=ALU.add))
                        e.mark(f"c_add{qt}_{ob}")
                        e.op(lambda ob=ob, qtg=qtg: nc.vector.bn_stats(
                            st2[:, 6 * ob:6 * ob + 6],
                            xre[qtg % 2][:, 512 * ob:512 * (ob + 1)]))
                    e.mark(f"c_xmidv{qt}")
                    if qt >= 1:
                        e.wait(f"c_ln2m{qt - 1}")
                    e.op(lambda: nc.vector.bn_aggr(mv2[:, :], st2[:, :]))
                    e.op(lambda: nc.vector.tensor_scalar(
                        out=mv2[:, 1:2], in0=mv2[:, 1:2], scalar1=EPS,
                        scalar2=None, op0=ALU.add))
                    e.mark(f"c_veps2{qt}")
                    if qt >= 1:
                        e.wait(f"c_x2n{qt - 1}")
                    e.op(lambda qtg=qtg: nc.vector.tensor_scalar(
                        out=xmn[:, :], in0=xre[qtg % 2][:, :],
                        scalar1=mv2[:, 0:1], scalar2=None, op0=ALU.subtract))
                    e.mark(f"c_sub2{qt}")

                def vtail(qtg, qt):
                    e.wait(f"c_ct{qt}")
                    e.op(lambda: nc.vector.reduce_max(
                        ctm[:, :], ps_ct[:, :], axis=AX.X))
                    e.op(lambda: nc.vector.tensor_scalar(
                        out=ctb[:, :], in0=ctm[:, :], scalar1=-CTEMP,
                        scalar2=None, op0=ALU.mult))
                    e.mark(f"c_ctb{qt}")
                    e.wait(f"c_cte{qt}")
                    e.op(lambda: nc.vector.tensor_scalar(
                        out=thr[:, :], in0=ctsum[:, :], scalar1=CTHR,
                        scalar2=None, op0=ALU.mult))
                    e.op(lambda qtg=qtg: nc.vector.tensor_scalar(
                        out=hits[qtg][:, :], in0=cte[:, :], scalar1=thr[:, :],
                        scalar2=None, op0=ALU.is_gt, op1=ALU.add,
                        accum_out=hitc[:, :]))
                    e.mark(f"c_hits{qt}")
                    e.op(lambda qt=qt: nc.vector.tensor_scalar(
                        out=qsel[:, qt:qt + 1], in0=hitc[:, :], scalar1=0.5,
                        scalar2=None, op0=ALU.is_gt))
                    e.mark(f"c_ctdone{qt}", "v")

                for qtg, qt in enumerate(qts):
                    vqt(qtg, qt)
                    if qtg >= 1:
                        vtail(qtg - 1, qt - 1)
                vtail(QG - 1, qts[-1])
                if grp == NG - 1:
                    e.wait(f"c_cntmm{NT - 1}")
                    e.op(lambda: nc.vector.tensor_copy(
                        cntsb[:, :], ps_cnt[:, :]))
                    e.mark("c_cntev")

            run_block(P, [
                ("sync", None, c_sync),
                ("gpsimd", "g", c_gpsimd),
                ("tensor", None, c_tensor),
                ("scalar", "s", c_scalar),
                ("vector", "v", c_vector),
            ], f"cblk{grp}")

    def phase_d(P, nc, env):
        g = env
        identB, onesrB = g["identB"], g["onesrB"]
        qsel = g["qsel"]

        with ExitStack() as std:
            def dsb(name, shape, dt):
                return (std.enter_context(nc.sbuf_tensor(name, shape, dt))
                        if P.emit else None)

            def dps(name, shape):
                return (std.enter_context(nc.psum_tensor(name, shape, F32))
                        if P.emit else None)

            WT = dsb("WT", [128, CI, 1024], BF16)
            bffnB = dsb("bffnB", [1, DM], BF16)
            xlt2l = [dsb(f"xlt2l{i}", [128, DM], BF16) for i in range(3)]
            xmidb = [dsb(f"xmidb{i}", [128, 1024], F32) for i in range(3)]
            outb = [dsb(f"outb{i}", [128, 1024], F32) for i in range(3)]
            dq = [dsb(f"dq{i}", [128, 512], F32) for i in range(2)]
            asgi = dsb("asgi", [1, 2048], I32)
            rowbuf = dsb("rowbuf", [1, DM], BF16)
            asgb = rowbuf
            onehot = dsb("onehot", [64, DM], BF16)
            rselR = rowbuf
            rselB = dsb("rselB", [128, DM], BF16)
            cntg = dsb("cntg", [1, 64], F32)
            cntb = dsb("cntb", [1, 64], BF16)
            cselc = dsb("cselc", [64, 1], BF16)
            ps_oh = dps("ps_oh", [64, 1024])
            ps_d = [dps(f"ps_d{i}", [128, 512]) for i in range(2)]

            ldwr_tok = {}
            ldd_tok = {}
            ldm_tok = {}
            stout_tok = {}
            fv = {k: P.cnt[k] for k in STORE_KEYS}

            def fence(e):
                for k, v in fv.items():
                    e.wait_sem(k, v)

            def d_gpsimd(e):
                fence(e)
                e.dma(lambda: nc.gpsimd.dma_start(
                    out=bffnB[:, :], in_=g["bffn"][:, :]), "ldg")
                e.mark("d_bffn", "ldg")
                for hf in range(2):
                    if hf == 1:
                        e.wait(f"d_asgc0")
                    e.dma(lambda hf=hf: nc.gpsimd.dma_start(
                        out=asgi[:, :],
                        in_=g["asg"][:, hf * 2048:(hf + 1) * 2048]), "ldcnt")


            def d_sync(e):
                fence(e)
                e.wait_sem("cc2", 1)
                e.dma(lambda: nc.sync.dma_start(
                    out=cntg[:, :], in_=g["CNTO"][:, :]), "ldd")
                NU = 4 * NT

                def wtload(oq):
                    if oq >= 1:
                        e.wait(f"d_mm{oq * NT - 1}")
                    ldwr_tok[oq] = e.dma(lambda oq=oq: nc.sync.dma_start(
                        out=WT[:, :, :],
                        in_=g["WTD"][:, :, oq * 1024:(oq + 1) * 1024]
                        .rearrange("ci p o -> p ci o")), f"ldwt{oq % 2}")
                wtload(0)

                def dloads(u):
                    oq, ti = divmod(u, NT)
                    if u >= 3:
                        e.wait(f"d_mm{u - 3}")
                    ldd_tok[u] = e.dma(lambda ti=ti, u=u: nc.sync.dma_start(
                        out=xlt2l[u % 3][:, :], in_=g["XLT2"][ti, :, :]),
                        f"ldd{u % 3}")
                    ldm_tok[u] = e.dma(lambda ti=ti, oq=oq, u=u: nc.sync.dma_start(
                        out=xmidb[u % 3][:, :],
                        in_=g["XMID"][ti * 128:(ti + 1) * 128,
                                      oq * 1024:(oq + 1) * 1024]),
                        f"ldm{u % 3}")
                for u in range(min(3, NU)):
                    dloads(u)
                for u in range(NU):
                    oq, ti = divmod(u, NT)
                    e.wait(f"d_out{u}")
                    stout_tok[u] = e.dma(
                        lambda ti=ti, oq=oq, u=u: nc.sync.dma_start(
                            out=g["out_s"][ti * 128:(ti + 1) * 128,
                                           oq * 1024:(oq + 1) * 1024],
                            in_=outb[u % 3][:, :]), f"stout{u % 3}")
                    if u + 3 < NU:
                        dloads(u + 3)
                    if (u + 1) % NT == 0 and u + 1 < NU:
                        wtload((u + 1) // NT)
                e.wait_tok(stout_tok[NU - 2])
                e.wait_tok(stout_tok[NU - 1])

            def d_tensor(e):
                for qr in range(4):
                    e.wait(f"d_asgc{qr // 2}")
                    if qr >= 1:
                        e.wait(f"d_oh{qr - 1}")
                    for b in range(2):
                        e.mm(lambda qr=qr, b=b: nc.tensor.matmul(
                            ps_oh[:, b * 512:(b + 1) * 512],
                            onesrB[0:1, 0:64],
                            asgb[0:1, qr * 1024 + b * 512:
                                 qr * 1024 + (b + 1) * 512],
                            start=True, stop=True), inc=(1 if b == 1 else 0))
                    e.mark(f"d_ohb{qr}", "t")
                e.wait(f"d_cntb")
                e.mm(lambda: nc.tensor.matmul(
                    ps_d[0][0:64, 0:1], cntb[0:1, 0:64], onesrB[0:1, 0:1],
                    start=True, stop=True), inc=1)
                e.mark("d_cc", "t")
                e.wait("d_csel")
                for b in range(8):
                    e.mm(lambda b=b: nc.tensor.matmul(
                        ps_d[0][0:1, 0:512], cselc[0:64, 0:1],
                        onehot[0:64, b * 512:(b + 1) * 512],
                        start=True, stop=True), inc=1)
                    e.mark(f"d_rs{b}", "t")
                    e.wait(f"d_rsev{b}")
                for b in range(8):
                    if b >= 2:
                        e.wait(f"d_rbev{b - 2}")
                    e.mm(lambda b=b: nc.tensor.matmul(
                        ps_d[1 - b % 2][:, 0:512], onesrB[0:1, 0:128],
                        rselR[0:1, b * 512:(b + 1) * 512],
                        start=True, stop=True), inc=1)
                    e.mark(f"d_rb{b}", "t")
                e.wait("d_bffn")
                for oq in range(4):
                    for ti in range(NT):
                        u = oq * NT + ti
                        e.wait_tok(ldd_tok[u])
                        if ti == 0:
                            e.wait_tok(ldwr_tok[oq])
                        for b in range(2):
                            if u >= 1:
                                e.wait(f"d_dqev{u - 1}_{b}")
                            elif b < 2:
                                e.wait(f"d_rbev{7 - b}")
                            for ci in range(CI):
                                e.mm(lambda u=u, ci=ci, b=b, oq=oq:
                                     nc.tensor.matmul(
                                         ps_d[b][:, :],
                                         xlt2l[u % 3][:, ci * 128:(ci + 1) * 128],
                                         WT[:, ci, b * 512:(b + 1) * 512],
                                         start=(ci == 0), stop=False))
                            e.mm(lambda oq=oq, b=b: nc.tensor.matmul(
                                ps_d[b][:, :], onesrB[0:1, 0:128],
                                bffnB[0:1, oq * 1024 + b * 512:
                                      oq * 1024 + (b + 1) * 512],
                                start=False, stop=True), inc=1)
                            e.mark(f"d_d{u}_{b}", "t")
                        e.mark(f"d_mm{u}", "t")

            def d_scalar(e):
                fence(e)
                for b in range(8):
                    e.wait(f"d_rb{b}")
                    e.op(lambda b=b: nc.scalar.copy(
                        rselB[:, b * 512:(b + 1) * 512], ps_d[1 - b % 2][:, 0:512]))
                    e.mark(f"d_rbev{b}")
                for oq in range(4):
                    for ti in range(NT):
                        u = oq * NT + ti
                        for b in range(2):
                            e.wait(f"d_d{u}_{b}")
                            if u >= 1:
                                e.wait(f"d_ob{u - 1}_{b}")
                            e.op(lambda u=u, b=b, ti=ti: nc.scalar.activation(
                                out=dq[b][:, :], in_=ps_d[b][:, :],
                                func=AF.Copy, scale=qsel[:, ti:ti + 1]))
                            e.mark(f"d_dqev{u}_{b}")

            def d_vector(e):
                fence(e)
                for hf in range(2):
                    e.wait_sem("ldcnt", 16 * (hf + 1))
                    e.op(lambda hf=hf: nc.vector.tensor_copy(
                        asgb[:, hf * 2048:(hf + 1) * 2048], asgi[:, :]))
                    e.mark(f"d_asgc{hf}")
                for qr in range(4):
                    e.wait(f"d_ohb{qr}")
                    e.op(lambda qr=qr: nc.vector.tensor_scalar(
                        out=onehot[0:64, qr * 1024:(qr + 1) * 1024],
                        in0=ps_oh[0:64, :], scalar1=g["iotaC"][:, :],
                        scalar2=None, op0=ALU.is_equal))
                    e.mark(f"d_oh{qr}")
                e.wait_sem("ldd", 16)
                e.op(lambda: nc.vector.tensor_copy(cntb[:, :], cntg[:, :]))
                e.mark("d_cntb")
                e.wait("d_cc")
                e.op(lambda: nc.vector.tensor_scalar(
                    out=cselc[:, :], in0=ps_d[0][0:64, 0:1], scalar1=0.5,
                    scalar2=None, op0=ALU.is_gt))
                e.mark("d_csel")
                for b in range(8):
                    e.wait(f"d_rs{b}")
                    if b == 0:
                        e.wait(f"d_oh3")
                    e.op(lambda b=b: nc.vector.tensor_copy(
                        rselR[:, b * 512:(b + 1) * 512], ps_d[0][0:1, 0:512]))
                    e.mark(f"d_rsev{b}")
                for oq in range(4):
                    for ti in range(NT):
                        u = oq * NT + ti
                        for b in range(2):
                            e.wait(f"d_dqev{u}_{b}")
                            if u == 0 and b == 0:
                                e.wait("d_rbev7")
                            e.op(lambda oq=oq, b=b: nc.vector.tensor_tensor(
                                out=dq[b][:, :], in0=dq[b][:, :],
                                in1=rselB[:, oq * 1024 + b * 512:
                                          oq * 1024 + (b + 1) * 512],
                                op=ALU.mult))
                            e.mark(f"d_t1_{u}_{b}")
                            if b == 0:
                                e.wait_tok(ldm_tok[u])
                                if u >= 3:
                                    e.wait_tok(stout_tok[u - 3])
                            e.op(lambda u=u, b=b: nc.vector.tensor_tensor(
                                out=outb[u % 3][:, b * 512:(b + 1) * 512],
                                in0=dq[b][:, :],
                                in1=xmidb[u % 3][:, b * 512:(b + 1) * 512],
                                op=ALU.add))
                            e.mark(f"d_ob{u}_{b}")
                        e.mark(f"d_out{u}")

            run_block(P, [
                ("gpsimd", "g", d_gpsimd),
                ("sync", None, d_sync),
                ("tensor", None, d_tensor),
                ("scalar", "s", d_scalar),
                ("vector", "v", d_vector),
            ], "phaseD")

    def dbg_block(P, nc, env, dbg):
        def dbg_sync(e):
            e.wait_sem("cc1", P.cnt["cc1"])
            for key in ["stq0", "stq1", "stkv0", "stkv1", "stxlt0", "stxlt1",
                        "stxm0", "stxm1", "stx2_0", "stx2_1"]:
                e.wait_sem(key, P.cnt[key])
            for name in dbg:
                if name == "A_ONLY":
                    continue
                src = env[name]
                dt = F32 if name in ("XMID",) else BF16
                shape = list(src.shape) if P.emit else [1, 1]
                dst = (nc.dram_tensor("DBG_" + name, shape, dt,
                                      kind="ExternalOutput") if P.emit else None)
                e.dma(lambda src=src, dst=dst: nc.sync.dma_start(
                    out=dst.ap(), in_=src.ap()), "ldd")
            e.wait_sem("ldd", P.cnt["ldd"])
        run_block(P, [("sync", None, dbg_sync)], "dbg")

    P.emit = False
    build_all()
    P.emit = True
    P.reset_counts()
    nc = bass.Bass("TRN2", target_bir_lowering=False)
    P.nc = nc
    with ExitStack() as stack:
        P.stack = stack
        for k in SEMS:
            P.sem[k] = stack.enter_context(nc.semaphore(k))
        build_all()
    return nc



def make_inputs(L0_attn_x, L0_attn_Wqkv, L0_attn_bqkv, L0_attn_Wo, L0_attn_bo,
                L0_ffn_weight, L0_ffn_bias, L0_ffn_centroids,
                L0_ffn_assignments, T):
    x = np.ascontiguousarray(np.asarray(L0_attn_x, dtype=np.float32)
                             .reshape(-1, DM))
    common = {
        "wqkv": np.ascontiguousarray(L0_attn_Wqkv, dtype=np.float32),
        "bqkv": np.ascontiguousarray(L0_attn_bqkv, dtype=np.float32).reshape(1, -1),
        "wo": np.ascontiguousarray(L0_attn_Wo, dtype=np.float32),
        "bo": np.ascontiguousarray(L0_attn_bo, dtype=np.float32).reshape(1, -1),
        "wffn": np.ascontiguousarray(L0_ffn_weight, dtype=np.float32),
        "bffn": np.ascontiguousarray(L0_ffn_bias, dtype=np.float32).reshape(1, -1),
        "cent": np.ascontiguousarray(L0_ffn_centroids, dtype=np.float32),
        "asg": np.ascontiguousarray(L0_ffn_assignments, dtype=np.int32).reshape(1, -1),
        "ident": np.eye(128, dtype=np.float32),
        "onesr": np.ones((1, 512), dtype=np.float32),
        "onesc": np.ones((128, 1), dtype=np.float32),
        "iota64": np.arange(64, dtype=np.float32).reshape(64, 1),
    }
    return [dict(common, x_s=np.ascontiguousarray(x[r * T:(r + 1) * T]))
            for r in range(NCORES)]


def kernel(L0_attn_x, L0_attn_Wqkv, L0_attn_bqkv, L0_attn_Wo, L0_attn_bo,
           L0_ffn_weight, L0_ffn_bias, L0_ffn_centroids, L0_ffn_assignments):
    T = 16384 // NCORES
    nc = build(T=T)
    in_maps = make_inputs(L0_attn_x, L0_attn_Wqkv, L0_attn_bqkv, L0_attn_Wo,
                          L0_attn_bo, L0_ffn_weight, L0_ffn_bias,
                          L0_ffn_centroids, L0_ffn_assignments, T)
    res = run_bass_kernel_spmd(nc, in_maps, core_ids=list(range(NCORES)))
    out = np.concatenate([res.results[r]["out_s"] for r in range(NCORES)], axis=0)
    return out.reshape(1, 16384, 4096)


# version 23
# speedup vs baseline: 1.1624x; 1.1624x over previous
import numpy as np
import concourse.bass as bass
import concourse.mybir as mybir
from concourse.bass_utils import run_bass_kernel_spmd
from contextlib import ExitStack

F32 = mybir.dt.float32
BF16 = mybir.dt.bfloat16
I32 = mybir.dt.int32
AF = mybir.ActivationFunctionType
ALU = mybir.AluOpType
AX = mybir.AxisListType

NCORES = 8
DM = 4096
CI = DM // 128
DH = 1024
DC = DH // 128
FQKV = 3072
EPS = 1e-5
ISCALE = 1.0 / 32.0
EMARGIN = 3.0
CTEMP = 10.0
CTHR = 0.01

STORE_KEYS = ["stxlt0", "stxlt1", "stq0", "stq1", "stkv0", "stkv1",
              "stxm0", "stxm1", "stx2_0", "stx2_1", "stcnt"]

SEMS = ["v", "s", "t", "g",
        "ldx0", "ldx1", "ldx8_0", "ldx8_1", "ldw0", "ldw1", "ldg",
        "ldk0", "ldk1", "ldv0", "ldv1", "ldq", "ldr0", "ldr1",
        "ldd0", "ldd1", "ldd2", "ldm0", "ldm1", "ldm2", "ldwr0", "ldwr1", "ldwr2", "ldcnt", "ldd",
        "stxlt0", "stxlt1", "stq0", "stq1", "stkv0", "stkv1",
        "stxm0", "stxm1", "stx2_0", "stx2_1", "stout0", "stout1", "stout2", "stcnt",
        "cc1", "cc2"]


class Prog:
    def __init__(self):
        self.emit = False
        self.marks = {}
        self.cnt = {k: 0 for k in SEMS}
        self.sem = {}
        self.nc = None
        self.stack = None

    def reset_counts(self):
        self.cnt = {k: 0 for k in SEMS}


class Eng:
    def __init__(self, P, name, prog_key=None):
        self.P = P
        self.name = name
        self.key = prog_key

    def _eng(self):
        return getattr(self.P.nc, self.name)

    def op(self, f, inc=1):
        P = self.P
        if P.emit:
            if self.key:
                self._eng().wait_ge(P.sem[self.key], P.cnt[self.key])
            instr = f()
            if self.key and inc:
                instr.then_inc(P.sem[self.key], inc)
        if self.key and inc:
            P.cnt[self.key] += inc

    def mm(self, f, inc=0):
        P = self.P
        if P.emit:
            instr = f()
            if inc:
                instr.then_inc(P.sem["t"], inc)
        if inc:
            P.cnt["t"] += inc

    def dma(self, f, sem_key):
        P = self.P
        if P.emit:
            if self.key:
                self._eng().wait_ge(P.sem[self.key], P.cnt[self.key])
            f().then_inc(P.sem[sem_key], 16)
        P.cnt[sem_key] += 16
        return (sem_key, P.cnt[sem_key])

    def wait(self, mark_name):
        P = self.P
        if P.emit:
            sem_key, val = P.marks[mark_name]
            self._eng().wait_ge(P.sem[sem_key], val)

    def wait_sem(self, sem_key, val):
        if val <= 0:
            return
        if self.P.emit:
            self._eng().wait_ge(self.P.sem[sem_key], val)

    def wait_tok(self, tok):
        if tok is not None:
            self.wait_sem(*tok)

    def mark(self, name, sem_key=None):
        P = self.P
        key = sem_key or self.key
        cur = (key, P.cnt[key])
        if not P.emit:
            P.marks[name] = cur
        else:
            assert P.marks.get(name) == cur, (
                f"mark mismatch {name}: dry {P.marks.get(name)} vs emit {cur}")


def run_block(P, bodies, name=None):
    if not P.emit:
        for ename, key, fn in bodies:
            fn(Eng(P, ename, key))
    else:
        with P.nc.Block(name, no_gpsimd_drain=True) as block:
            for ename, key, fn in bodies:
                deco = getattr(block, ename)

                def body(_eng, fn=fn, ename=ename, key=key):
                    fn(Eng(P, ename, key))
                deco(body)



def build(T=2048, dbg=()):
    NT = T // 128
    NSEG = NT // 2
    KB = T // 2
    NKB = 2 * NCORES
    KSB = max(1, KB // 512)
    NSC = min(512, KB)
    KC = KB // 128
    QG = min(4, NT)
    NG = NT // QG

    P = Prog()

    def build_all():
        nc = P.nc

        def dram(name, shape, dt, kind=None):
            if not P.emit:
                return None
            if kind:
                return nc.dram_tensor(name, shape, dt, kind=kind)
            return nc.dram_tensor(name, shape, dt)

        def sbuf(name, shape, dt):
            if not P.emit:
                return None
            return P.stack.enter_context(nc.sbuf_tensor(name, shape, dt))

        x_s = dram("x_s", [T, DM], F32, "ExternalInput")
        wqkv = dram("wqkv", [DM, FQKV], F32, "ExternalInput")
        bqkv = dram("bqkv", [1, FQKV], F32, "ExternalInput")
        wo = dram("wo", [DH, DM], F32, "ExternalInput")
        bo = dram("bo", [1, DM], F32, "ExternalInput")
        wffn = dram("wffn", [DM, DM], F32, "ExternalInput")
        bffn = dram("bffn", [1, DM], F32, "ExternalInput")
        cent = dram("cent", [64, DM], F32, "ExternalInput")
        asg = dram("asg", [1, DM], I32, "ExternalInput")
        ident = dram("ident", [128, 128], F32, "ExternalInput")
        onesr = dram("onesr", [1, 512], F32, "ExternalInput")
        onesc = dram("onesc", [128, 1], F32, "ExternalInput")
        iota64 = dram("iota64", [64, 1], F32, "ExternalInput")
        out_s = dram("out_s", [T, DM], F32, "ExternalOutput")

        XLT = dram("XLT", [NT, 128, DM], BF16)
        XLT2 = dram("XLT2", [NT, 128, DM], BF16)
        XMID = dram("XMID", [T, DM], F32)
        KVIN = dram("KVIN", [2048, T], BF16)
        KGk = (nc.dram_tensor("KGk", [NCORES * 1024, T], BF16,
                              addr_space="Shared") if P.emit else None)
        KGv = (nc.dram_tensor("KGv", [NCORES * 1024, T], BF16,
                              addr_space="Shared") if P.emit else None)
        QTD = dram("QTD", [DC, 128, T], BF16)
        CNTI = dram("CNTI", [1, 64], F32)
        CNTO = dram("CNTO", [1, 64], F32)

        identB = sbuf("identB", [128, 128], BF16)
        onesrB = sbuf("onesrB", [1, 512], BF16)
        onescB = sbuf("onescB", [128, 1], BF16)
        iotaC = sbuf("iotaC", [64, 1], F32)
        bqkvB = sbuf("bqkvB", [1, FQKV], BF16)
        boB = sbuf("boB", [1, DM], BF16)
        bffnB = sbuf("bffnB", [1, DM], BF16)
        centT = sbuf("centT", [128, CI * 64], BF16)
        qsel = sbuf("qsel", [128, NT], F32)

        NCONST = 8

        phase_a(P, nc, locals())
        phase_bc(P, nc, locals())
        phase_d(P, nc, locals())
        if dbg:
            dbg_block(P, nc, locals(), dbg)

    def phase_a(P, nc, env):
        g = env
        identB, onesrB, onescB = g["identB"], g["onesrB"], g["onescB"]
        bqkvB, centT = g["bqkvB"], g["centT"]

        with ExitStack() as sta0:
            wreg = ([sta0.enter_context(nc.sbuf_tensor(f"wreg{i}", [128, CI, 512], BF16))
                     for i in range(2)] if P.emit else [None, None])
            phase_a_inner(P, nc, env, wreg)

    def phase_a_inner(P, nc, env, wreg):
        g = env
        identB, onesrB, onescB = g["identB"], g["onesrB"], g["onescB"]
        bqkvB, centT = g["bqkvB"], g["centT"]
        hpseq0 = [2, 3, 4, 5, 0, 1]

        with ExitStack() as sta:
            def asb(name, shape, dt):
                return (sta.enter_context(nc.sbuf_tensor(name, shape, dt))
                        if P.emit else None)

            def aps(name, shape):
                return (sta.enter_context(nc.psum_tensor(name, shape, F32))
                        if P.emit else None)

            xbuf = [asb(f"xbuf{i}", [128, DM], F32) for i in range(2)]
            stats = asb("stats", [128, 48], F32)
            mv = asb("mv", [128, 2], F32)
            lnv = asb("lnv", [128, 1], F32)
            rstd = asb("rstd", [128, 1], F32)
            xlnb = [asb(f"xlnb{i}", [128, DM], BF16) for i in range(2)]
            xltsb = [asb(f"xltsb{i}", [128, DM], BF16) for i in range(2)]
            centS = asb("centS", [64, DM], BF16)
            trps = [aps(f"trps{i}", [128, 1024]) for i in range(2)]

            def a_gpsimd(e):
                e.dma(lambda: nc.gpsimd.dma_start(out=identB[:, :], in_=g["ident"][:, :]), "ldg")
                e.dma(lambda: nc.gpsimd.dma_start(out=onesrB[:, :], in_=g["onesr"][:, :]), "ldg")
                e.dma(lambda: nc.gpsimd.dma_start(out=onescB[:, :], in_=g["onesc"][:, :]), "ldg")
                e.dma(lambda: nc.gpsimd.dma_start(out=g["iotaC"][:, :], in_=g["iota64"][:, :]), "ldg")
                e.dma(lambda: nc.gpsimd.dma_start(out=bqkvB[:, :], in_=g["bqkv"][:, :]), "ldg")
                e.dma(lambda: nc.gpsimd.dma_start(out=g["boB"][:, :], in_=g["bo"][:, :]), "ldg")
                e.dma(lambda: nc.gpsimd.dma_start(out=g["bffnB"][:, :], in_=g["bffn"][:, :]), "ldg")
                e.dma(lambda: nc.gpsimd.dma_start(out=centS[:, :], in_=g["cent"][:, :]), "ldg")
                for pos in range(2):
                    off = hpseq0[pos] * 512
                    e.dma(lambda off=off, pos=pos: nc.gpsimd.dma_start(
                        out=wreg[pos % 2][:, :, :],
                        in_=g["wqkv"][:, off:off + 512].rearrange(
                            "(ci p) f -> p ci f", p=128)), f"ldw{pos % 2}")

            def a_sync(e):
                for ti in range(NT):
                    if ti >= 2:
                        e.wait(f"a_xnorm{ti - 2}")
                    e.dma(lambda ti=ti: nc.sync.dma_start(
                        out=xbuf[ti % 2][:, :],
                        in_=g["x_s"][ti * 128:(ti + 1) * 128, :]), f"ldx{ti % 2}")

            def a_vector(e):
                for ti in range(NT):
                    e.wait_sem(f"ldx{ti % 2}", 16 * (ti // 2 + 1))
                    if ti >= 1:
                        e.wait(f"a_ln{ti - 1}")
                    for j in range(8):
                        e.op(lambda j=j, ti=ti: nc.vector.bn_stats(
                            stats[:, 6 * j:6 * j + 6],
                            xbuf[ti % 2][:, 512 * j:512 * (j + 1)]))
                    e.op(lambda: nc.vector.bn_aggr(mv[:, :], stats[:, :]))
                    e.mark(f"a_aggr{ti}")
                    e.op(lambda: nc.vector.tensor_scalar(
                        out=mv[:, 1:2], in0=mv[:, 1:2],
                        scalar1=EPS, scalar2=None, op0=ALU.add))
                    e.mark(f"a_veps{ti}")
                    e.op(lambda ti=ti: nc.vector.tensor_scalar(
                        out=xbuf[ti % 2][:, :], in0=xbuf[ti % 2][:, :],
                        scalar1=mv[:, 0:1], scalar2=None, op0=ALU.subtract))
                    e.mark(f"a_sub{ti}")

            def a_scalar(e):
                for ti in range(NT):
                    e.wait(f"a_veps{ti}")
                    e.op(lambda: nc.scalar.activation(
                        out=lnv[:, :], in_=mv[:, 1:2], func=AF.Ln, bias=0.0))
                    e.mark(f"a_ln{ti}")
                    e.op(lambda: nc.scalar.activation(
                        out=rstd[:, :], in_=lnv[:, :], func=AF.Exp, scale=-0.5))
                    e.wait(f"a_sub{ti}")
                    if ti >= 2:
                        e.wait(f"a_tr{ti - 2}_3")
                    e.op(lambda ti=ti: nc.scalar.activation(
                        out=xlnb[ti % 2][:, :], in_=xbuf[ti % 2][:, :],
                        func=AF.Copy, scale=rstd[:, :]))
                    e.mark(f"a_xnorm{ti}")
                    for grp in range(4):
                        e.wait(f"a_tr{ti}_{grp}")
                        e.op(lambda ti=ti, grp=grp: nc.scalar.copy(
                            xltsb[ti % 2][:, grp * 1024:(grp + 1) * 1024],
                            trps[grp % 2][:, :]))
                        e.mark(f"a_trev{ti * 4 + grp}")
                    if ti >= 2:
                        e.wait_sem(f"stxlt{ti % 2}", 16 * (ti // 2))
                    e.dma(lambda ti=ti: nc.scalar.dma_start(
                        out=g["XLT"][ti, :, :], in_=xltsb[ti % 2][:, :]), f"stxlt{ti % 2}")
                for cg in range(2):
                    e.wait(f"a_ctr{cg}")
                    e.op(lambda cg=cg: nc.scalar.copy(
                        centT[:, cg * 1024:(cg + 1) * 1024], trps[cg % 2][:, :]))
                    e.mark(f"a_ctrev{cg}")

            def a_tensor(e):
                e.wait_sem("ldg", 16 * 8)
                for ti in range(NT):
                    e.wait(f"a_xnorm{ti}")
                    for grp in range(4):
                        G = ti * 4 + grp
                        if G >= 2:
                            e.wait(f"a_trev{G - 2}")
                        for j in range(8):
                            ci = grp * 8 + j
                            e.mm(lambda ti=ti, grp=grp, ci=ci, j=j:
                                 nc.tensor.matmul(
                                     trps[grp % 2][:, j * 128:(j + 1) * 128],
                                     xlnb[ti % 2][:, ci * 128:(ci + 1) * 128],
                                     identB[:, :], start=True, stop=True),
                                 inc=1 if j == 7 else 0)
                        e.mark(f"a_tr{ti}_{grp}", "t")
                for cg in range(2):
                    e.wait(f"a_trev{4 * NT - 2 + cg}")
                    for j in range(16):
                        ci = cg * 16 + j
                        e.mm(lambda ci=ci, cg=cg, j=j: nc.tensor.matmul(
                            trps[cg % 2][:, j * 64:(j + 1) * 64],
                            centS[0:64, ci * 128:(ci + 1) * 128],
                            identB[0:64, 0:64], start=True, stop=True),
                            inc=1 if j == 15 else 0)
                    e.mark(f"a_ctr{cg}", "t")

            run_block(P, [
                ("gpsimd", "g", a_gpsimd),
                ("sync", None, a_sync),
                ("vector", "v", a_vector),
                ("scalar", "s", a_scalar),
                ("tensor", None, a_tensor),
            ], "phaseA0")

        QS = min(512, T)
        NQ = T // QS
        TPQ = QS // 128

        with ExitStack() as sta1:
            def asb1(name, shape, dt):
                return (sta1.enter_context(nc.sbuf_tensor(name, shape, dt))
                        if P.emit else None)

            def aps1(name, shape):
                return (sta1.enter_context(nc.psum_tensor(name, shape, F32))
                        if P.emit else None)

            xlt4 = [asb1(f"xlt4_{i}", [128, TPQ, DM], BF16) for i in range(2)]
            qst = [asb1(f"qst{i}", [128, 512], BF16) for i in range(2)]
            qkps = [aps1(f"qkps{i}", [128, 512]) for i in range(2)]

            vview = (g["KVIN"][1024:2048, :].flatten()
                     .rearrange("(t d) -> t d", d=1024) if P.emit else None)

            hpseq = [2, 3, 4, 5, 0, 1]
            items = []
            for pos, hp in enumerate(hpseq):
                for q4 in range(NQ):
                    for k in range(4 if hp < 4 else TPQ):
                        items.append((pos, hp, q4, k))
            n_kv = sum(1 for (pos, hp, q4, k) in items if hp >= 2)
            n_k = sum(1 for (pos, hp, q4, k) in items if hp in (2, 3))

            fvA = {k: P.cnt[k] for k in ["stxlt0", "stxlt1"]}

            def a1_gpsimd(e):
                for k, v in fvA.items():
                    e.wait_sem(k, v)
                for pos, hp in enumerate(hpseq):
                    if pos < 2:
                        continue
                    off = hp * 512
                    e.wait(f"a_wfree{pos - 2}")
                    e.dma(lambda off=off, pos=pos: nc.gpsimd.dma_start(
                        out=wreg[pos % 2][:, :, :],
                        in_=g["wqkv"][:, off:off + 512].rearrange(
                            "(ci p) f -> p ci f", p=128)), f"ldw{pos % 2}")
                e.wait("a_kdone0")
                e.wait("a_kdone1")
                if P.emit:
                    nc.gpsimd.collective_compute(
                        "AllGather", ALU.bypass,
                        replica_groups=[list(range(NCORES))],
                        ins=[g["KVIN"][0:1024, :].opt()],
                        outs=[g["KGk"].ap().opt()],
                    ).then_inc(P.sem["cc1"], 1)
                P.cnt["cc1"] += 1
                e.wait("a_vdone0")
                e.wait("a_vdone1")
                if P.emit:
                    nc.gpsimd.collective_compute(
                        "AllGather", ALU.bypass,
                        replica_groups=[list(range(NCORES))],
                        ins=[g["KVIN"][1024:2048, :].opt()],
                        outs=[g["KGv"].ap().opt()],
                    ).then_inc(P.sem["cc1"], 1)
                P.cnt["cc1"] += 1

            def a1_sync(e):
                for k, v in fvA.items():
                    e.wait_sem(k, v)
                for pos in range(6):
                    for q4 in range(NQ):
                        idx = pos * NQ + q4
                        if idx >= 2:
                            e.wait(f"a_x4free{idx - 2}")
                        e.dma(lambda q4=q4, idx=idx: nc.sync.dma_start(
                            out=xlt4[idx % 2][:, :, :],
                            in_=g["XLT"][q4 * TPQ:(q4 + 1) * TPQ, :, :]
                            .rearrange("ti p e -> p ti e")), f"ldx8_{idx % 2}")

            def a1_scalar(e):
                stg_tok = {0: None, 1: None}
                for k2, v2 in fvA.items():
                    e.wait_sem(k2, v2)
                for u, (pos, hp, q4, k) in enumerate(items):
                    pas, half = hp // 2, hp % 2
                    w = QS if pas < 2 else 512
                    e.wait(f"a_mm{u}")
                    e.wait_tok(stg_tok[u % 2])
                    e.op(lambda u=u, w=w: nc.scalar.copy(
                        qst[u % 2][:, 0:w], qkps[u % 2][:, 0:w]))
                    e.mark(f"a_ev{u}")
                    if pas == 0:
                        fi = half * 4 + k
                        tok = e.dma(lambda fi=fi, q4=q4, u=u:
                                    nc.scalar.dma_start(
                                        out=g["QTD"][fi, :, q4 * QS:(q4 + 1) * QS],
                                        in_=qst[u % 2][:, 0:QS]), f"stq{u % 2}")
                    elif pas == 1:
                        fi = half * 4 + k
                        tok = e.dma(lambda fi=fi, q4=q4, u=u:
                                    nc.scalar.dma_start(
                                        out=g["KVIN"][fi * 128:(fi + 1) * 128,
                                                      q4 * QS:(q4 + 1) * QS],
                                        in_=qst[u % 2][:, 0:QS]), f"stkv{u % 2}")
                    else:
                        ti = q4 * TPQ + k
                        tok = e.dma(lambda ti=ti, half=half, u=u:
                                    nc.scalar.dma_start(
                                        out=vview[ti * 128:(ti + 1) * 128,
                                                  half * 512:(half + 1) * 512],
                                        in_=qst[u % 2][:, :]), f"stkv{u % 2}")
                    stg_tok[u % 2] = tok
                    if u == n_k - 1:
                        e.mark("a_kdone0", "stkv0")
                        e.mark("a_kdone1", "stkv1")
                    if u == n_kv - 1:
                        e.mark("a_vdone0", "stkv0")
                        e.mark("a_vdone1", "stkv1")

            def a1_tensor(e):
                for u, (pos, hp, q4, k) in enumerate(items):
                    pas, half = hp // 2, hp % 2
                    if k == 0 and q4 == 0:
                        e.wait_sem(f"ldw{pos % 2}", 16 * (pos // 2 + 1))
                    idx = pos * NQ + q4
                    if k == 0:
                        e.wait_sem(f"ldx8_{idx % 2}", 16 * (idx // 2 + 1))
                    if u >= 2:
                        e.wait(f"a_ev{u - 2}")
                    if pas < 2:
                        for ci in range(CI):
                            e.mm(lambda k=k, ci=ci, u=u, pos=pos, idx=idx:
                                 nc.tensor.matmul(
                                     qkps[u % 2][:, 0:QS],
                                     wreg[pos % 2][:, ci, k * 128:(k + 1) * 128],
                                     xlt4[idx % 2][:, :, ci * 128:(ci + 1) * 128],
                                     start=(ci == 0), stop=False))
                        boff = pas * 1024 + half * 512 + k * 128
                        e.mm(lambda boff=boff, u=u: nc.tensor.matmul(
                            qkps[u % 2][:, 0:QS],
                            bqkvB[0:1, boff:boff + 128],
                            onesrB[0:1, 0:QS],
                            start=False, stop=True), inc=1)
                    else:
                        for ci in range(CI):
                            e.mm(lambda k=k, ci=ci, u=u, pos=pos, idx=idx:
                                 nc.tensor.matmul(
                                     qkps[u % 2][:, 0:512],
                                     xlt4[idx % 2][:, k, ci * 128:(ci + 1) * 128],
                                     wreg[pos % 2][:, ci, :],
                                     start=(ci == 0), stop=False))
                        boff = 2048 + half * 512
                        e.mm(lambda boff=boff, u=u: nc.tensor.matmul(
                            qkps[u % 2][:, 0:512],
                            onesrB[0:1, 0:128],
                            bqkvB[0:1, boff:boff + 512],
                            start=False, stop=True), inc=1)
                    e.mark(f"a_mm{u}", "t")
                    if k == (4 if pas < 2 else TPQ) - 1:
                        e.mark(f"a_x4free{idx}", "t")
                        if q4 == NQ - 1:
                            e.mark(f"a_wfree{pos}", "t")

            run_block(P, [
                ("gpsimd", "g", a1_gpsimd),
                ("sync", None, a1_sync),
                ("scalar", "s", a1_scalar),
                ("tensor", None, a1_tensor),
            ], "phaseA1")
    def phase_bc(P, nc, env):
        g = env
        identB, onesrB, onescB = g["identB"], g["onesrB"], g["onescB"]
        boB, centT, qsel = g["boB"], g["centT"], g["qsel"]

        with ExitStack() as stb:
            def bsb(name, shape, dt):
                return (stb.enter_context(nc.sbuf_tensor(name, shape, dt))
                        if P.emit else None)

            def bps(name, shape):
                return (stb.enter_context(nc.psum_tensor(name, shape, F32))
                        if P.emit else None)

            WoB = bsb("WoB", [128, DC, DM], BF16)
            QTg = bsb("QTg", [128, DC, QG * 128], BF16)
            oacc = bsb("oacc", [128, QG * 1025], F32)
            colC = bsb("colC", [128, 1], F32)
            auxb = bsb("auxb", [128, 2], BF16)
            pmax = bsb("pmax", [128, 1], F32)
            m1 = bsb("m1", [1, 1], F32)
            cntsb = bsb("cntsb", [1, 64], F32)
            ps_cnt = bps("ps_cnt", [1, 64])

            sh = {}
            for grp in range(NG):
                flash_block(P, nc, env, grp, dict(
                    WoB=WoB, QTg=QTg, oacc=oacc, colC=colC, auxb=auxb,
                    pmax=pmax, m1=m1, sh=sh))
                c_block(P, nc, env, grp, dict(
                    WoB=WoB, oacc=oacc, cntsb=cntsb,
                    ps_cnt=ps_cnt, sh=sh))

    def flash_block(P, nc, env, grp, h):
        g = env
        identB, onesrB = g["identB"], g["onesrB"]
        QTg, oacc = h["QTg"], h["oacc"]
        colC, auxb = h["colC"], h["auxb"]
        NQC = QG * 128

        with ExitStack() as stf:
            def fsb(name, shape, dt):
                return (stf.enter_context(nc.sbuf_tensor(f"{name}_g{grp}", shape, dt))
                        if P.emit else None)

            def fps(name, shape):
                return (stf.enter_context(nc.psum_tensor(f"{name}_g{grp}", shape, F32))
                        if P.emit else None)

            Kb = [fsb(f"Kb{i}", [128, DC, KB], BF16) for i in range(2)]
            Vb = [fsb(f"Vb{i}", [128, KC, 1032], BF16) for i in range(2)]
            PT = [fsb(f"PT{i}", [128, KC, NQC], BF16) for i in range(2)]
            pvs = [fsb(f"pvs{i}", [128, 1025], F32) for i in range(2)]
            ps_sT = [fps(f"ps_sT{i}", [128, NQC]) for i in range(2)]
            ps_pv = [fps(f"ps_pv{i}", [128, DH]) for i in range(2)]
            ps_l = fps("ps_l", [128, 512])

            fv = {k: P.cnt[k] for k in STORE_KEYS}

            def fence(e):
                for k, v in fv.items():
                    e.wait_sem(k, v)

            def b_gpsimd(e):
                fence(e)
                if grp == 0:
                    h["sh"]["wob_tok"] = e.dma(lambda: nc.gpsimd.dma_start(
                        out=h["WoB"][:, :, :],
                        in_=g["wo"].rearrange("(dc p) c -> p dc c", p=128)),
                        "ldw0")

            def b_sync(e):
                fence(e)
                if grp > 0:
                    e.wait(f"b_qtfree{grp - 1}")
                e.dma(lambda: nc.sync.dma_start(
                    out=QTg[:, :, :],
                    in_=g["QTD"][:, :, grp * NQC:(grp + 1) * NQC]
                    .rearrange("dc p t -> p dc t")), "ldq")
                if grp == 0:
                    e.wait_sem("cc1", 1)
                for kb in range(NKB):
                    r, hf = divmod(kb, 2)
                    if kb >= 2:
                        e.wait(f"b_kfree{grp}_{kb - 2}")
                    e.dma(lambda r=r, hf=hf, kb=kb: nc.sync.dma_start(
                        out=Kb[kb % 2][:, :, :],
                        in_=g["KGk"][r * 1024:(r + 1) * 1024,
                                     hf * KB:(hf + 1) * KB]
                        .rearrange("(dc p) j -> p dc j", p=128)),
                        f"ldk{kb % 2}")
                    if grp == 0 and kb == 0:
                        e.wait_sem("cc1", 2)
                    if kb >= 2:
                        e.wait(f"b_vfree{grp}_{kb - 2}")
                    e.dma(lambda r=r, hf=hf, kb=kb: nc.sync.dma_start(
                        out=Vb[kb % 2][:, :, 0:DH],
                        in_=g["KGv"][r * 1024:(r + 1) * 1024, :]
                        .flatten().rearrange("(t d) -> t d", d=DH)
                        [hf * KB:(hf + 1) * KB, :]
                        .rearrange("(kc p) d -> p kc d", p=128)),
                        f"ldv{kb % 2}")

            def b_tensor(e):
                def S(kb, kc):
                    si = kb * KC + kc
                    if kc == 0:
                        e.wait_sem(f"ldk{kb % 2}", 16 * (kb // 2 + 1 + grp * NKB // 2))
                        if kb == 0:
                            e.wait_sem("ldq", 16 * (grp + 1))
                    if si >= 2:
                        p = si - 2
                        e.wait(f"b_exp{grp}_{p // KC}_{p % KC}")
                    for dc in range(DC):
                        e.mm(lambda kc=kc, dc=dc, kb=kb, si=si: nc.tensor.matmul(
                            ps_sT[si % 2][:, :],
                            Kb[kb % 2][:, dc, kc * 128:(kc + 1) * 128],
                            QTg[:, dc, :],
                            start=(dc == 0), stop=(dc == DC - 1)),
                            inc=1 if dc == DC - 1 else 0)
                    e.mark(f"b_sT{grp}_{kb}_{kc}", "t")
                    if kc == KC - 1:
                        e.mark(f"b_kfree{grp}_{kb}", "t")
                        if kb == NKB - 1:
                            e.mark(f"b_qtfree{grp}", "t")

                def PV(kb, qt):
                    if qt == 0:
                        e.wait(f"b_exp{grp}_{kb}_{KC - 1}")
                        e.wait_sem(f"ldv{kb % 2}", 16 * (kb // 2 + 1 + grp * NKB // 2))
                        e.wait(f"b_vset{grp}_{kb % 2}")
                    a = kb * QG + qt
                    if a >= 2:
                        pa = a - 2
                        e.wait(f"b_pvev{grp}_{pa // QG}_{pa % QG}")
                    for kc in range(KC):
                        for sb in range(2):
                            e.mm(lambda kc=kc, kb=kb, qt=qt, sb=sb, a=a:
                                 nc.tensor.matmul(
                                     ps_pv[a % 2][:, sb * 512:(sb + 1) * 512],
                                     PT[kb % 2][:, kc, qt * 128:(qt + 1) * 128],
                                     Vb[kb % 2][:, kc, sb * 512:(sb + 1) * 512],
                                     start=(kc == 0), stop=(kc == KC - 1)))
                    for kc in range(KC):
                        e.mm(lambda kc=kc, kb=kb, qt=qt, a=a:
                             nc.tensor.matmul(
                                 ps_l[:, (a % 2):(a % 2) + 1],
                                 PT[kb % 2][:, kc, qt * 128:(qt + 1) * 128],
                                 Vb[kb % 2][:, kc, 1024:1025],
                                 start=(kc == 0), stop=(kc == KC - 1),
                                 skip_group_check=True),
                             inc=1 if kc == KC - 1 else 0)
                    e.mark(f"b_PV{grp}_{kb}_{qt}", "t")
                    if qt == QG - 1:
                        e.mark(f"b_vfree{grp}_{kb}", "t")

                def aux1():
                    e.wait(f"b_pmaxb{grp}")
                    e.mm(lambda: nc.tensor.matmul(
                        ps_l[0:1, 0:128], auxb[:, 0:1], identB[:, :],
                        start=True, stop=True, skip_group_check=True), inc=1)
                    e.mark(f"b_aux1{grp}", "t")

                def aux2():
                    e.wait(f"b_m1b{grp}")
                    e.mm(lambda: nc.tensor.matmul(
                        ps_l[:, 0:1], onesrB[0:1, 0:128], auxb[0:1, 1:2],
                        start=True, stop=True, skip_group_check=True), inc=1)
                    e.mark(f"b_aux2{grp}", "t")

                S(0, 0)
                if KC > 1:
                    S(0, 1)
                aux1()
                aux2()
                for kc in range(min(2, KC), KC):
                    S(0, kc)
                for kb in range(1, NKB):
                    for kc in range(KC):
                        S(kb, kc)
                    for qt in range(QG):
                        PV(kb - 1, qt)
                for qt in range(QG):
                    PV(NKB - 1, qt)

            def b_scalar(e):
                fence(e)
                for kb in range(NKB):
                    for kc in range(KC):
                        e.wait(f"b_sT{grp}_{kb}_{kc}")
                        if kb == 0 and kc == 0:
                            e.wait(f"b_colC{grp}")
                        si = kb * KC + kc
                        if si >= 2 * KC:
                            e.wait(f"b_PV{grp}_{kb - 2}_{QG - 1}")
                        e.op(lambda kb=kb, kc=kc, si=si: nc.scalar.activation(
                            out=PT[kb % 2][:, kc, :], in_=ps_sT[si % 2][:, :],
                            func=AF.Exp, scale=ISCALE, bias=colC[:, :]))
                        e.mark(f"b_exp{grp}_{kb}_{kc}")
                    if kb >= 1:
                        for qt in range(QG):
                            e.wait(f"b_PV{grp}_{kb - 1}_{qt}")
                            a = (kb - 1) * QG + qt
                            if a >= 2:
                                pa = a - 2
                                e.wait(f"b_acc{grp}_{pa // QG}_{pa % QG}")
                            e.op(lambda qt=qt, a=a: nc.scalar.copy(
                                pvs[qt % 2][:, 0:DH], ps_pv[a % 2][:, :]))
                            e.op(lambda qt=qt, a=a: nc.scalar.copy(
                                pvs[qt % 2][:, DH:DH + 1],
                                ps_l[:, (a % 2):(a % 2) + 1]))
                            e.mark(f"b_pvev{grp}_{kb - 1}_{qt}")
                for qt in range(QG):
                    e.wait(f"b_PV{grp}_{NKB - 1}_{qt}")
                    a = (NKB - 1) * QG + qt
                    pa = a - 2
                    e.wait(f"b_acc{grp}_{pa // QG}_{pa % QG}")
                    e.op(lambda qt=qt, a=a: nc.scalar.copy(
                        pvs[qt % 2][:, 0:DH], ps_pv[a % 2][:, :]))
                    e.op(lambda qt=qt, a=a: nc.scalar.copy(
                        pvs[qt % 2][:, DH:DH + 1],
                        ps_l[:, (a % 2):(a % 2) + 1]))
                    e.mark(f"b_pvev{grp}_{NKB - 1}_{qt}")

            def b_vector(e):
                e.wait(f"b_sT{grp}_0_0")
                e.op(lambda: nc.vector.reduce_max(
                    h["pmax"][:, :], ps_sT[0][:, :], axis=AX.X))
                e.op(lambda: nc.vector.tensor_copy(auxb[:, 0:1], h["pmax"][:, :]))
                e.mark(f"b_pmaxb{grp}")
                e.wait(f"b_aux1{grp}")
                e.op(lambda: nc.vector.reduce_max(
                    h["m1"][:, :], ps_l[0:1, 0:128], axis=AX.X))
                e.op(lambda: nc.vector.tensor_copy(auxb[0:1, 1:2], h["m1"][:, :]))
                e.mark(f"b_m1b{grp}")
                e.wait(f"b_aux2{grp}")
                e.op(lambda: nc.vector.tensor_scalar(
                    out=colC[:, :], in0=ps_l[:, 0:1], scalar1=-ISCALE,
                    scalar2=-EMARGIN, op0=ALU.mult, op1=ALU.add))
                e.mark(f"b_colC{grp}")
                for kb in range(NKB):
                    for qt in range(QG):
                        e.wait(f"b_pvev{grp}_{kb}_{qt}")
                        if kb == 0:
                            e.op(lambda qt=qt: nc.vector.tensor_copy(
                                oacc[:, qt * 1025:(qt + 1) * 1025],
                                pvs[qt % 2][:, :]))
                        else:
                            e.op(lambda qt=qt: nc.vector.tensor_tensor(
                                out=oacc[:, qt * 1025:(qt + 1) * 1025],
                                in0=oacc[:, qt * 1025:(qt + 1) * 1025],
                                in1=pvs[qt % 2][:, :], op=ALU.add))
                        e.mark(f"b_acc{grp}_{kb}_{qt}")

            def b_memset(e):
                for i in range(2):
                    e.op(lambda i=i: nc.vector.memset(
                        Vb[i][:, :, 1024:1025], 1.0))
                    e.mark(f"b_vset{grp}_{i}", "v")

            def b_vector_all(e):
                fence(e)
                b_memset(e)
                b_vector(e)

            run_block(P, [
                ("gpsimd", "g", b_gpsimd),
                ("sync", None, b_sync),
                ("tensor", None, b_tensor),
                ("scalar", "s", b_scalar),
                ("vector", "v", b_vector_all),
            ], f"flash{grp}")

    def c_block(P, nc, env, grp, h):
        g = env
        identB, onesrB, onescB = g["identB"], g["onesrB"], g["onescB"]
        boB, centT, qsel = g["boB"], g["centT"], g["qsel"]
        WoB, oacc = h["WoB"], h["oacc"]
        ps_cnt, cntsb = h["ps_cnt"], h["cntsb"]

        with ExitStack() as stc:
            def csb(name, shape, dt):
                return (stc.enter_context(nc.sbuf_tensor(f"{name}_g{grp}", shape, dt))
                        if P.emit else None)

            def cps(name, shape):
                return (stc.enter_context(nc.psum_tensor(f"{name}_g{grp}", shape, F32))
                        if P.emit else None)

            xre = [csb(f"xre{i}", [128, DM], F32) for i in range(2)]
            pb512 = [csb(f"pb512_{i}", [128, 512], F32) for i in range(2)]
            oab = [csb(f"oab{i}", [128, DH], BF16) for i in range(2)]
            oat = [csb(f"oat{i}", [128, DH], BF16) for i in range(2)]
            linv = [csb(f"linv{i}", [128, 1], F32) for i in range(2)]
            xln2b = csb("xln2b", [128, DM], BF16)
            xmn = csb("xmn", [128, DM], F32)
            x2sb = csb("x2sb", [128, DM], BF16)
            cte = csb("cte", [128, 64], F32)
            hits = [csb(f"hits{i}", [128, 64], BF16) for i in range(QG)]
            ctsum = csb("ctsum", [128, 1], F32)
            thr = csb("thr", [128, 1], F32)
            hitc = csb("hitc", [128, 1], F32)
            ctm = csb("ctm", [128, 1], F32)
            ctb = csb("ctb", [128, 1], F32)
            st2 = csb("st2", [128, 48], F32)
            mv2 = csb("mv2", [128, 2], F32)
            ln2 = csb("ln2", [128, 1], F32)
            rstd2 = csb("rstd2", [128, 1], F32)
            ps_wo = [cps(f"ps_wo{i}", [128, 512]) for i in range(2)]
            ps_tr2 = [cps(f"ps_tr2_{i}", [128, 1024]) for i in range(2)]
            ps_ct = cps("ps_ct", [128, 64])

            qts = [grp * QG + j for j in range(QG)]
            fv = {k: P.cnt[k] for k in STORE_KEYS}

            def fence(e):
                for k, v in fv.items():
                    e.wait_sem(k, v)
            ldr_tok = {}
            stxm_tok = {}
            stx2_tok = {}
            rounds = []
            for qtg in range(QG):
                rounds.append(("oa", qtg, 0))
                if qtg >= 1:
                    for r in range(4):
                        rounds.append(("l2", qtg - 1, r))
            for r in range(4):
                rounds.append(("l2", QG - 1, r))
            RR = {key: i for i, key in enumerate(rounds)}

            def c_sync(e):
                fence(e)
                for qtg, qt in enumerate(qts):
                    if qt >= 2:
                        e.wait(f"c_sub2{qt - 2}")
                        if (qt - 2) in stxm_tok:
                            e.wait_tok(stxm_tok[qt - 2])
                    ldr_tok[qt] = e.dma(lambda qt=qt, qtg=qtg: nc.sync.dma_start(
                        out=xre[qtg % 2][:, :],
                        in_=g["x_s"][qt * 128:(qt + 1) * 128, :]),
                        f"ldr{qtg % 2}")
                    e.wait(f"c_xmidv{qt}")
                    stxm_tok[qt] = e.dma(lambda qt=qt, qtg=qtg: nc.sync.dma_start(
                        out=g["XMID"][qt * 128:(qt + 1) * 128, :],
                        in_=xre[qtg % 2][:, :]), f"stxm{qtg % 2}")
                if grp == NG - 1:
                    e.wait(f"c_cntev")
                    e.dma(lambda: nc.sync.dma_start(
                        out=g["CNTI"][:, :], in_=cntsb[:, :]), "stcnt")

            def c_gpsimd(e):
                if grp == NG - 1:
                    e.wait_sem("stcnt", 16)
                    if P.emit:
                        nc.gpsimd.collective_compute(
                            "AllReduce", ALU.add,
                            replica_groups=[list(range(NCORES))],
                            ins=[g["CNTI"].ap().opt()],
                            outs=[g["CNTO"].ap().opt()],
                        ).then_inc(P.sem["cc2"], 1)
                    P.cnt["cc2"] += 1

            def c_tensor(e):
                if grp == 0:
                    e.wait_tok(h["sh"]["wob_tok"])

                def oaT(qtg, qt):
                    rr = RR[("oa", qtg, 0)]
                    e.wait(f"c_oab{qt}")
                    if rr >= 2:
                        e.wait(f"c_trev{grp}_{rr - 2}")
                    for kc in range(DC):
                        e.mm(lambda kc=kc, rr=rr: nc.tensor.matmul(
                            ps_tr2[rr % 2][:, kc * 128:(kc + 1) * 128],
                            oab[qtg % 2][:, kc * 128:(kc + 1) * 128],
                            identB[:, :], start=True, stop=True),
                            inc=1 if kc == DC - 1 else 0)
                    e.mark(f"c_oat{qt}", "t")

                def Wo(qtg, qt):
                    e.wait(f"c_oatev{qt}")
                    for ob in range(8):
                        if ob >= 2 or qtg > 0:
                            pi = (qt, ob - 2) if ob >= 2 else (qt - 1, 6 + ob)
                            e.wait(f"c_woev{pi[0]}_{pi[1]}")
                        for dc in range(DC):
                            e.mm(lambda dc=dc, ob=ob, qtg=qtg: nc.tensor.matmul(
                                ps_wo[ob % 2][:, :],
                                oat[qtg % 2][:, dc * 128:(dc + 1) * 128],
                                WoB[:, dc, ob * 512:(ob + 1) * 512],
                                start=(dc == 0), stop=False))
                        e.mm(lambda ob=ob: nc.tensor.matmul(
                            ps_wo[ob % 2][:, :],
                            onesrB[0:1, 0:128],
                            boB[0:1, ob * 512:(ob + 1) * 512],
                            start=False, stop=True), inc=1)
                        e.mark(f"c_wo{qt}_{ob}", "t")

                def L2T(qtg, qt):
                    e.wait(f"c_x2n{qt}")
                    for r in range(4):
                        rr = RR[("l2", qtg, r)]
                        if rr >= 2:
                            e.wait(f"c_trev{grp}_{rr - 2}")
                        for j in range(8):
                            ci = r * 8 + j
                            e.mm(lambda ci=ci, j=j, rr=rr: nc.tensor.matmul(
                                ps_tr2[rr % 2][:, j * 128:(j + 1) * 128],
                                xln2b[:, ci * 128:(ci + 1) * 128],
                                identB[:, :], start=True, stop=True),
                                inc=1 if j == 7 else 0)
                        e.mark(f"c_l2t{qt}_{r}", "t")

                def cent(qtg, qt):
                    e.wait(f"c_x2ev{qt}_3")
                    if qtg > 0 or grp > 0:
                        e.wait(f"c_ctdone{qt - 1}")
                    for ci in range(CI):
                        e.mm(lambda ci=ci: nc.tensor.matmul(
                            ps_ct[:, :],
                            x2sb[:, ci * 128:(ci + 1) * 128],
                            centT[:, ci * 64:(ci + 1) * 64],
                            start=(ci == 0), stop=(ci == CI - 1)),
                            inc=1 if ci == CI - 1 else 0)
                    e.mark(f"c_ct{qt}", "t")

                for qtg, qt in enumerate(qts):
                    oaT(qtg, qt)
                    Wo(qtg, qt)
                    if qtg >= 1:
                        L2T(qtg - 1, qt - 1)
                        cent(qtg - 1, qt - 1)
                L2T(QG - 1, qts[-1])
                cent(QG - 1, qts[-1])
                for qtg, qt in enumerate(qts):
                    e.wait(f"c_hits{qt}")
                    e.mm(lambda qt=qt, qtg=qtg: nc.tensor.matmul(
                        ps_cnt[:, :], onescB[:, :], hits[qtg][:, :],
                        start=(qt == 0), stop=(qt == NT - 1),
                        skip_group_check=True), inc=1)
                    e.mark(f"c_cntmm{qt}", "t")

            def c_scalar(e):
                fence(e)

                def head(qtg, qt):
                    rr = RR[("oa", qtg, 0)]
                    e.wait(f"c_linv{qt}")
                    e.op(lambda qtg=qtg: nc.scalar.activation(
                        out=oab[qtg % 2][:, :],
                        in_=oacc[:, qtg * 1025:qtg * 1025 + DH],
                        func=AF.Copy, scale=linv[qtg % 2][:, :]))
                    e.mark(f"c_oab{qt}")
                    e.wait(f"c_oat{qt}")
                    e.op(lambda qtg=qtg, rr=rr: nc.scalar.copy(
                        oat[qtg % 2][:, :], ps_tr2[rr % 2][:, 0:DH]))
                    e.mark(f"c_oatev{qt}")
                    e.mark(f"c_trev{grp}_{rr}")

                def woevs(qtg, qt):
                    for ob in range(8):
                        e.wait(f"c_wo{qt}_{ob}")
                        if ob >= 2 or qtg > 0:
                            pi = (qt, ob - 2) if ob >= 2 else (qt - 1, 6 + ob)
                            e.wait(f"c_add{pi[0]}_{pi[1]}")
                        e.op(lambda ob=ob: nc.scalar.copy(
                            pb512[ob % 2][:, :], ps_wo[ob % 2][:, :]))
                        e.mark(f"c_woev{qt}_{ob}")

                def chain(qtg, qt):
                    e.wait(f"c_veps2{qt}")
                    e.op(lambda: nc.scalar.activation(
                        out=ln2[:, :], in_=mv2[:, 1:2], func=AF.Ln, bias=0.0))
                    e.mark(f"c_ln2m{qt}")
                    e.op(lambda: nc.scalar.activation(
                        out=rstd2[:, :], in_=ln2[:, :], func=AF.Exp, scale=-0.5))
                    e.wait(f"c_sub2{qt}")
                    if qtg > 0 or grp > 0:
                        e.wait(f"c_ct{qt - 1}")
                    e.op(lambda: nc.scalar.activation(
                        out=xln2b[:, :], in_=xmn[:, :],
                        func=AF.Copy, scale=rstd2[:, :]))
                    e.mark(f"c_x2n{qt}")
                    for r in range(4):
                        rr = RR[("l2", qtg, r)]
                        e.wait(f"c_l2t{qt}_{r}")
                        if r == 0 and qt >= 1:
                            e.wait(f"c_ct{qt - 1}")
                        if r == 0 and qt >= 1 and (qt - 1) in stx2_tok:
                            e.wait_tok(stx2_tok[qt - 1])
                        e.op(lambda r=r, rr=rr: nc.scalar.copy(
                            x2sb[:, r * 1024:(r + 1) * 1024],
                            ps_tr2[rr % 2][:, :]))
                        e.mark(f"c_x2ev{qt}_{r}")
                        e.mark(f"c_trev{grp}_{rr}")
                    stx2_tok[qt] = e.dma(lambda qt=qt: nc.scalar.dma_start(
                        out=g["XLT2"][qt, :, :], in_=x2sb[:, :]),
                        f"stx2_{qt % 2}")
                    e.wait(f"c_ct{qt}")
                    e.wait(f"c_ctb{qt}")
                    if qt >= 1:
                        e.wait(f"c_hits{qt - 1}")
                    e.op(lambda qtg=qtg: nc.scalar.activation(
                        out=cte[:, :], in_=ps_ct[:, :], func=AF.Exp,
                        scale=CTEMP, bias=ctb[:, :], accum_out=ctsum[:, :]))
                    e.mark(f"c_cte{qt}")

                for qtg, qt in enumerate(qts):
                    head(qtg, qt)
                    woevs(qtg, qt)
                    if qtg >= 1:
                        chain(qtg - 1, qt - 1)
                chain(QG - 1, qts[-1])

            def c_vector(e):
                fence(e)

                def vqt(qtg, qt):
                    if qtg >= 1:
                        e.wait(f"c_oab{qt - 1}")
                    e.op(lambda qtg=qtg: nc.vector.reciprocal(
                        linv[qtg % 2][:, :],
                        oacc[:, qtg * 1025 + DH:qtg * 1025 + DH + 1]))
                    e.mark(f"c_linv{qt}")
                    for ob in range(8):
                        e.wait(f"c_woev{qt}_{ob}")
                        if ob == 0:
                            e.wait_tok(ldr_tok[qt])
                        e.op(lambda ob=ob, qtg=qtg: nc.vector.tensor_tensor(
                            out=xre[qtg % 2][:, ob * 512:(ob + 1) * 512],
                            in0=xre[qtg % 2][:, ob * 512:(ob + 1) * 512],
                            in1=pb512[ob % 2][:, :], op=ALU.add))
                        e.mark(f"c_add{qt}_{ob}")
                        e.op(lambda ob=ob, qtg=qtg: nc.vector.bn_stats(
                            st2[:, 6 * ob:6 * ob + 6],
                            xre[qtg % 2][:, 512 * ob:512 * (ob + 1)]))
                    e.mark(f"c_xmidv{qt}")
                    if qt >= 1:
                        e.wait(f"c_ln2m{qt - 1}")
                    e.op(lambda: nc.vector.bn_aggr(mv2[:, :], st2[:, :]))
                    e.op(lambda: nc.vector.tensor_scalar(
                        out=mv2[:, 1:2], in0=mv2[:, 1:2], scalar1=EPS,
                        scalar2=None, op0=ALU.add))
                    e.mark(f"c_veps2{qt}")
                    if qt >= 1:
                        e.wait(f"c_x2n{qt - 1}")
                    e.op(lambda qtg=qtg: nc.vector.tensor_scalar(
                        out=xmn[:, :], in0=xre[qtg % 2][:, :],
                        scalar1=mv2[:, 0:1], scalar2=None, op0=ALU.subtract))
                    e.mark(f"c_sub2{qt}")

                def vtail(qtg, qt):
                    e.wait(f"c_ct{qt}")
                    e.op(lambda: nc.vector.reduce_max(
                        ctm[:, :], ps_ct[:, :], axis=AX.X))
                    e.op(lambda: nc.vector.tensor_scalar(
                        out=ctb[:, :], in0=ctm[:, :], scalar1=-CTEMP,
                        scalar2=None, op0=ALU.mult))
                    e.mark(f"c_ctb{qt}")
                    e.wait(f"c_cte{qt}")
                    e.op(lambda: nc.vector.tensor_scalar(
                        out=thr[:, :], in0=ctsum[:, :], scalar1=CTHR,
                        scalar2=None, op0=ALU.mult))
                    e.op(lambda qtg=qtg: nc.vector.tensor_scalar(
                        out=hits[qtg][:, :], in0=cte[:, :], scalar1=thr[:, :],
                        scalar2=None, op0=ALU.is_gt, op1=ALU.add,
                        accum_out=hitc[:, :]))
                    e.mark(f"c_hits{qt}")
                    e.op(lambda qt=qt: nc.vector.tensor_scalar(
                        out=qsel[:, qt:qt + 1], in0=hitc[:, :], scalar1=0.5,
                        scalar2=None, op0=ALU.is_gt))
                    e.mark(f"c_ctdone{qt}", "v")

                for qtg, qt in enumerate(qts):
                    vqt(qtg, qt)
                    if qtg >= 1:
                        vtail(qtg - 1, qt - 1)
                vtail(QG - 1, qts[-1])
                if grp == NG - 1:
                    e.wait(f"c_cntmm{NT - 1}")
                    e.op(lambda: nc.vector.tensor_copy(
                        cntsb[:, :], ps_cnt[:, :]))
                    e.mark("c_cntev")

            run_block(P, [
                ("sync", None, c_sync),
                ("gpsimd", "g", c_gpsimd),
                ("tensor", None, c_tensor),
                ("scalar", "s", c_scalar),
                ("vector", "v", c_vector),
            ], f"cblk{grp}")

    def phase_d(P, nc, env):
        g = env
        identB, onesrB = g["identB"], g["onesrB"]
        bffnB, qsel = g["bffnB"], g["qsel"]

        with ExitStack() as std:
            def dsb(name, shape, dt):
                return (std.enter_context(nc.sbuf_tensor(name, shape, dt))
                        if P.emit else None)

            def dps(name, shape):
                return (std.enter_context(nc.psum_tensor(name, shape, F32))
                        if P.emit else None)

            WT = dsb("WT", [128, CI, 1024], BF16)
            wrow = [dsb(f"wrow{i}", [128, DM], BF16) for i in range(3)]
            xlt2l = [dsb(f"xlt2l{i}", [128, DM], BF16) for i in range(3)]
            xmidb = [dsb(f"xmidb{i}", [128, 1024], F32) for i in range(3)]
            outb = [dsb(f"outb{i}", [128, 1024], F32) for i in range(3)]
            dq = [dsb(f"dq{i}", [128, 512], F32) for i in range(2)]
            asgi = dsb("asgi", [1, 2048], I32)
            rowbuf = dsb("rowbuf", [1, DM], BF16)
            asgb = rowbuf
            onehot = dsb("onehot", [64, DM], BF16)
            rselR = rowbuf
            rselB = dsb("rselB", [128, DM], BF16)
            cntg = dsb("cntg", [1, 64], F32)
            cntb = dsb("cntb", [1, 64], BF16)
            cselc = dsb("cselc", [64, 1], BF16)
            ps_oh = dps("ps_oh", [64, 1024])
            ps_d = [dps(f"ps_d{i}", [128, 512]) for i in range(2)]
            ps_wtr = [dps(f"ps_wtr{i}", [128, 1024]) for i in range(2)]

            ldwr_tok = {}
            ldd_tok = {}
            ldm_tok = {}
            stout_tok = {}
            fv = {k: P.cnt[k] for k in STORE_KEYS}

            def fence(e):
                for k, v in fv.items():
                    e.wait_sem(k, v)

            def d_gpsimd(e):
                fence(e)
                for hf in range(2):
                    if hf == 1:
                        e.wait(f"d_asgc0")
                    e.dma(lambda hf=hf: nc.gpsimd.dma_start(
                        out=asgi[:, :],
                        in_=g["asg"][:, hf * 2048:(hf + 1) * 2048]), "ldcnt")
                def wrows(oq):
                    for ob in range(8):
                        u = oq * 8 + ob
                        if u >= 3:
                            e.wait(f"d_wt{u - 3}")
                        ldwr_tok[u] = e.dma(
                            lambda oq=oq, ob=ob, u=u: nc.gpsimd.dma_start(
                                out=wrow[u % 3][:, :],
                                in_=g["wffn"][oq * 1024 + ob * 128:
                                              oq * 1024 + (ob + 1) * 128, :]),
                            f"ldwr{u % 3}")
                for oq in range(4):
                    wrows(oq)

            def d_sync(e):
                fence(e)
                e.wait_sem("cc2", 1)
                e.dma(lambda: nc.sync.dma_start(
                    out=cntg[:, :], in_=g["CNTO"][:, :]), "ldd")
                NU = 4 * NT

                def dloads(u):
                    oq, ti = divmod(u, NT)
                    if u >= 3:
                        e.wait(f"d_mm{u - 3}")
                    ldd_tok[u] = e.dma(lambda ti=ti, u=u: nc.sync.dma_start(
                        out=xlt2l[u % 3][:, :], in_=g["XLT2"][ti, :, :]),
                        f"ldd{u % 3}")
                    ldm_tok[u] = e.dma(lambda ti=ti, oq=oq, u=u: nc.sync.dma_start(
                        out=xmidb[u % 3][:, :],
                        in_=g["XMID"][ti * 128:(ti + 1) * 128,
                                      oq * 1024:(oq + 1) * 1024]),
                        f"ldm{u % 3}")
                for u in range(min(3, NU)):
                    dloads(u)
                for u in range(NU):
                    oq, ti = divmod(u, NT)
                    e.wait(f"d_out{u}")
                    stout_tok[u] = e.dma(
                        lambda ti=ti, oq=oq, u=u: nc.sync.dma_start(
                            out=g["out_s"][ti * 128:(ti + 1) * 128,
                                           oq * 1024:(oq + 1) * 1024],
                            in_=outb[u % 3][:, :]), f"stout{u % 3}")
                    if u + 3 < NU:
                        dloads(u + 3)
                e.wait_tok(stout_tok[NU - 2])
                e.wait_tok(stout_tok[NU - 1])

            def d_tensor(e):
                for qr in range(4):
                    e.wait(f"d_asgc{qr // 2}")
                    if qr >= 1:
                        e.wait(f"d_oh{qr - 1}")
                    for b in range(2):
                        e.mm(lambda qr=qr, b=b: nc.tensor.matmul(
                            ps_oh[:, b * 512:(b + 1) * 512],
                            onesrB[0:1, 0:64],
                            asgb[0:1, qr * 1024 + b * 512:
                                 qr * 1024 + (b + 1) * 512],
                            start=True, stop=True), inc=(1 if b == 1 else 0))
                    e.mark(f"d_ohb{qr}", "t")
                def wprep(oq):
                    for ob in range(8):
                        u = oq * 8 + ob
                        e.wait_tok(ldwr_tok[u])
                        if oq > 0 and ob == 0:
                            e.wait(f"d_mm{oq * NT - 1}")
                        for r in range(4):
                            ur = u * 4 + r
                            if ur >= 2:
                                e.wait(f"d_wtev{ur - 2}")
                            for j in range(8):
                                ci = r * 8 + j
                                e.mm(lambda u=u, ci=ci, j=j, ur=ur:
                                     nc.tensor.matmul(
                                         ps_wtr[ur % 2][:, j * 128:(j + 1) * 128],
                                         wrow[u % 3][:, ci * 128:(ci + 1) * 128],
                                         identB[:, :], start=True, stop=True),
                                     inc=1 if j == 7 else 0)
                            e.mark(f"d_wtr{ur}", "t")
                        e.mark(f"d_wt{u}", "t")
                wprep(0)
                e.wait(f"d_cntb")
                e.mm(lambda: nc.tensor.matmul(
                    ps_d[0][0:64, 0:1], cntb[0:1, 0:64], onesrB[0:1, 0:1],
                    start=True, stop=True), inc=1)
                e.mark("d_cc", "t")
                e.wait("d_csel")
                for b in range(8):
                    e.mm(lambda b=b: nc.tensor.matmul(
                        ps_d[0][0:1, 0:512], cselc[0:64, 0:1],
                        onehot[0:64, b * 512:(b + 1) * 512],
                        start=True, stop=True), inc=1)
                    e.mark(f"d_rs{b}", "t")
                    e.wait(f"d_rsev{b}")
                for b in range(8):
                    if b >= 2:
                        e.wait(f"d_rbev{b - 2}")
                    e.mm(lambda b=b: nc.tensor.matmul(
                        ps_d[1 - b % 2][:, 0:512], onesrB[0:1, 0:128],
                        rselR[0:1, b * 512:(b + 1) * 512],
                        start=True, stop=True), inc=1)
                    e.mark(f"d_rb{b}", "t")
                for oq in range(4):
                    if oq > 0:
                        wprep(oq)
                    for ti in range(NT):
                        u = oq * NT + ti
                        e.wait_tok(ldd_tok[u])
                        if ti == 0:
                            e.wait(f"d_wtev{(oq * 8 + 7) * 4 + 3}")
                        for b in range(2):
                            if u >= 1:
                                e.wait(f"d_dqev{u - 1}_{b}")
                            elif b < 2:
                                e.wait(f"d_rbev{7 - b}")
                            for ci in range(CI):
                                e.mm(lambda u=u, ci=ci, b=b, oq=oq:
                                     nc.tensor.matmul(
                                         ps_d[b][:, :],
                                         xlt2l[u % 3][:, ci * 128:(ci + 1) * 128],
                                         WT[:, ci, b * 512:(b + 1) * 512],
                                         start=(ci == 0), stop=False))
                            e.mm(lambda oq=oq, b=b: nc.tensor.matmul(
                                ps_d[b][:, :], onesrB[0:1, 0:128],
                                bffnB[0:1, oq * 1024 + b * 512:
                                      oq * 1024 + (b + 1) * 512],
                                start=False, stop=True), inc=1)
                            e.mark(f"d_d{u}_{b}", "t")
                        e.mark(f"d_mm{u}", "t")

            def d_scalar(e):
                fence(e)
                def wtev(oq):
                    for ob in range(8):
                        u = oq * 8 + ob
                        for r in range(4):
                            ur = u * 4 + r
                            e.wait(f"d_wtr{ur}")
                            e.op(lambda ob=ob, r=r, ur=ur: nc.scalar.copy(
                                WT[:, r * 8:(r + 1) * 8,
                                   ob * 128:(ob + 1) * 128],
                                ps_wtr[ur % 2][:, :].rearrange(
                                    "p (a b) -> p a b", a=8)))
                            e.mark(f"d_wtev{ur}")
                wtev(0)
                for b in range(8):
                    e.wait(f"d_rb{b}")
                    e.op(lambda b=b: nc.scalar.copy(
                        rselB[:, b * 512:(b + 1) * 512], ps_d[1 - b % 2][:, 0:512]))
                    e.mark(f"d_rbev{b}")
                for oq in range(4):
                    if oq > 0:
                        wtev(oq)
                    for ti in range(NT):
                        u = oq * NT + ti
                        for b in range(2):
                            e.wait(f"d_d{u}_{b}")
                            if u >= 1:
                                e.wait(f"d_ob{u - 1}_{b}")
                            e.op(lambda u=u, b=b, ti=ti: nc.scalar.activation(
                                out=dq[b][:, :], in_=ps_d[b][:, :],
                                func=AF.Copy, scale=qsel[:, ti:ti + 1]))
                            e.mark(f"d_dqev{u}_{b}")

            def d_vector(e):
                fence(e)
                for hf in range(2):
                    e.wait_sem("ldcnt", 16 * (hf + 1))
                    e.op(lambda hf=hf: nc.vector.tensor_copy(
                        asgb[:, hf * 2048:(hf + 1) * 2048], asgi[:, :]))
                    e.mark(f"d_asgc{hf}")
                for qr in range(4):
                    e.wait(f"d_ohb{qr}")
                    e.op(lambda qr=qr: nc.vector.tensor_scalar(
                        out=onehot[0:64, qr * 1024:(qr + 1) * 1024],
                        in0=ps_oh[0:64, :], scalar1=g["iotaC"][:, :],
                        scalar2=None, op0=ALU.is_equal))
                    e.mark(f"d_oh{qr}")
                e.wait_sem("ldd", 16)
                e.op(lambda: nc.vector.tensor_copy(cntb[:, :], cntg[:, :]))
                e.mark("d_cntb")
                e.wait("d_cc")
                e.op(lambda: nc.vector.tensor_scalar(
                    out=cselc[:, :], in0=ps_d[0][0:64, 0:1], scalar1=0.5,
                    scalar2=None, op0=ALU.is_gt))
                e.mark("d_csel")
                for b in range(8):
                    e.wait(f"d_rs{b}")
                    if b == 0:
                        e.wait(f"d_oh3")
                    e.op(lambda b=b: nc.vector.tensor_copy(
                        rselR[:, b * 512:(b + 1) * 512], ps_d[0][0:1, 0:512]))
                    e.mark(f"d_rsev{b}")
                for oq in range(4):
                    for ti in range(NT):
                        u = oq * NT + ti
                        for b in range(2):
                            e.wait(f"d_dqev{u}_{b}")
                            if u == 0 and b == 0:
                                e.wait("d_rbev7")
                            e.op(lambda oq=oq, b=b: nc.vector.tensor_tensor(
                                out=dq[b][:, :], in0=dq[b][:, :],
                                in1=rselB[:, oq * 1024 + b * 512:
                                          oq * 1024 + (b + 1) * 512],
                                op=ALU.mult))
                            e.mark(f"d_t1_{u}_{b}")
                            if b == 0:
                                e.wait_tok(ldm_tok[u])
                                if u >= 3:
                                    e.wait_tok(stout_tok[u - 3])
                            e.op(lambda u=u, b=b: nc.vector.tensor_tensor(
                                out=outb[u % 3][:, b * 512:(b + 1) * 512],
                                in0=dq[b][:, :],
                                in1=xmidb[u % 3][:, b * 512:(b + 1) * 512],
                                op=ALU.add))
                            e.mark(f"d_ob{u}_{b}")
                        e.mark(f"d_out{u}")

            run_block(P, [
                ("gpsimd", "g", d_gpsimd),
                ("sync", None, d_sync),
                ("tensor", None, d_tensor),
                ("scalar", "s", d_scalar),
                ("vector", "v", d_vector),
            ], "phaseD")

    def dbg_block(P, nc, env, dbg):
        def dbg_sync(e):
            e.wait_sem("cc1", P.cnt["cc1"])
            for key in ["stq0", "stq1", "stkv0", "stkv1", "stxlt0", "stxlt1",
                        "stxm0", "stxm1", "stx2_0", "stx2_1"]:
                e.wait_sem(key, P.cnt[key])
            for name in dbg:
                if name == "A_ONLY":
                    continue
                src = env[name]
                dt = F32 if name in ("XMID",) else BF16
                shape = list(src.shape) if P.emit else [1, 1]
                dst = (nc.dram_tensor("DBG_" + name, shape, dt,
                                      kind="ExternalOutput") if P.emit else None)
                e.dma(lambda src=src, dst=dst: nc.sync.dma_start(
                    out=dst.ap(), in_=src.ap()), "ldd")
            e.wait_sem("ldd", P.cnt["ldd"])
        run_block(P, [("sync", None, dbg_sync)], "dbg")

    P.emit = False
    build_all()
    P.emit = True
    P.reset_counts()
    nc = bass.Bass("TRN2", target_bir_lowering=False)
    P.nc = nc
    with ExitStack() as stack:
        P.stack = stack
        for k in SEMS:
            P.sem[k] = stack.enter_context(nc.semaphore(k))
        build_all()
    return nc



def make_inputs(L0_attn_x, L0_attn_Wqkv, L0_attn_bqkv, L0_attn_Wo, L0_attn_bo,
                L0_ffn_weight, L0_ffn_bias, L0_ffn_centroids,
                L0_ffn_assignments, T):
    x = np.ascontiguousarray(np.asarray(L0_attn_x, dtype=np.float32)
                             .reshape(-1, DM))
    common = {
        "wqkv": np.ascontiguousarray(L0_attn_Wqkv, dtype=np.float32),
        "bqkv": np.ascontiguousarray(L0_attn_bqkv, dtype=np.float32).reshape(1, -1),
        "wo": np.ascontiguousarray(L0_attn_Wo, dtype=np.float32),
        "bo": np.ascontiguousarray(L0_attn_bo, dtype=np.float32).reshape(1, -1),
        "wffn": np.ascontiguousarray(L0_ffn_weight, dtype=np.float32),
        "bffn": np.ascontiguousarray(L0_ffn_bias, dtype=np.float32).reshape(1, -1),
        "cent": np.ascontiguousarray(L0_ffn_centroids, dtype=np.float32),
        "asg": np.ascontiguousarray(L0_ffn_assignments, dtype=np.int32).reshape(1, -1),
        "ident": np.eye(128, dtype=np.float32),
        "onesr": np.ones((1, 512), dtype=np.float32),
        "onesc": np.ones((128, 1), dtype=np.float32),
        "iota64": np.arange(64, dtype=np.float32).reshape(64, 1),
    }
    return [dict(common, x_s=np.ascontiguousarray(x[r * T:(r + 1) * T]))
            for r in range(NCORES)]


def kernel(L0_attn_x, L0_attn_Wqkv, L0_attn_bqkv, L0_attn_Wo, L0_attn_bo,
           L0_ffn_weight, L0_ffn_bias, L0_ffn_centroids, L0_ffn_assignments):
    T = 16384 // NCORES
    nc = build(T=T)
    in_maps = make_inputs(L0_attn_x, L0_attn_Wqkv, L0_attn_bqkv, L0_attn_Wo,
                          L0_attn_bo, L0_ffn_weight, L0_ffn_bias,
                          L0_ffn_centroids, L0_ffn_assignments, T)
    res = run_bass_kernel_spmd(nc, in_maps, core_ids=list(range(NCORES)))
    out = np.concatenate([res.results[r]["out_s"] for r in range(NCORES)], axis=0)
    return out.reshape(1, 16384, 4096)


# version 24 (final)
# speedup vs baseline: 1.1641x; 1.0014x over previous
import numpy as np
import concourse.bass as bass
import concourse.mybir as mybir
from concourse.bass_utils import run_bass_kernel_spmd
from contextlib import ExitStack

F32 = mybir.dt.float32
BF16 = mybir.dt.bfloat16
I32 = mybir.dt.int32
AF = mybir.ActivationFunctionType
ALU = mybir.AluOpType
AX = mybir.AxisListType

NCORES = 8
DM = 4096
CI = DM // 128
DH = 1024
DC = DH // 128
FQKV = 3072
EPS = 1e-5
ISCALE = 1.0 / 32.0
EMARGIN = 3.0
CTEMP = 10.0
CTHR = 0.01

STORE_KEYS = ["stxlt0", "stxlt1", "stq0", "stq1", "stkv0", "stkv1",
              "stxm0", "stxm1", "stx2_0", "stx2_1", "stcnt"]

SEMS = ["v", "s", "t", "g",
        "ldx0", "ldx1", "ldx8_0", "ldx8_1", "ldw0", "ldw1", "ldg",
        "ldk0", "ldk1", "ldv0", "ldv1", "ldq", "ldr0", "ldr1",
        "ldd0", "ldd1", "ldd2", "ldm0", "ldm1", "ldm2", "ldwr0", "ldwr1", "ldwr2", "ldcnt", "ldd",
        "stxlt0", "stxlt1", "stq0", "stq1", "stkv0", "stkv1",
        "stxm0", "stxm1", "stx2_0", "stx2_1", "stout0", "stout1", "stout2", "stcnt",
        "cc1", "cc2"]


class Prog:
    def __init__(self):
        self.emit = False
        self.marks = {}
        self.cnt = {k: 0 for k in SEMS}
        self.sem = {}
        self.nc = None
        self.stack = None

    def reset_counts(self):
        self.cnt = {k: 0 for k in SEMS}


class Eng:
    def __init__(self, P, name, prog_key=None):
        self.P = P
        self.name = name
        self.key = prog_key

    def _eng(self):
        return getattr(self.P.nc, self.name)

    def op(self, f, inc=1):
        P = self.P
        if P.emit:
            if self.key:
                self._eng().wait_ge(P.sem[self.key], P.cnt[self.key])
            instr = f()
            if self.key and inc:
                instr.then_inc(P.sem[self.key], inc)
        if self.key and inc:
            P.cnt[self.key] += inc

    def mm(self, f, inc=0):
        P = self.P
        if P.emit:
            instr = f()
            if inc:
                instr.then_inc(P.sem["t"], inc)
        if inc:
            P.cnt["t"] += inc

    def dma(self, f, sem_key):
        P = self.P
        if P.emit:
            if self.key:
                self._eng().wait_ge(P.sem[self.key], P.cnt[self.key])
            f().then_inc(P.sem[sem_key], 16)
        P.cnt[sem_key] += 16
        return (sem_key, P.cnt[sem_key])

    def wait(self, mark_name):
        P = self.P
        if P.emit:
            sem_key, val = P.marks[mark_name]
            self._eng().wait_ge(P.sem[sem_key], val)

    def wait_sem(self, sem_key, val):
        if val <= 0:
            return
        if self.P.emit:
            self._eng().wait_ge(self.P.sem[sem_key], val)

    def wait_tok(self, tok):
        if tok is not None:
            self.wait_sem(*tok)

    def mark(self, name, sem_key=None):
        P = self.P
        key = sem_key or self.key
        cur = (key, P.cnt[key])
        if not P.emit:
            P.marks[name] = cur
        else:
            assert P.marks.get(name) == cur, (
                f"mark mismatch {name}: dry {P.marks.get(name)} vs emit {cur}")


def run_block(P, bodies, name=None):
    if not P.emit:
        for ename, key, fn in bodies:
            fn(Eng(P, ename, key))
    else:
        with P.nc.Block(name, no_gpsimd_drain=True) as block:
            for ename, key, fn in bodies:
                deco = getattr(block, ename)

                def body(_eng, fn=fn, ename=ename, key=key):
                    fn(Eng(P, ename, key))
                deco(body)



def build(T=2048, dbg=()):
    NT = T // 128
    NSEG = NT // 2
    KB = T // 2
    NKB = 2 * NCORES
    KSB = max(1, KB // 512)
    NSC = min(512, KB)
    KC = KB // 128
    QG = min(4, NT)
    NG = NT // QG

    P = Prog()

    def build_all():
        nc = P.nc

        def dram(name, shape, dt, kind=None):
            if not P.emit:
                return None
            if kind:
                return nc.dram_tensor(name, shape, dt, kind=kind)
            return nc.dram_tensor(name, shape, dt)

        def sbuf(name, shape, dt):
            if not P.emit:
                return None
            return P.stack.enter_context(nc.sbuf_tensor(name, shape, dt))

        x_s = dram("x_s", [T, DM], F32, "ExternalInput")
        wqkv = dram("wqkv", [DM, FQKV], F32, "ExternalInput")
        bqkv = dram("bqkv", [1, FQKV], F32, "ExternalInput")
        wo = dram("wo", [DH, DM], F32, "ExternalInput")
        bo = dram("bo", [1, DM], F32, "ExternalInput")
        wffn = dram("wffn", [DM, DM], F32, "ExternalInput")
        bffn = dram("bffn", [1, DM], F32, "ExternalInput")
        cent = dram("cent", [64, DM], F32, "ExternalInput")
        asg = dram("asg", [1, DM], I32, "ExternalInput")
        ident = dram("ident", [128, 128], F32, "ExternalInput")
        onesr = dram("onesr", [1, 512], F32, "ExternalInput")
        onesc = dram("onesc", [128, 1], F32, "ExternalInput")
        iota64 = dram("iota64", [64, 1], F32, "ExternalInput")
        out_s = dram("out_s", [T, DM], F32, "ExternalOutput")

        XLT = dram("XLT", [NT, 128, DM], BF16)
        XLT2 = dram("XLT2", [NT, 128, DM], BF16)
        XMID = dram("XMID", [T, DM], F32)
        KVIN = dram("KVIN", [2048, T], BF16)
        KGk = (nc.dram_tensor("KGk", [NCORES * 1024, T], BF16,
                              addr_space="Shared") if P.emit else None)
        KGv = (nc.dram_tensor("KGv", [NCORES * 1024, T], BF16,
                              addr_space="Shared") if P.emit else None)
        QTD = dram("QTD", [DC, 128, T], BF16)
        CNTI = dram("CNTI", [1, 64], F32)
        CNTO = dram("CNTO", [1, 64], F32)

        identB = sbuf("identB", [128, 128], BF16)
        onesrB = sbuf("onesrB", [1, 512], BF16)
        onescB = sbuf("onescB", [128, 1], BF16)
        iotaC = sbuf("iotaC", [64, 1], F32)
        bqkvB = sbuf("bqkvB", [1, FQKV], BF16)
        boB = sbuf("boB", [1, DM], BF16)
        bffnB = sbuf("bffnB", [1, DM], BF16)
        centT = sbuf("centT", [128, CI * 64], BF16)
        qsel = sbuf("qsel", [128, NT], F32)

        NCONST = 8

        phase_a(P, nc, locals())
        phase_bc(P, nc, locals())
        phase_d(P, nc, locals())
        if dbg:
            dbg_block(P, nc, locals(), dbg)

    def phase_a(P, nc, env):
        g = env
        identB, onesrB, onescB = g["identB"], g["onesrB"], g["onescB"]
        bqkvB, centT = g["bqkvB"], g["centT"]

        with ExitStack() as sta0:
            wreg = ([sta0.enter_context(nc.sbuf_tensor(f"wreg{i}", [128, CI, 512], BF16))
                     for i in range(2)] if P.emit else [None, None])
            phase_a_inner(P, nc, env, wreg)

    def phase_a_inner(P, nc, env, wreg):
        g = env
        identB, onesrB, onescB = g["identB"], g["onesrB"], g["onescB"]
        bqkvB, centT = g["bqkvB"], g["centT"]
        hpseq0 = [2, 3, 4, 5, 0, 1]

        with ExitStack() as sta:
            def asb(name, shape, dt):
                return (sta.enter_context(nc.sbuf_tensor(name, shape, dt))
                        if P.emit else None)

            def aps(name, shape):
                return (sta.enter_context(nc.psum_tensor(name, shape, F32))
                        if P.emit else None)

            xbuf = [asb(f"xbuf{i}", [128, DM], F32) for i in range(2)]
            stats = asb("stats", [128, 48], F32)
            mv = asb("mv", [128, 2], F32)
            lnv = asb("lnv", [128, 1], F32)
            rstd = asb("rstd", [128, 1], F32)
            xlnb = [asb(f"xlnb{i}", [128, DM], BF16) for i in range(2)]
            xltsb = [asb(f"xltsb{i}", [128, DM], BF16) for i in range(2)]
            centS = asb("centS", [64, DM], BF16)
            trps = [aps(f"trps{i}", [128, 1024]) for i in range(2)]

            def a_gpsimd(e):
                e.dma(lambda: nc.gpsimd.dma_start(out=identB[:, :], in_=g["ident"][:, :]), "ldg")
                e.dma(lambda: nc.gpsimd.dma_start(out=onesrB[:, :], in_=g["onesr"][:, :]), "ldg")
                e.dma(lambda: nc.gpsimd.dma_start(out=onescB[:, :], in_=g["onesc"][:, :]), "ldg")
                e.dma(lambda: nc.gpsimd.dma_start(out=g["iotaC"][:, :], in_=g["iota64"][:, :]), "ldg")
                e.dma(lambda: nc.gpsimd.dma_start(out=bqkvB[:, :], in_=g["bqkv"][:, :]), "ldg")
                e.dma(lambda: nc.gpsimd.dma_start(out=g["boB"][:, :], in_=g["bo"][:, :]), "ldg")
                e.dma(lambda: nc.gpsimd.dma_start(out=g["bffnB"][:, :], in_=g["bffn"][:, :]), "ldg")
                e.dma(lambda: nc.gpsimd.dma_start(out=centS[:, :], in_=g["cent"][:, :]), "ldg")
                for pos in range(2):
                    off = hpseq0[pos] * 512
                    e.dma(lambda off=off, pos=pos: nc.gpsimd.dma_start(
                        out=wreg[pos % 2][:, :, :],
                        in_=g["wqkv"][:, off:off + 512].rearrange(
                            "(ci p) f -> p ci f", p=128)), f"ldw{pos % 2}")

            def a_sync(e):
                for ti in range(NT):
                    if ti >= 2:
                        e.wait(f"a_xnorm{ti - 2}")
                    e.dma(lambda ti=ti: nc.sync.dma_start(
                        out=xbuf[ti % 2][:, :],
                        in_=g["x_s"][ti * 128:(ti + 1) * 128, :]), f"ldx{ti % 2}")

            def a_vector(e):
                for ti in range(NT):
                    e.wait_sem(f"ldx{ti % 2}", 16 * (ti // 2 + 1))
                    if ti >= 1:
                        e.wait(f"a_ln{ti - 1}")
                    for j in range(8):
                        e.op(lambda j=j, ti=ti: nc.vector.bn_stats(
                            stats[:, 6 * j:6 * j + 6],
                            xbuf[ti % 2][:, 512 * j:512 * (j + 1)]))
                    e.op(lambda: nc.vector.bn_aggr(mv[:, :], stats[:, :]))
                    e.mark(f"a_aggr{ti}")
                    e.op(lambda: nc.vector.tensor_scalar(
                        out=mv[:, 1:2], in0=mv[:, 1:2],
                        scalar1=EPS, scalar2=None, op0=ALU.add))
                    e.mark(f"a_veps{ti}")
                    e.op(lambda ti=ti: nc.vector.tensor_scalar(
                        out=xbuf[ti % 2][:, :], in0=xbuf[ti % 2][:, :],
                        scalar1=mv[:, 0:1], scalar2=None, op0=ALU.subtract))
                    e.mark(f"a_sub{ti}")

            def a_scalar(e):
                for ti in range(NT):
                    e.wait(f"a_veps{ti}")
                    e.op(lambda: nc.scalar.activation(
                        out=lnv[:, :], in_=mv[:, 1:2], func=AF.Ln, bias=0.0))
                    e.mark(f"a_ln{ti}")
                    e.op(lambda: nc.scalar.activation(
                        out=rstd[:, :], in_=lnv[:, :], func=AF.Exp, scale=-0.5))
                    e.wait(f"a_sub{ti}")
                    if ti >= 2:
                        e.wait(f"a_tr{ti - 2}_3")
                    e.op(lambda ti=ti: nc.scalar.activation(
                        out=xlnb[ti % 2][:, :], in_=xbuf[ti % 2][:, :],
                        func=AF.Copy, scale=rstd[:, :]))
                    e.mark(f"a_xnorm{ti}")
                    if ti >= 2:
                        e.wait_sem(f"stxlt{ti % 2}", 16 * (ti // 2))
                    for grp in range(4):
                        e.wait(f"a_tr{ti}_{grp}")
                        e.op(lambda ti=ti, grp=grp: nc.scalar.copy(
                            xltsb[ti % 2][:, grp * 1024:(grp + 1) * 1024],
                            trps[grp % 2][:, :]))
                        e.mark(f"a_trev{ti * 4 + grp}")
                    e.dma(lambda ti=ti: nc.scalar.dma_start(
                        out=g["XLT"][ti, :, :], in_=xltsb[ti % 2][:, :]), f"stxlt{ti % 2}")
                for cg in range(2):
                    e.wait(f"a_ctr{cg}")
                    e.op(lambda cg=cg: nc.scalar.copy(
                        centT[:, cg * 1024:(cg + 1) * 1024], trps[cg % 2][:, :]))
                    e.mark(f"a_ctrev{cg}")

            def a_tensor(e):
                e.wait_sem("ldg", 16 * 8)
                for ti in range(NT):
                    e.wait(f"a_xnorm{ti}")
                    for grp in range(4):
                        G = ti * 4 + grp
                        if G >= 2:
                            e.wait(f"a_trev{G - 2}")
                        for j in range(8):
                            ci = grp * 8 + j
                            e.mm(lambda ti=ti, grp=grp, ci=ci, j=j:
                                 nc.tensor.matmul(
                                     trps[grp % 2][:, j * 128:(j + 1) * 128],
                                     xlnb[ti % 2][:, ci * 128:(ci + 1) * 128],
                                     identB[:, :], start=True, stop=True),
                                 inc=1 if j == 7 else 0)
                        e.mark(f"a_tr{ti}_{grp}", "t")
                for cg in range(2):
                    e.wait(f"a_trev{4 * NT - 2 + cg}")
                    for j in range(16):
                        ci = cg * 16 + j
                        e.mm(lambda ci=ci, cg=cg, j=j: nc.tensor.matmul(
                            trps[cg % 2][:, j * 64:(j + 1) * 64],
                            centS[0:64, ci * 128:(ci + 1) * 128],
                            identB[0:64, 0:64], start=True, stop=True),
                            inc=1 if j == 15 else 0)
                    e.mark(f"a_ctr{cg}", "t")

            run_block(P, [
                ("gpsimd", "g", a_gpsimd),
                ("sync", None, a_sync),
                ("vector", "v", a_vector),
                ("scalar", "s", a_scalar),
                ("tensor", None, a_tensor),
            ], "phaseA0")

        QS = min(512, T)
        NQ = T // QS
        TPQ = QS // 128

        with ExitStack() as sta1:
            def asb1(name, shape, dt):
                return (sta1.enter_context(nc.sbuf_tensor(name, shape, dt))
                        if P.emit else None)

            def aps1(name, shape):
                return (sta1.enter_context(nc.psum_tensor(name, shape, F32))
                        if P.emit else None)

            xlt4 = [asb1(f"xlt4_{i}", [128, TPQ, DM], BF16) for i in range(2)]
            qst = [asb1(f"qst{i}", [128, 512], BF16) for i in range(2)]
            qkps = [aps1(f"qkps{i}", [128, 512]) for i in range(2)]

            vview = (g["KVIN"][1024:2048, :].flatten()
                     .rearrange("(t d) -> t d", d=1024) if P.emit else None)

            hpseq = [2, 3, 4, 5, 0, 1]
            items = []
            for pos, hp in enumerate(hpseq):
                for q4 in range(NQ):
                    for k in range(4 if hp < 4 else TPQ):
                        items.append((pos, hp, q4, k))
            n_kv = sum(1 for (pos, hp, q4, k) in items if hp >= 2)
            n_k = sum(1 for (pos, hp, q4, k) in items if hp in (2, 3))

            fvA = {k: P.cnt[k] for k in ["stxlt0", "stxlt1"]}

            def a1_gpsimd(e):
                for k, v in fvA.items():
                    e.wait_sem(k, v)
                for pos, hp in enumerate(hpseq):
                    if pos < 2:
                        continue
                    off = hp * 512
                    e.wait(f"a_wfree{pos - 2}")
                    e.dma(lambda off=off, pos=pos: nc.gpsimd.dma_start(
                        out=wreg[pos % 2][:, :, :],
                        in_=g["wqkv"][:, off:off + 512].rearrange(
                            "(ci p) f -> p ci f", p=128)), f"ldw{pos % 2}")
                e.wait("a_kdone0")
                e.wait("a_kdone1")
                if P.emit:
                    nc.gpsimd.collective_compute(
                        "AllGather", ALU.bypass,
                        replica_groups=[list(range(NCORES))],
                        ins=[g["KVIN"][0:1024, :].opt()],
                        outs=[g["KGk"].ap().opt()],
                    ).then_inc(P.sem["cc1"], 1)
                P.cnt["cc1"] += 1
                e.wait("a_vdone0")
                e.wait("a_vdone1")
                if P.emit:
                    nc.gpsimd.collective_compute(
                        "AllGather", ALU.bypass,
                        replica_groups=[list(range(NCORES))],
                        ins=[g["KVIN"][1024:2048, :].opt()],
                        outs=[g["KGv"].ap().opt()],
                    ).then_inc(P.sem["cc1"], 1)
                P.cnt["cc1"] += 1

            def a1_sync(e):
                for k, v in fvA.items():
                    e.wait_sem(k, v)
                for pos in range(6):
                    for q4 in range(NQ):
                        idx = pos * NQ + q4
                        if idx >= 2:
                            e.wait(f"a_x4free{idx - 2}")
                        e.dma(lambda q4=q4, idx=idx: nc.sync.dma_start(
                            out=xlt4[idx % 2][:, :, :],
                            in_=g["XLT"][q4 * TPQ:(q4 + 1) * TPQ, :, :]
                            .rearrange("ti p e -> p ti e")), f"ldx8_{idx % 2}")

            def a1_scalar(e):
                stg_tok = {0: None, 1: None}
                for k2, v2 in fvA.items():
                    e.wait_sem(k2, v2)
                for u, (pos, hp, q4, k) in enumerate(items):
                    pas, half = hp // 2, hp % 2
                    w = QS if pas < 2 else 512
                    e.wait(f"a_mm{u}")
                    e.wait_tok(stg_tok[u % 2])
                    e.op(lambda u=u, w=w: nc.scalar.copy(
                        qst[u % 2][:, 0:w], qkps[u % 2][:, 0:w]))
                    e.mark(f"a_ev{u}")
                    if pas == 0:
                        fi = half * 4 + k
                        tok = e.dma(lambda fi=fi, q4=q4, u=u:
                                    nc.scalar.dma_start(
                                        out=g["QTD"][fi, :, q4 * QS:(q4 + 1) * QS],
                                        in_=qst[u % 2][:, 0:QS]), f"stq{u % 2}")
                    elif pas == 1:
                        fi = half * 4 + k
                        tok = e.dma(lambda fi=fi, q4=q4, u=u:
                                    nc.scalar.dma_start(
                                        out=g["KVIN"][fi * 128:(fi + 1) * 128,
                                                      q4 * QS:(q4 + 1) * QS],
                                        in_=qst[u % 2][:, 0:QS]), f"stkv{u % 2}")
                    else:
                        ti = q4 * TPQ + k
                        tok = e.dma(lambda ti=ti, half=half, u=u:
                                    nc.scalar.dma_start(
                                        out=vview[ti * 128:(ti + 1) * 128,
                                                  half * 512:(half + 1) * 512],
                                        in_=qst[u % 2][:, :]), f"stkv{u % 2}")
                    stg_tok[u % 2] = tok
                    if u == n_k - 1:
                        e.mark("a_kdone0", "stkv0")
                        e.mark("a_kdone1", "stkv1")
                    if u == n_kv - 1:
                        e.mark("a_vdone0", "stkv0")
                        e.mark("a_vdone1", "stkv1")

            def a1_tensor(e):
                for u, (pos, hp, q4, k) in enumerate(items):
                    pas, half = hp // 2, hp % 2
                    if k == 0 and q4 == 0:
                        e.wait_sem(f"ldw{pos % 2}", 16 * (pos // 2 + 1))
                    idx = pos * NQ + q4
                    if k == 0:
                        e.wait_sem(f"ldx8_{idx % 2}", 16 * (idx // 2 + 1))
                    if u >= 2:
                        e.wait(f"a_ev{u - 2}")
                    if pas < 2:
                        for ci in range(CI):
                            e.mm(lambda k=k, ci=ci, u=u, pos=pos, idx=idx:
                                 nc.tensor.matmul(
                                     qkps[u % 2][:, 0:QS],
                                     wreg[pos % 2][:, ci, k * 128:(k + 1) * 128],
                                     xlt4[idx % 2][:, :, ci * 128:(ci + 1) * 128],
                                     start=(ci == 0), stop=False))
                        boff = pas * 1024 + half * 512 + k * 128
                        e.mm(lambda boff=boff, u=u: nc.tensor.matmul(
                            qkps[u % 2][:, 0:QS],
                            bqkvB[0:1, boff:boff + 128],
                            onesrB[0:1, 0:QS],
                            start=False, stop=True), inc=1)
                    else:
                        for ci in range(CI):
                            e.mm(lambda k=k, ci=ci, u=u, pos=pos, idx=idx:
                                 nc.tensor.matmul(
                                     qkps[u % 2][:, 0:512],
                                     xlt4[idx % 2][:, k, ci * 128:(ci + 1) * 128],
                                     wreg[pos % 2][:, ci, :],
                                     start=(ci == 0), stop=False))
                        boff = 2048 + half * 512
                        e.mm(lambda boff=boff, u=u: nc.tensor.matmul(
                            qkps[u % 2][:, 0:512],
                            onesrB[0:1, 0:128],
                            bqkvB[0:1, boff:boff + 512],
                            start=False, stop=True), inc=1)
                    e.mark(f"a_mm{u}", "t")
                    if k == (4 if pas < 2 else TPQ) - 1:
                        e.mark(f"a_x4free{idx}", "t")
                        if q4 == NQ - 1:
                            e.mark(f"a_wfree{pos}", "t")

            run_block(P, [
                ("gpsimd", "g", a1_gpsimd),
                ("sync", None, a1_sync),
                ("scalar", "s", a1_scalar),
                ("tensor", None, a1_tensor),
            ], "phaseA1")
    def phase_bc(P, nc, env):
        g = env
        identB, onesrB, onescB = g["identB"], g["onesrB"], g["onescB"]
        boB, centT, qsel = g["boB"], g["centT"], g["qsel"]

        with ExitStack() as stb:
            def bsb(name, shape, dt):
                return (stb.enter_context(nc.sbuf_tensor(name, shape, dt))
                        if P.emit else None)

            def bps(name, shape):
                return (stb.enter_context(nc.psum_tensor(name, shape, F32))
                        if P.emit else None)

            WoB = bsb("WoB", [128, DC, DM], BF16)
            QTg = bsb("QTg", [128, DC, QG * 128], BF16)
            oacc = bsb("oacc", [128, QG * 1025], F32)
            colC = bsb("colC", [128, 1], F32)
            auxb = bsb("auxb", [128, 2], BF16)
            pmax = bsb("pmax", [128, 1], F32)
            m1 = bsb("m1", [1, 1], F32)
            cntsb = bsb("cntsb", [1, 64], F32)
            ps_cnt = bps("ps_cnt", [1, 64])

            sh = {}
            for grp in range(NG):
                flash_block(P, nc, env, grp, dict(
                    WoB=WoB, QTg=QTg, oacc=oacc, colC=colC, auxb=auxb,
                    pmax=pmax, m1=m1, sh=sh))
                c_block(P, nc, env, grp, dict(
                    WoB=WoB, oacc=oacc, cntsb=cntsb,
                    ps_cnt=ps_cnt, sh=sh))

    def flash_block(P, nc, env, grp, h):
        g = env
        identB, onesrB = g["identB"], g["onesrB"]
        QTg, oacc = h["QTg"], h["oacc"]
        colC, auxb = h["colC"], h["auxb"]
        NQC = QG * 128

        with ExitStack() as stf:
            def fsb(name, shape, dt):
                return (stf.enter_context(nc.sbuf_tensor(f"{name}_g{grp}", shape, dt))
                        if P.emit else None)

            def fps(name, shape):
                return (stf.enter_context(nc.psum_tensor(f"{name}_g{grp}", shape, F32))
                        if P.emit else None)

            Kb = [fsb(f"Kb{i}", [128, DC, KB], BF16) for i in range(2)]
            Vb = [fsb(f"Vb{i}", [128, KC, 1032], BF16) for i in range(2)]
            PT = [fsb(f"PT{i}", [128, KC, NQC], BF16) for i in range(2)]
            pvs = [fsb(f"pvs{i}", [128, 1025], F32) for i in range(2)]
            ps_sT = [fps(f"ps_sT{i}", [128, NQC]) for i in range(2)]
            ps_pv = [fps(f"ps_pv{i}", [128, DH]) for i in range(2)]
            ps_l = fps("ps_l", [128, 512])

            fv = {k: P.cnt[k] for k in STORE_KEYS}

            def fence(e):
                for k, v in fv.items():
                    e.wait_sem(k, v)

            def b_gpsimd(e):
                fence(e)
                if grp == 0:
                    h["sh"]["wob_tok"] = e.dma(lambda: nc.gpsimd.dma_start(
                        out=h["WoB"][:, :, :],
                        in_=g["wo"].rearrange("(dc p) c -> p dc c", p=128)),
                        "ldw0")

            def b_sync(e):
                fence(e)
                if grp > 0:
                    e.wait(f"b_qtfree{grp - 1}")
                e.dma(lambda: nc.sync.dma_start(
                    out=QTg[:, :, :],
                    in_=g["QTD"][:, :, grp * NQC:(grp + 1) * NQC]
                    .rearrange("dc p t -> p dc t")), "ldq")
                if grp == 0:
                    e.wait_sem("cc1", 1)
                for kb in range(NKB):
                    r, hf = divmod(kb, 2)
                    if kb >= 2:
                        e.wait(f"b_kfree{grp}_{kb - 2}")
                    e.dma(lambda r=r, hf=hf, kb=kb: nc.sync.dma_start(
                        out=Kb[kb % 2][:, :, :],
                        in_=g["KGk"][r * 1024:(r + 1) * 1024,
                                     hf * KB:(hf + 1) * KB]
                        .rearrange("(dc p) j -> p dc j", p=128)),
                        f"ldk{kb % 2}")
                    if grp == 0 and kb == 0:
                        e.wait_sem("cc1", 2)
                    if kb >= 2:
                        e.wait(f"b_vfree{grp}_{kb - 2}")
                    e.dma(lambda r=r, hf=hf, kb=kb: nc.sync.dma_start(
                        out=Vb[kb % 2][:, :, 0:DH],
                        in_=g["KGv"][r * 1024:(r + 1) * 1024, :]
                        .flatten().rearrange("(t d) -> t d", d=DH)
                        [hf * KB:(hf + 1) * KB, :]
                        .rearrange("(kc p) d -> p kc d", p=128)),
                        f"ldv{kb % 2}")

            def b_tensor(e):
                def S(kb, kc):
                    si = kb * KC + kc
                    if kc == 0:
                        e.wait_sem(f"ldk{kb % 2}", 16 * (kb // 2 + 1 + grp * NKB // 2))
                        if kb == 0:
                            e.wait_sem("ldq", 16 * (grp + 1))
                    if si >= 2:
                        p = si - 2
                        e.wait(f"b_exp{grp}_{p // KC}_{p % KC}")
                    for dc in range(DC):
                        e.mm(lambda kc=kc, dc=dc, kb=kb, si=si: nc.tensor.matmul(
                            ps_sT[si % 2][:, :],
                            Kb[kb % 2][:, dc, kc * 128:(kc + 1) * 128],
                            QTg[:, dc, :],
                            start=(dc == 0), stop=(dc == DC - 1)),
                            inc=1 if dc == DC - 1 else 0)
                    e.mark(f"b_sT{grp}_{kb}_{kc}", "t")
                    if kc == KC - 1:
                        e.mark(f"b_kfree{grp}_{kb}", "t")
                        if kb == NKB - 1:
                            e.mark(f"b_qtfree{grp}", "t")

                def PV(kb, qt):
                    if qt == 0:
                        e.wait(f"b_exp{grp}_{kb}_{KC - 1}")
                        e.wait_sem(f"ldv{kb % 2}", 16 * (kb // 2 + 1 + grp * NKB // 2))
                        e.wait(f"b_vset{grp}_{kb % 2}")
                    a = kb * QG + qt
                    if a >= 2:
                        pa = a - 2
                        e.wait(f"b_pvev{grp}_{pa // QG}_{pa % QG}")
                    for kc in range(KC):
                        for sb in range(2):
                            e.mm(lambda kc=kc, kb=kb, qt=qt, sb=sb, a=a:
                                 nc.tensor.matmul(
                                     ps_pv[a % 2][:, sb * 512:(sb + 1) * 512],
                                     PT[kb % 2][:, kc, qt * 128:(qt + 1) * 128],
                                     Vb[kb % 2][:, kc, sb * 512:(sb + 1) * 512],
                                     start=(kc == 0), stop=(kc == KC - 1)))
                    for kc in range(KC):
                        e.mm(lambda kc=kc, kb=kb, qt=qt, a=a:
                             nc.tensor.matmul(
                                 ps_l[:, (a % 2):(a % 2) + 1],
                                 PT[kb % 2][:, kc, qt * 128:(qt + 1) * 128],
                                 Vb[kb % 2][:, kc, 1024:1025],
                                 start=(kc == 0), stop=(kc == KC - 1),
                                 skip_group_check=True),
                             inc=1 if kc == KC - 1 else 0)
                    e.mark(f"b_PV{grp}_{kb}_{qt}", "t")
                    if qt == QG - 1:
                        e.mark(f"b_vfree{grp}_{kb}", "t")

                def aux1():
                    e.wait(f"b_pmaxb{grp}")
                    e.mm(lambda: nc.tensor.matmul(
                        ps_l[0:1, 0:128], auxb[:, 0:1], identB[:, :],
                        start=True, stop=True, skip_group_check=True), inc=1)
                    e.mark(f"b_aux1{grp}", "t")

                def aux2():
                    e.wait(f"b_m1b{grp}")
                    e.mm(lambda: nc.tensor.matmul(
                        ps_l[:, 0:1], onesrB[0:1, 0:128], auxb[0:1, 1:2],
                        start=True, stop=True, skip_group_check=True), inc=1)
                    e.mark(f"b_aux2{grp}", "t")

                S(0, 0)
                if KC > 1:
                    S(0, 1)
                aux1()
                aux2()
                for kc in range(min(2, KC), KC):
                    S(0, kc)
                for kb in range(1, NKB):
                    for kc in range(KC):
                        S(kb, kc)
                    for qt in range(QG):
                        PV(kb - 1, qt)
                for qt in range(QG):
                    PV(NKB - 1, qt)

            def b_scalar(e):
                fence(e)
                for kb in range(NKB):
                    for kc in range(KC):
                        e.wait(f"b_sT{grp}_{kb}_{kc}")
                        if kb == 0 and kc == 0:
                            e.wait(f"b_colC{grp}")
                        si = kb * KC + kc
                        if si >= 2 * KC:
                            e.wait(f"b_PV{grp}_{kb - 2}_{QG - 1}")
                        e.op(lambda kb=kb, kc=kc, si=si: nc.scalar.activation(
                            out=PT[kb % 2][:, kc, :], in_=ps_sT[si % 2][:, :],
                            func=AF.Exp, scale=ISCALE, bias=colC[:, :]))
                        e.mark(f"b_exp{grp}_{kb}_{kc}")
                    if kb >= 1:
                        for qt in range(QG):
                            e.wait(f"b_PV{grp}_{kb - 1}_{qt}")
                            a = (kb - 1) * QG + qt
                            if a >= 2:
                                pa = a - 2
                                e.wait(f"b_acc{grp}_{pa // QG}_{pa % QG}")
                            e.op(lambda qt=qt, a=a: nc.scalar.copy(
                                pvs[qt % 2][:, 0:DH], ps_pv[a % 2][:, :]))
                            e.op(lambda qt=qt, a=a: nc.scalar.copy(
                                pvs[qt % 2][:, DH:DH + 1],
                                ps_l[:, (a % 2):(a % 2) + 1]))
                            e.mark(f"b_pvev{grp}_{kb - 1}_{qt}")
                for qt in range(QG):
                    e.wait(f"b_PV{grp}_{NKB - 1}_{qt}")
                    a = (NKB - 1) * QG + qt
                    pa = a - 2
                    e.wait(f"b_acc{grp}_{pa // QG}_{pa % QG}")
                    e.op(lambda qt=qt, a=a: nc.scalar.copy(
                        pvs[qt % 2][:, 0:DH], ps_pv[a % 2][:, :]))
                    e.op(lambda qt=qt, a=a: nc.scalar.copy(
                        pvs[qt % 2][:, DH:DH + 1],
                        ps_l[:, (a % 2):(a % 2) + 1]))
                    e.mark(f"b_pvev{grp}_{NKB - 1}_{qt}")

            def b_vector(e):
                e.wait(f"b_sT{grp}_0_0")
                e.op(lambda: nc.vector.reduce_max(
                    h["pmax"][:, :], ps_sT[0][:, :], axis=AX.X))
                e.op(lambda: nc.vector.tensor_copy(auxb[:, 0:1], h["pmax"][:, :]))
                e.mark(f"b_pmaxb{grp}")
                e.wait(f"b_aux1{grp}")
                e.op(lambda: nc.vector.reduce_max(
                    h["m1"][:, :], ps_l[0:1, 0:128], axis=AX.X))
                e.op(lambda: nc.vector.tensor_copy(auxb[0:1, 1:2], h["m1"][:, :]))
                e.mark(f"b_m1b{grp}")
                e.wait(f"b_aux2{grp}")
                e.op(lambda: nc.vector.tensor_scalar(
                    out=colC[:, :], in0=ps_l[:, 0:1], scalar1=-ISCALE,
                    scalar2=-EMARGIN, op0=ALU.mult, op1=ALU.add))
                e.mark(f"b_colC{grp}")
                for kb in range(NKB):
                    for qt in range(QG):
                        e.wait(f"b_pvev{grp}_{kb}_{qt}")
                        if kb == 0:
                            e.op(lambda qt=qt: nc.vector.tensor_copy(
                                oacc[:, qt * 1025:(qt + 1) * 1025],
                                pvs[qt % 2][:, :]))
                        else:
                            e.op(lambda qt=qt: nc.vector.tensor_tensor(
                                out=oacc[:, qt * 1025:(qt + 1) * 1025],
                                in0=oacc[:, qt * 1025:(qt + 1) * 1025],
                                in1=pvs[qt % 2][:, :], op=ALU.add))
                        e.mark(f"b_acc{grp}_{kb}_{qt}")

            def b_memset(e):
                for i in range(2):
                    e.op(lambda i=i: nc.vector.memset(
                        Vb[i][:, :, 1024:1025], 1.0))
                    e.mark(f"b_vset{grp}_{i}", "v")

            def b_vector_all(e):
                fence(e)
                b_memset(e)
                b_vector(e)

            run_block(P, [
                ("gpsimd", "g", b_gpsimd),
                ("sync", None, b_sync),
                ("tensor", None, b_tensor),
                ("scalar", "s", b_scalar),
                ("vector", "v", b_vector_all),
            ], f"flash{grp}")

    def c_block(P, nc, env, grp, h):
        g = env
        identB, onesrB, onescB = g["identB"], g["onesrB"], g["onescB"]
        boB, centT, qsel = g["boB"], g["centT"], g["qsel"]
        WoB, oacc = h["WoB"], h["oacc"]
        ps_cnt, cntsb = h["ps_cnt"], h["cntsb"]

        with ExitStack() as stc:
            def csb(name, shape, dt):
                return (stc.enter_context(nc.sbuf_tensor(f"{name}_g{grp}", shape, dt))
                        if P.emit else None)

            def cps(name, shape):
                return (stc.enter_context(nc.psum_tensor(f"{name}_g{grp}", shape, F32))
                        if P.emit else None)

            xre = [csb(f"xre{i}", [128, DM], F32) for i in range(2)]
            pb512 = [csb(f"pb512_{i}", [128, 512], F32) for i in range(2)]
            oab = [csb(f"oab{i}", [128, DH], BF16) for i in range(2)]
            oat = [csb(f"oat{i}", [128, DH], BF16) for i in range(2)]
            linv = [csb(f"linv{i}", [128, 1], F32) for i in range(2)]
            xln2b = csb("xln2b", [128, DM], BF16)
            xmn = csb("xmn", [128, DM], F32)
            x2sb = csb("x2sb", [128, DM], BF16)
            cte = csb("cte", [128, 64], F32)
            hits = [csb(f"hits{i}", [128, 64], BF16) for i in range(QG)]
            ctsum = csb("ctsum", [128, 1], F32)
            thr = csb("thr", [128, 1], F32)
            hitc = csb("hitc", [128, 1], F32)
            ctm = csb("ctm", [128, 1], F32)
            ctb = csb("ctb", [128, 1], F32)
            st2 = csb("st2", [128, 48], F32)
            mv2 = csb("mv2", [128, 2], F32)
            ln2 = csb("ln2", [128, 1], F32)
            rstd2 = csb("rstd2", [128, 1], F32)
            ps_wo = [cps(f"ps_wo{i}", [128, 512]) for i in range(2)]
            ps_tr2 = [cps(f"ps_tr2_{i}", [128, 1024]) for i in range(2)]
            ps_ct = cps("ps_ct", [128, 64])

            qts = [grp * QG + j for j in range(QG)]
            fv = {k: P.cnt[k] for k in STORE_KEYS}

            def fence(e):
                for k, v in fv.items():
                    e.wait_sem(k, v)
            ldr_tok = {}
            stxm_tok = {}
            stx2_tok = {}
            rounds = []
            for qtg in range(QG):
                rounds.append(("oa", qtg, 0))
                if qtg >= 1:
                    for r in range(4):
                        rounds.append(("l2", qtg - 1, r))
            for r in range(4):
                rounds.append(("l2", QG - 1, r))
            RR = {key: i for i, key in enumerate(rounds)}

            def c_sync(e):
                fence(e)
                for qtg, qt in enumerate(qts):
                    if qt >= 2:
                        e.wait(f"c_sub2{qt - 2}")
                        if (qt - 2) in stxm_tok:
                            e.wait_tok(stxm_tok[qt - 2])
                    ldr_tok[qt] = e.dma(lambda qt=qt, qtg=qtg: nc.sync.dma_start(
                        out=xre[qtg % 2][:, :],
                        in_=g["x_s"][qt * 128:(qt + 1) * 128, :]),
                        f"ldr{qtg % 2}")
                    e.wait(f"c_xmidv{qt}")
                    stxm_tok[qt] = e.dma(lambda qt=qt, qtg=qtg: nc.sync.dma_start(
                        out=g["XMID"][qt * 128:(qt + 1) * 128, :],
                        in_=xre[qtg % 2][:, :]), f"stxm{qtg % 2}")
                if grp == NG - 1:
                    e.wait(f"c_cntev")
                    e.dma(lambda: nc.sync.dma_start(
                        out=g["CNTI"][:, :], in_=cntsb[:, :]), "stcnt")

            def c_gpsimd(e):
                if grp == NG - 1:
                    e.wait_sem("stcnt", 16)
                    if P.emit:
                        nc.gpsimd.collective_compute(
                            "AllReduce", ALU.add,
                            replica_groups=[list(range(NCORES))],
                            ins=[g["CNTI"].ap().opt()],
                            outs=[g["CNTO"].ap().opt()],
                        ).then_inc(P.sem["cc2"], 1)
                    P.cnt["cc2"] += 1

            def c_tensor(e):
                if grp == 0:
                    e.wait_tok(h["sh"]["wob_tok"])

                def oaT(qtg, qt):
                    rr = RR[("oa", qtg, 0)]
                    e.wait(f"c_oab{qt}")
                    if rr >= 2:
                        e.wait(f"c_trev{grp}_{rr - 2}")
                    for kc in range(DC):
                        e.mm(lambda kc=kc, rr=rr: nc.tensor.matmul(
                            ps_tr2[rr % 2][:, kc * 128:(kc + 1) * 128],
                            oab[qtg % 2][:, kc * 128:(kc + 1) * 128],
                            identB[:, :], start=True, stop=True),
                            inc=1 if kc == DC - 1 else 0)
                    e.mark(f"c_oat{qt}", "t")

                def Wo(qtg, qt):
                    e.wait(f"c_oatev{qt}")
                    for ob in range(8):
                        if ob >= 2 or qtg > 0:
                            pi = (qt, ob - 2) if ob >= 2 else (qt - 1, 6 + ob)
                            e.wait(f"c_woev{pi[0]}_{pi[1]}")
                        for dc in range(DC):
                            e.mm(lambda dc=dc, ob=ob, qtg=qtg: nc.tensor.matmul(
                                ps_wo[ob % 2][:, :],
                                oat[qtg % 2][:, dc * 128:(dc + 1) * 128],
                                WoB[:, dc, ob * 512:(ob + 1) * 512],
                                start=(dc == 0), stop=False))
                        e.mm(lambda ob=ob: nc.tensor.matmul(
                            ps_wo[ob % 2][:, :],
                            onesrB[0:1, 0:128],
                            boB[0:1, ob * 512:(ob + 1) * 512],
                            start=False, stop=True), inc=1)
                        e.mark(f"c_wo{qt}_{ob}", "t")

                def L2T(qtg, qt):
                    e.wait(f"c_x2n{qt}")
                    for r in range(4):
                        rr = RR[("l2", qtg, r)]
                        if rr >= 2:
                            e.wait(f"c_trev{grp}_{rr - 2}")
                        for j in range(8):
                            ci = r * 8 + j
                            e.mm(lambda ci=ci, j=j, rr=rr: nc.tensor.matmul(
                                ps_tr2[rr % 2][:, j * 128:(j + 1) * 128],
                                xln2b[:, ci * 128:(ci + 1) * 128],
                                identB[:, :], start=True, stop=True),
                                inc=1 if j == 7 else 0)
                        e.mark(f"c_l2t{qt}_{r}", "t")

                def cent(qtg, qt):
                    e.wait(f"c_x2ev{qt}_3")
                    if qtg > 0 or grp > 0:
                        e.wait(f"c_ctdone{qt - 1}")
                    for ci in range(CI):
                        e.mm(lambda ci=ci: nc.tensor.matmul(
                            ps_ct[:, :],
                            x2sb[:, ci * 128:(ci + 1) * 128],
                            centT[:, ci * 64:(ci + 1) * 64],
                            start=(ci == 0), stop=(ci == CI - 1)),
                            inc=1 if ci == CI - 1 else 0)
                    e.mark(f"c_ct{qt}", "t")

                for qtg, qt in enumerate(qts):
                    oaT(qtg, qt)
                    Wo(qtg, qt)
                    if qtg >= 1:
                        L2T(qtg - 1, qt - 1)
                        cent(qtg - 1, qt - 1)
                L2T(QG - 1, qts[-1])
                cent(QG - 1, qts[-1])
                for qtg, qt in enumerate(qts):
                    e.wait(f"c_hits{qt}")
                    e.mm(lambda qt=qt, qtg=qtg: nc.tensor.matmul(
                        ps_cnt[:, :], onescB[:, :], hits[qtg][:, :],
                        start=(qt == 0), stop=(qt == NT - 1),
                        skip_group_check=True), inc=1)
                    e.mark(f"c_cntmm{qt}", "t")

            def c_scalar(e):
                fence(e)

                def head(qtg, qt):
                    rr = RR[("oa", qtg, 0)]
                    e.wait(f"c_linv{qt}")
                    e.op(lambda qtg=qtg: nc.scalar.activation(
                        out=oab[qtg % 2][:, :],
                        in_=oacc[:, qtg * 1025:qtg * 1025 + DH],
                        func=AF.Copy, scale=linv[qtg % 2][:, :]))
                    e.mark(f"c_oab{qt}")
                    e.wait(f"c_oat{qt}")
                    e.op(lambda qtg=qtg, rr=rr: nc.scalar.copy(
                        oat[qtg % 2][:, :], ps_tr2[rr % 2][:, 0:DH]))
                    e.mark(f"c_oatev{qt}")
                    e.mark(f"c_trev{grp}_{rr}")

                def woevs(qtg, qt):
                    for ob in range(8):
                        e.wait(f"c_wo{qt}_{ob}")
                        if ob >= 2 or qtg > 0:
                            pi = (qt, ob - 2) if ob >= 2 else (qt - 1, 6 + ob)
                            e.wait(f"c_add{pi[0]}_{pi[1]}")
                        e.op(lambda ob=ob: nc.scalar.copy(
                            pb512[ob % 2][:, :], ps_wo[ob % 2][:, :]))
                        e.mark(f"c_woev{qt}_{ob}")

                def chain(qtg, qt):
                    e.wait(f"c_veps2{qt}")
                    e.op(lambda: nc.scalar.activation(
                        out=ln2[:, :], in_=mv2[:, 1:2], func=AF.Ln, bias=0.0))
                    e.mark(f"c_ln2m{qt}")
                    e.op(lambda: nc.scalar.activation(
                        out=rstd2[:, :], in_=ln2[:, :], func=AF.Exp, scale=-0.5))
                    e.wait(f"c_sub2{qt}")
                    if qtg > 0 or grp > 0:
                        e.wait(f"c_ct{qt - 1}")
                    e.op(lambda: nc.scalar.activation(
                        out=xln2b[:, :], in_=xmn[:, :],
                        func=AF.Copy, scale=rstd2[:, :]))
                    e.mark(f"c_x2n{qt}")
                    for r in range(4):
                        rr = RR[("l2", qtg, r)]
                        e.wait(f"c_l2t{qt}_{r}")
                        if r == 0 and qt >= 1:
                            e.wait(f"c_ct{qt - 1}")
                        if r == 0 and qt >= 1 and (qt - 1) in stx2_tok:
                            e.wait_tok(stx2_tok[qt - 1])
                        e.op(lambda r=r, rr=rr: nc.scalar.copy(
                            x2sb[:, r * 1024:(r + 1) * 1024],
                            ps_tr2[rr % 2][:, :]))
                        e.mark(f"c_x2ev{qt}_{r}")
                        e.mark(f"c_trev{grp}_{rr}")
                    stx2_tok[qt] = e.dma(lambda qt=qt: nc.scalar.dma_start(
                        out=g["XLT2"][qt, :, :], in_=x2sb[:, :]),
                        f"stx2_{qt % 2}")
                    e.wait(f"c_ct{qt}")
                    e.wait(f"c_ctb{qt}")
                    if qt >= 1:
                        e.wait(f"c_hits{qt - 1}")
                    e.op(lambda qtg=qtg: nc.scalar.activation(
                        out=cte[:, :], in_=ps_ct[:, :], func=AF.Exp,
                        scale=CTEMP, bias=ctb[:, :], accum_out=ctsum[:, :]))
                    e.mark(f"c_cte{qt}")

                for qtg, qt in enumerate(qts):
                    head(qtg, qt)
                    woevs(qtg, qt)
                    if qtg >= 1:
                        chain(qtg - 1, qt - 1)
                chain(QG - 1, qts[-1])

            def c_vector(e):
                fence(e)

                def vqt(qtg, qt):
                    if qtg >= 1:
                        e.wait(f"c_oab{qt - 1}")
                    e.op(lambda qtg=qtg: nc.vector.reciprocal(
                        linv[qtg % 2][:, :],
                        oacc[:, qtg * 1025 + DH:qtg * 1025 + DH + 1]))
                    e.mark(f"c_linv{qt}")
                    for ob in range(8):
                        e.wait(f"c_woev{qt}_{ob}")
                        if ob == 0:
                            e.wait_tok(ldr_tok[qt])
                        e.op(lambda ob=ob, qtg=qtg: nc.vector.tensor_tensor(
                            out=xre[qtg % 2][:, ob * 512:(ob + 1) * 512],
                            in0=xre[qtg % 2][:, ob * 512:(ob + 1) * 512],
                            in1=pb512[ob % 2][:, :], op=ALU.add))
                        e.mark(f"c_add{qt}_{ob}")
                        e.op(lambda ob=ob, qtg=qtg: nc.vector.bn_stats(
                            st2[:, 6 * ob:6 * ob + 6],
                            xre[qtg % 2][:, 512 * ob:512 * (ob + 1)]))
                    e.mark(f"c_xmidv{qt}")
                    if qt >= 1:
                        e.wait(f"c_ln2m{qt - 1}")
                    e.op(lambda: nc.vector.bn_aggr(mv2[:, :], st2[:, :]))
                    e.op(lambda: nc.vector.tensor_scalar(
                        out=mv2[:, 1:2], in0=mv2[:, 1:2], scalar1=EPS,
                        scalar2=None, op0=ALU.add))
                    e.mark(f"c_veps2{qt}")
                    if qt >= 1:
                        e.wait(f"c_x2n{qt - 1}")
                    e.op(lambda qtg=qtg: nc.vector.tensor_scalar(
                        out=xmn[:, :], in0=xre[qtg % 2][:, :],
                        scalar1=mv2[:, 0:1], scalar2=None, op0=ALU.subtract))
                    e.mark(f"c_sub2{qt}")

                def vtail(qtg, qt):
                    e.wait(f"c_ct{qt}")
                    e.op(lambda: nc.vector.reduce_max(
                        ctm[:, :], ps_ct[:, :], axis=AX.X))
                    e.op(lambda: nc.vector.tensor_scalar(
                        out=ctb[:, :], in0=ctm[:, :], scalar1=-CTEMP,
                        scalar2=None, op0=ALU.mult))
                    e.mark(f"c_ctb{qt}")
                    e.wait(f"c_cte{qt}")
                    e.op(lambda: nc.vector.tensor_scalar(
                        out=thr[:, :], in0=ctsum[:, :], scalar1=CTHR,
                        scalar2=None, op0=ALU.mult))
                    e.op(lambda qtg=qtg: nc.vector.tensor_scalar(
                        out=hits[qtg][:, :], in0=cte[:, :], scalar1=thr[:, :],
                        scalar2=None, op0=ALU.is_gt, op1=ALU.add,
                        accum_out=hitc[:, :]))
                    e.mark(f"c_hits{qt}")
                    e.op(lambda qt=qt: nc.vector.tensor_scalar(
                        out=qsel[:, qt:qt + 1], in0=hitc[:, :], scalar1=0.5,
                        scalar2=None, op0=ALU.is_gt))
                    e.mark(f"c_ctdone{qt}", "v")

                for qtg, qt in enumerate(qts):
                    vqt(qtg, qt)
                    if qtg >= 1:
                        vtail(qtg - 1, qt - 1)
                vtail(QG - 1, qts[-1])
                if grp == NG - 1:
                    e.wait(f"c_cntmm{NT - 1}")
                    e.op(lambda: nc.vector.tensor_copy(
                        cntsb[:, :], ps_cnt[:, :]))
                    e.mark("c_cntev")

            run_block(P, [
                ("sync", None, c_sync),
                ("gpsimd", "g", c_gpsimd),
                ("tensor", None, c_tensor),
                ("scalar", "s", c_scalar),
                ("vector", "v", c_vector),
            ], f"cblk{grp}")

    def phase_d(P, nc, env):
        g = env
        identB, onesrB = g["identB"], g["onesrB"]
        bffnB, qsel = g["bffnB"], g["qsel"]

        with ExitStack() as std:
            def dsb(name, shape, dt):
                return (std.enter_context(nc.sbuf_tensor(name, shape, dt))
                        if P.emit else None)

            def dps(name, shape):
                return (std.enter_context(nc.psum_tensor(name, shape, F32))
                        if P.emit else None)

            WT = dsb("WT", [128, CI, 1024], BF16)
            wrow = [dsb(f"wrow{i}", [128, DM], BF16) for i in range(3)]
            xlt2l = [dsb(f"xlt2l{i}", [128, DM], BF16) for i in range(3)]
            xmidb = [dsb(f"xmidb{i}", [128, 1024], F32) for i in range(3)]
            outb = [dsb(f"outb{i}", [128, 1024], F32) for i in range(3)]
            dq = [dsb(f"dq{i}", [128, 512], F32) for i in range(2)]
            asgi = dsb("asgi", [1, 2048], I32)
            rowbuf = dsb("rowbuf", [1, DM], BF16)
            asgb = rowbuf
            onehot = dsb("onehot", [64, DM], BF16)
            rselR = rowbuf
            rselB = dsb("rselB", [128, DM], BF16)
            cntg = dsb("cntg", [1, 64], F32)
            cntb = dsb("cntb", [1, 64], BF16)
            cselc = dsb("cselc", [64, 1], BF16)
            ps_oh = dps("ps_oh", [64, 1024])
            ps_d = [dps(f"ps_d{i}", [128, 512]) for i in range(2)]
            ps_wtr = [dps(f"ps_wtr{i}", [128, 1024]) for i in range(2)]

            ldwr_tok = {}
            ldd_tok = {}
            ldm_tok = {}
            stout_tok = {}
            fv = {k: P.cnt[k] for k in STORE_KEYS}

            def fence(e):
                for k, v in fv.items():
                    e.wait_sem(k, v)

            def d_gpsimd(e):
                fence(e)
                for hf in range(2):
                    if hf == 1:
                        e.wait(f"d_asgc0")
                    e.dma(lambda hf=hf: nc.gpsimd.dma_start(
                        out=asgi[:, :],
                        in_=g["asg"][:, hf * 2048:(hf + 1) * 2048]), "ldcnt")
                def wrows(oq):
                    for ob in range(8):
                        u = oq * 8 + ob
                        if u >= 3:
                            e.wait(f"d_wt{u - 3}")
                        ldwr_tok[u] = e.dma(
                            lambda oq=oq, ob=ob, u=u: nc.gpsimd.dma_start(
                                out=wrow[u % 3][:, :],
                                in_=g["wffn"][oq * 1024 + ob * 128:
                                              oq * 1024 + (ob + 1) * 128, :]),
                            f"ldwr{u % 3}")
                for oq in range(4):
                    wrows(oq)

            def d_sync(e):
                fence(e)
                e.wait_sem("cc2", 1)
                e.dma(lambda: nc.sync.dma_start(
                    out=cntg[:, :], in_=g["CNTO"][:, :]), "ldd")
                NU = 4 * NT

                def dloads(u):
                    oq, ti = divmod(u, NT)
                    if u >= 3:
                        e.wait(f"d_mm{u - 3}")
                    ldd_tok[u] = e.dma(lambda ti=ti, u=u: nc.sync.dma_start(
                        out=xlt2l[u % 3][:, :], in_=g["XLT2"][ti, :, :]),
                        f"ldd{u % 3}")
                    ldm_tok[u] = e.dma(lambda ti=ti, oq=oq, u=u: nc.sync.dma_start(
                        out=xmidb[u % 3][:, :],
                        in_=g["XMID"][ti * 128:(ti + 1) * 128,
                                      oq * 1024:(oq + 1) * 1024]),
                        f"ldm{u % 3}")
                for u in range(min(3, NU)):
                    dloads(u)
                for u in range(NU):
                    oq, ti = divmod(u, NT)
                    e.wait(f"d_out{u}")
                    stout_tok[u] = e.dma(
                        lambda ti=ti, oq=oq, u=u: nc.sync.dma_start(
                            out=g["out_s"][ti * 128:(ti + 1) * 128,
                                           oq * 1024:(oq + 1) * 1024],
                            in_=outb[u % 3][:, :]), f"stout{u % 3}")
                    if u + 3 < NU:
                        dloads(u + 3)
                e.wait_tok(stout_tok[NU - 2])
                e.wait_tok(stout_tok[NU - 1])

            def d_tensor(e):
                for qr in range(4):
                    e.wait(f"d_asgc{qr // 2}")
                    if qr >= 1:
                        e.wait(f"d_oh{qr - 1}")
                    for b in range(2):
                        e.mm(lambda qr=qr, b=b: nc.tensor.matmul(
                            ps_oh[:, b * 512:(b + 1) * 512],
                            onesrB[0:1, 0:64],
                            asgb[0:1, qr * 1024 + b * 512:
                                 qr * 1024 + (b + 1) * 512],
                            start=True, stop=True), inc=(1 if b == 1 else 0))
                    e.mark(f"d_ohb{qr}", "t")
                def wprep(oq):
                    for ob in range(8):
                        u = oq * 8 + ob
                        e.wait_tok(ldwr_tok[u])
                        if oq > 0 and ob == 0:
                            e.wait(f"d_mm{oq * NT - 1}")
                        for r in range(4):
                            ur = u * 4 + r
                            if ur >= 2:
                                e.wait(f"d_wtev{ur - 2}")
                            for j in range(8):
                                ci = r * 8 + j
                                e.mm(lambda u=u, ci=ci, j=j, ur=ur:
                                     nc.tensor.matmul(
                                         ps_wtr[ur % 2][:, j * 128:(j + 1) * 128],
                                         wrow[u % 3][:, ci * 128:(ci + 1) * 128],
                                         identB[:, :], start=True, stop=True),
                                     inc=1 if j == 7 else 0)
                            e.mark(f"d_wtr{ur}", "t")
                        e.mark(f"d_wt{u}", "t")
                wprep(0)
                e.wait(f"d_cntb")
                e.mm(lambda: nc.tensor.matmul(
                    ps_d[0][0:64, 0:1], cntb[0:1, 0:64], onesrB[0:1, 0:1],
                    start=True, stop=True), inc=1)
                e.mark("d_cc", "t")
                e.wait("d_csel")
                for b in range(8):
                    e.mm(lambda b=b: nc.tensor.matmul(
                        ps_d[0][0:1, 0:512], cselc[0:64, 0:1],
                        onehot[0:64, b * 512:(b + 1) * 512],
                        start=True, stop=True), inc=1)
                    e.mark(f"d_rs{b}", "t")
                    e.wait(f"d_rsev{b}")
                for b in range(8):
                    if b >= 2:
                        e.wait(f"d_rbev{b - 2}")
                    e.mm(lambda b=b: nc.tensor.matmul(
                        ps_d[1 - b % 2][:, 0:512], onesrB[0:1, 0:128],
                        rselR[0:1, b * 512:(b + 1) * 512],
                        start=True, stop=True), inc=1)
                    e.mark(f"d_rb{b}", "t")
                for oq in range(4):
                    if oq > 0:
                        wprep(oq)
                    for ti in range(NT):
                        u = oq * NT + ti
                        e.wait_tok(ldd_tok[u])
                        if ti == 0:
                            e.wait(f"d_wtev{(oq * 8 + 7) * 4 + 3}")
                        for b in range(2):
                            if u >= 1:
                                e.wait(f"d_dqev{u - 1}_{b}")
                            elif b < 2:
                                e.wait(f"d_rbev{7 - b}")
                            for ci in range(CI):
                                e.mm(lambda u=u, ci=ci, b=b, oq=oq:
                                     nc.tensor.matmul(
                                         ps_d[b][:, :],
                                         xlt2l[u % 3][:, ci * 128:(ci + 1) * 128],
                                         WT[:, ci, b * 512:(b + 1) * 512],
                                         start=(ci == 0), stop=False))
                            e.mm(lambda oq=oq, b=b: nc.tensor.matmul(
                                ps_d[b][:, :], onesrB[0:1, 0:128],
                                bffnB[0:1, oq * 1024 + b * 512:
                                      oq * 1024 + (b + 1) * 512],
                                start=False, stop=True), inc=1)
                            e.mark(f"d_d{u}_{b}", "t")
                        e.mark(f"d_mm{u}", "t")

            def d_scalar(e):
                fence(e)
                def wtev(oq):
                    for ob in range(8):
                        u = oq * 8 + ob
                        for r in range(4):
                            ur = u * 4 + r
                            e.wait(f"d_wtr{ur}")
                            e.op(lambda ob=ob, r=r, ur=ur: nc.scalar.copy(
                                WT[:, r * 8:(r + 1) * 8,
                                   ob * 128:(ob + 1) * 128],
                                ps_wtr[ur % 2][:, :].rearrange(
                                    "p (a b) -> p a b", a=8)))
                            e.mark(f"d_wtev{ur}")
                wtev(0)
                for b in range(8):
                    e.wait(f"d_rb{b}")
                    e.op(lambda b=b: nc.scalar.copy(
                        rselB[:, b * 512:(b + 1) * 512], ps_d[1 - b % 2][:, 0:512]))
                    e.mark(f"d_rbev{b}")
                for oq in range(4):
                    if oq > 0:
                        wtev(oq)
                    for ti in range(NT):
                        u = oq * NT + ti
                        for b in range(2):
                            e.wait(f"d_d{u}_{b}")
                            if u >= 1:
                                e.wait(f"d_ob{u - 1}_{b}")
                            e.op(lambda u=u, b=b, ti=ti: nc.scalar.activation(
                                out=dq[b][:, :], in_=ps_d[b][:, :],
                                func=AF.Copy, scale=qsel[:, ti:ti + 1]))
                            e.mark(f"d_dqev{u}_{b}")

            def d_vector(e):
                fence(e)
                for hf in range(2):
                    e.wait_sem("ldcnt", 16 * (hf + 1))
                    e.op(lambda hf=hf: nc.vector.tensor_copy(
                        asgb[:, hf * 2048:(hf + 1) * 2048], asgi[:, :]))
                    e.mark(f"d_asgc{hf}")
                for qr in range(4):
                    e.wait(f"d_ohb{qr}")
                    e.op(lambda qr=qr: nc.vector.tensor_scalar(
                        out=onehot[0:64, qr * 1024:(qr + 1) * 1024],
                        in0=ps_oh[0:64, :], scalar1=g["iotaC"][:, :],
                        scalar2=None, op0=ALU.is_equal))
                    e.mark(f"d_oh{qr}")
                e.wait_sem("ldd", 16)
                e.op(lambda: nc.vector.tensor_copy(cntb[:, :], cntg[:, :]))
                e.mark("d_cntb")
                e.wait("d_cc")
                e.op(lambda: nc.vector.tensor_scalar(
                    out=cselc[:, :], in0=ps_d[0][0:64, 0:1], scalar1=0.5,
                    scalar2=None, op0=ALU.is_gt))
                e.mark("d_csel")
                for b in range(8):
                    e.wait(f"d_rs{b}")
                    if b == 0:
                        e.wait(f"d_oh3")
                    e.op(lambda b=b: nc.vector.tensor_copy(
                        rselR[:, b * 512:(b + 1) * 512], ps_d[0][0:1, 0:512]))
                    e.mark(f"d_rsev{b}")
                for oq in range(4):
                    for ti in range(NT):
                        u = oq * NT + ti
                        for b in range(2):
                            e.wait(f"d_dqev{u}_{b}")
                            if u == 0 and b == 0:
                                e.wait("d_rbev7")
                            e.op(lambda oq=oq, b=b: nc.vector.tensor_tensor(
                                out=dq[b][:, :], in0=dq[b][:, :],
                                in1=rselB[:, oq * 1024 + b * 512:
                                          oq * 1024 + (b + 1) * 512],
                                op=ALU.mult))
                            e.mark(f"d_t1_{u}_{b}")
                            if b == 0:
                                e.wait_tok(ldm_tok[u])
                                if u >= 3:
                                    e.wait_tok(stout_tok[u - 3])
                            e.op(lambda u=u, b=b: nc.vector.tensor_tensor(
                                out=outb[u % 3][:, b * 512:(b + 1) * 512],
                                in0=dq[b][:, :],
                                in1=xmidb[u % 3][:, b * 512:(b + 1) * 512],
                                op=ALU.add))
                            e.mark(f"d_ob{u}_{b}")
                        e.mark(f"d_out{u}")

            run_block(P, [
                ("gpsimd", "g", d_gpsimd),
                ("sync", None, d_sync),
                ("tensor", None, d_tensor),
                ("scalar", "s", d_scalar),
                ("vector", "v", d_vector),
            ], "phaseD")

    def dbg_block(P, nc, env, dbg):
        def dbg_sync(e):
            e.wait_sem("cc1", P.cnt["cc1"])
            for key in ["stq0", "stq1", "stkv0", "stkv1", "stxlt0", "stxlt1",
                        "stxm0", "stxm1", "stx2_0", "stx2_1"]:
                e.wait_sem(key, P.cnt[key])
            for name in dbg:
                if name == "A_ONLY":
                    continue
                src = env[name]
                dt = F32 if name in ("XMID",) else BF16
                shape = list(src.shape) if P.emit else [1, 1]
                dst = (nc.dram_tensor("DBG_" + name, shape, dt,
                                      kind="ExternalOutput") if P.emit else None)
                e.dma(lambda src=src, dst=dst: nc.sync.dma_start(
                    out=dst.ap(), in_=src.ap()), "ldd")
            e.wait_sem("ldd", P.cnt["ldd"])
        run_block(P, [("sync", None, dbg_sync)], "dbg")

    P.emit = False
    build_all()
    P.emit = True
    P.reset_counts()
    nc = bass.Bass("TRN2", target_bir_lowering=False)
    P.nc = nc
    with ExitStack() as stack:
        P.stack = stack
        for k in SEMS:
            P.sem[k] = stack.enter_context(nc.semaphore(k))
        build_all()
    return nc



def make_inputs(L0_attn_x, L0_attn_Wqkv, L0_attn_bqkv, L0_attn_Wo, L0_attn_bo,
                L0_ffn_weight, L0_ffn_bias, L0_ffn_centroids,
                L0_ffn_assignments, T):
    x = np.ascontiguousarray(np.asarray(L0_attn_x, dtype=np.float32)
                             .reshape(-1, DM))
    common = {
        "wqkv": np.ascontiguousarray(L0_attn_Wqkv, dtype=np.float32),
        "bqkv": np.ascontiguousarray(L0_attn_bqkv, dtype=np.float32).reshape(1, -1),
        "wo": np.ascontiguousarray(L0_attn_Wo, dtype=np.float32),
        "bo": np.ascontiguousarray(L0_attn_bo, dtype=np.float32).reshape(1, -1),
        "wffn": np.ascontiguousarray(L0_ffn_weight, dtype=np.float32),
        "bffn": np.ascontiguousarray(L0_ffn_bias, dtype=np.float32).reshape(1, -1),
        "cent": np.ascontiguousarray(L0_ffn_centroids, dtype=np.float32),
        "asg": np.ascontiguousarray(L0_ffn_assignments, dtype=np.int32).reshape(1, -1),
        "ident": np.eye(128, dtype=np.float32),
        "onesr": np.ones((1, 512), dtype=np.float32),
        "onesc": np.ones((128, 1), dtype=np.float32),
        "iota64": np.arange(64, dtype=np.float32).reshape(64, 1),
    }
    return [dict(common, x_s=np.ascontiguousarray(x[r * T:(r + 1) * T]))
            for r in range(NCORES)]


def kernel(L0_attn_x, L0_attn_Wqkv, L0_attn_bqkv, L0_attn_Wo, L0_attn_bo,
           L0_ffn_weight, L0_ffn_bias, L0_ffn_centroids, L0_ffn_assignments):
    T = 16384 // NCORES
    nc = build(T=T)
    in_maps = make_inputs(L0_attn_x, L0_attn_Wqkv, L0_attn_bqkv, L0_attn_Wo,
                          L0_attn_bo, L0_ffn_weight, L0_ffn_bias,
                          L0_ffn_centroids, L0_ffn_assignments, T)
    res = run_bass_kernel_spmd(nc, in_maps, core_ids=list(range(NCORES)))
    out = np.concatenate([res.results[r]["out_s"] for r in range(NCORES)], axis=0)
    return out.reshape(1, 16384, 4096)
